# version 02
# baseline (speedup 1.0000x reference)
import math
import numpy as np
import ml_dtypes
from contextlib import ExitStack

import concourse.bass as bass
import concourse.mybir as mybir

F32 = mybir.dt.float32
BF16 = mybir.dt.bfloat16
AF = mybir.ActivationFunctionType
ALU = mybir.AluOpType
AX = mybir.AxisListType

BF = ml_dtypes.bfloat16

D = 2048
S = 4096
B = 2
SCALE0 = 1.0 / math.sqrt(128.0)
NEG = -3.0e38


class Sem:
    __slots__ = ("h", "v")

    def __init__(self, ctx, nc, name):
        self.h = ctx.enter_context(nc.semaphore(name))
        self.v = 0


class Em:

    ENGS = ("t", "v", "s", "g", "y")

    def __init__(self, nc, ctx):
        self.nc = nc
        self.ctx = ctx
        self.q = {k: [] for k in self.ENGS}
        self._sems = {}

    def sem(self, name) -> Sem:
        if name not in self._sems:
            self._sems[name] = Sem(self.ctx, self.nc, name)
        return self._sems[name]

    def set_autochain(self, eng, sem):
        if not hasattr(self, "_auto"):
            self._auto = {}
        self._auto[eng] = sem

    def op(self, eng, fn, waits=(), inc=None, inc_n=1):
        ac = getattr(self, "_auto", {}).get(eng)
        if ac is not None:
            waits = list(waits) + [(ac, ac.v)]
            if inc is None:
                inc = ac
        ret = None
        if inc is not None:
            inc.v += inc_n
            ret = inc.v
        wl = tuple((s.h, v) for (s, v) in waits if v > 0)

        def run(e, _fn=fn, _wl=wl, _inc=inc, _n=inc_n):
            for h, v in _wl:
                e.wait_ge(h, v)
            inst = _fn(e)
            if _inc is not None:
                inst.then_inc(_inc.h, _n)
        self.q[eng].append(run)
        return ret

    def wait(self, eng, sem, val):
        if val > 0:
            self.q[eng].append(lambda e, _h=sem.h, _v=val: e.wait_ge(_h, _v))

    def dma(self, eng, out_ap, in_ap, sem, waits=()):
        return self.op(eng, lambda e, _o=out_ap, _i=in_ap: e.dma_start(out=_o, in_=_i),
                       waits=waits, inc=sem, inc_n=16)

    def run_block(self):
        block = self.ctx.enter_context(self.nc.Block())

        def mk(lst):
            def body(eng):
                for fn in lst:
                    fn(eng)
            return body
        block.tensor(mk(self.q["t"]))
        block.vector(mk(self.q["v"]))
        block.scalar(mk(self.q["s"]))
        block.gpsimd(mk(self.q["g"]))
        block.sync(mk(self.q["y"]))


def sb(ctx, nc, name, shape, dt=BF16):
    return ctx.enter_context(nc.sbuf_tensor(name, shape, dt))


def ps(ctx, nc, name, shape, dt=F32):
    return ctx.enter_context(nc.psum_tensor(name, shape, dt))



def host_masks():
    qi = np.arange(128)[:, None]
    kj = np.arange(512)[None, :]
    out = np.zeros((4, 128, 512), np.float32)
    for u in range(4):
        out[u] = np.where(kj <= u * 128 + qi, 0.0, NEG)
    return np.ascontiguousarray(out)


def ident_np():
    return np.eye(128, dtype=BF)


def stripes_of_core(c):
    j = c % 4
    return (j, 7 - j)



def build_a1():
    nc = bass.Bass("TRN2", target_bir_lowering=False)
    xb = nc.dram_tensor("xb", [D, S], F32, kind="ExternalInput")
    wq_t = nc.dram_tensor("wq_t", [D, 512], F32, kind="ExternalInput")
    wk_t = nc.dram_tensor("wk_t", [D, 128], F32, kind="ExternalInput")
    wv_t = nc.dram_tensor("wv_t", [D, 128], F32, kind="ExternalInput")
    masks = nc.dram_tensor("masks", [4, 128, 512], F32, kind="ExternalInput")
    ident = nc.dram_tensor("ident", [128, 128], BF16, kind="ExternalInput")
    att = nc.dram_tensor("att", [4, 128, S], BF16, kind="ExternalOutput")

    KC = D // 128

    ctx = ExitStack()
    with ctx:
        em = Em(nc, ctx)
        wq = sb(ctx, nc, "wq", [128, KC * 512])
        wk = sb(ctx, nc, "wk", [128, KC * 128])
        wv = sb(ctx, nc, "wv", [128, KC * 128])
        xs = [sb(ctx, nc, f"xs{i}", [128, KC * 512]) for i in range(2)]
        qf = [sb(ctx, nc, f"qf{h}", [128, S]) for h in range(4)]
        kf = sb(ctx, nc, "kf", [128, S])
        vt = sb(ctx, nc, "vt", [128, 32 * 128])
        msk = sb(ctx, nc, "msk", [128, 4 * 512], F32)
        idn = sb(ctx, nc, "idn", [128, 128])
        p_sb = [sb(ctx, nc, f"p{h}", [128, 8 * 512]) for h in range(4)]
        pt_sb = sb(ctx, nc, "pt", [128, 512])
        att_sb = [sb(ctx, nc, f"atts{h}", [128, S]) for h in range(4)]
        stat = sb(ctx, nc, "stat", [128, 16], F32)
        sc_ps = ps(ctx, nc, "sc_ps", [128, 3072])
        pt_ps = ps(ctx, nc, "pt_ps", [128, 512], BF16)
        av_ps = ps(ctx, nc, "av_ps", [128, 512])

        ldw = em.sem("ldw")
        ldx = [em.sem("ldx0"), em.sem("ldx1")]
        st = em.sem("st")
        tp = em.sem("tp")
        ta = em.sem("ta")
        vv = em.sem("vv")
        ex = em.sem("ex")

        wqv = wq[:, :].rearrange("p (kc m) -> p kc m", kc=KC)
        wkv = wk[:, :].rearrange("p (kc m) -> p kc m", kc=KC)
        wvv = wv[:, :].rearrange("p (kc m) -> p kc m", kc=KC)

        em.dma("g", wqv, wq_t.ap().rearrange("(kc p) m -> p kc m", p=128), ldw)
        em.dma("g", wkv, wk_t.ap().rearrange("(kc p) m -> p kc m", p=128), ldw)
        em.dma("g", wvv, wv_t.ap().rearrange("(kc p) m -> p kc m", p=128), ldw)
        em.dma("g", msk[:, :].rearrange("p (u m) -> p u m", u=4),
               masks.ap().transpose([1, 0, 2]), ldw)
        v_allw = em.dma("g", idn[:, :], ident[:, :], ldw)

        xb_v = xb.ap().rearrange("(kc p) t -> p kc t", p=128)
        x_use = [0, 0]
        for tc in range(8):
            buf = tc % 2
            tsl = slice(tc * 512, (tc + 1) * 512)
            v_x = em.dma("g", xs[buf][:, :].rearrange("p (kc t) -> p kc t", kc=KC),
                         xb_v[:, :, tsl], ldx[buf],
                         waits=[(tp, x_use[buf])])
            xr = xs[buf][:, :].rearrange("p (kc t) -> p kc t", kc=KC)
            em.wait("t", vv, 9 * tc)
            em.wait("t", ldw, v_allw)
            em.wait("t", ldx[buf], v_x)
            for oc in range(5):
                dst = sc_ps[:, oc * 512:(oc + 1) * 512]
                for kc in range(KC):
                    w_ap = wqv[:, kc, oc * 128:(oc + 1) * 128] if oc < 4 else wkv[:, kc, :]
                    em.op("t", lambda e, _d=dst, _w=w_ap, _x=xr[:, kc, :],
                          _s=(kc == 0), _p=(kc == KC - 1):
                          e.matmul(_d, _w, _x, start=_s, stop=_p),
                          inc=tp if kc == KC - 1 else None)
            xr4 = xr.rearrange("p kc (a q) -> p kc a q", a=4)
            for stt in range(4):
                dst = av_ps[:, stt * 128:(stt + 1) * 128]
                for kc in range(KC):
                    em.op("t", lambda e, _d=dst, _x=xr4[:, kc, stt, :], _w=wvv[:, kc, :],
                          _s=(kc == 0), _p=(kc == KC - 1):
                          e.matmul(_d, _x, _w, start=_s, stop=_p),
                          inc=tp if kc == KC - 1 else None)
            x_use[buf] = tp.v
            base = 9 * tc
            for oc in range(4):
                em.op("v", lambda e, _d=qf[oc][:, tsl], _s=sc_ps[:, oc * 512:(oc + 1) * 512]:
                      e.tensor_copy(_d, _s), waits=[(tp, base + oc + 1)], inc=vv)
            em.op("v", lambda e, _d=kf[:, tsl], _s=sc_ps[:, 4 * 512:5 * 512]:
                  e.tensor_copy(_d, _s), waits=[(tp, base + 5)], inc=vv)
            for stt in range(4):
                dcol = (tc * 4 + stt) * 128
                em.op("v", lambda e, _d=vt[:, dcol:dcol + 128], _s=av_ps[:, stt * 128:(stt + 1) * 128]:
                      e.tensor_copy(_d, _s), waits=[(tp, base + 9)], inc=vv)
        proj_done = vv.v

        def stc(i):
            return stat[:, i:i + 1]

        em.wait("t", vv, proj_done)
        em.wait("v", ldw, v_allw)
        for t in range(32):
            s_stripe, u = t // 4, t % 4
            nkc = s_stripe + 1
            passes = [(0, min(nkc, 6))]
            if nkc > 6:
                passes.append((6, nkc))
            em.wait("s", ta, ta.v)
            em.wait("v", ta, ta.v)
            for h in range(4):
                q_ap = qf[h][:, t * 128:(t + 1) * 128]
                em.wait("v", ex, ex.v)
                for pi, (c0, c1) in enumerate(passes):
                    em.wait("t", ex, ex.v)
                    for c in range(c0, c1):
                        em.op("t", lambda e, _d=sc_ps[:, (c - c0) * 512:(c - c0 + 1) * 512],
                              _q=q_ap, _k=kf[:, c * 512:(c + 1) * 512]:
                              e.matmul(_d, _q, _k, start=True, stop=True),
                              inc=tp if c == c1 - 1 else None)
                    plen = (c1 - c0) * 512
                    wl = [(tp, tp.v)]
                    if pi == len(passes) - 1:
                        moff = (nkc - 1 - c0) * 512
                        em.op("v", lambda e, _d=sc_ps[:, moff:moff + 512], _m=msk[:, u * 512:(u + 1) * 512]:
                              e.tensor_tensor(_d, _d, _m, ALU.add), waits=wl, inc=vv)
                        wl = [(vv, vv.v)]
                    mcol = 0 if pi == 0 else 1
                    em.op("v", lambda e, _d=stc(mcol), _s=sc_ps[:, 0:plen]:
                          e.reduce_max(_d, _s, axis=AX.X), waits=wl, inc=vv)
                    em.wait("v", vv, vv.v)
                    if pi == 0:
                        em.op("v", lambda e: e.tensor_scalar_mul(stc(3), stc(0), -SCALE0), inc=vv)
                    else:
                        em.op("v", lambda e: e.tensor_tensor(stc(2), stc(0), stc(1), ALU.max), inc=vv)
                        em.wait("v", vv, vv.v)
                        em.op("v", lambda e: e.tensor_scalar_mul(stc(3), stc(2), -SCALE0), inc=vv)
                    vb = vv.v
                    scol = 4 if pi == 0 else 5
                    em.op("s", lambda e, _d=p_sb[h][:, c0 * 512:c0 * 512 + plen], _s=sc_ps[:, 0:plen],
                          _b=stc(3), _a=stc(scol):
                          e.activation(_d, _s, AF.Exp, bias=_b, scale=SCALE0, accum_out=_a),
                          waits=[(vv, vb)], inc=ex)
                if len(passes) == 1:
                    em.op("v", lambda e: e.reciprocal(stc(7), stc(4)), waits=[(ex, ex.v)], inc=vv)
                    em.wait("v", vv, vv.v)
                    tot = nkc * 512
                    em.op("v", lambda e, _p=p_sb[h][:, 0:tot]:
                          e.tensor_scalar_mul(_p, _p, stc(7)), inc=vv)
                else:
                    em.op("v", lambda e: e.tensor_tensor(stc(8), stc(0), stc(2), ALU.subtract),
                          waits=[(ex, ex.v)], inc=vv)
                    em.op("s", lambda e: e.activation(stc(9), stc(8), AF.Exp, bias=0.0, scale=SCALE0),
                          waits=[(vv, vv.v)], inc=ex)
                    em.op("v", lambda e: e.scalar_tensor_tensor(stc(6), stc(4), stc(9), stc(5),
                                                                ALU.mult, ALU.add),
                          waits=[(ex, ex.v)], inc=vv)
                    em.wait("v", vv, vv.v)
                    em.op("v", lambda e: e.reciprocal(stc(7), stc(6)), inc=vv)
                    em.wait("v", vv, vv.v)
                    em.op("v", lambda e: e.tensor_mul(stc(10), stc(9), stc(7)), inc=vv)
                    em.wait("v", vv, vv.v)
                    em.op("v", lambda e, _p=p_sb[h][:, 0:3072]:
                          e.tensor_scalar_mul(_p, _p, stc(10)), inc=vv)
                    tot2 = (nkc - 6) * 512
                    em.op("v", lambda e, _p=p_sb[h][:, 3072:3072 + tot2]:
                          e.tensor_scalar_mul(_p, _p, stc(7)), inc=vv)
            p_done = vv.v
            nkb = nkc * 4
            em.wait("t", vv, p_done)
            for kb in range(nkb):
                for h in range(4):
                    em.op("t", lambda e, _d=pt_ps[:, h * 128:(h + 1) * 128],
                          _s=p_sb[h][:, kb * 128:(kb + 1) * 128]:
                          e.transpose(_d, _s, idn[:, :]),
                          waits=[(vv, vv.v)] if h == 0 else (),
                          inc=ta if h == 3 else None)
                em.op("v", lambda e, _d=pt_sb[:, :], _s=pt_ps[:, :]:
                      e.tensor_copy(_d, _s), waits=[(ta, ta.v)], inc=vv)
                em.op("t", lambda e, _v=vt[:, kb * 128:(kb + 1) * 128], _p=pt_sb[:, :],
                      _s=(kb == 0), _e=(kb == nkb - 1):
                      e.matmul(av_ps[:, :], _v, _p, start=_s, stop=_e),
                      waits=[(vv, vv.v)], inc=ta if kb == nkb - 1 else None)
            for h in range(4):
                em.op("v", lambda e, _d=att_sb[h][:, t * 128:(t + 1) * 128],
                      _s=av_ps[:, h * 128:(h + 1) * 128]:
                      e.tensor_copy(_d, _s), waits=[(ta, ta.v)] if h == 0 else (), inc=vv)
        fin = vv.v
        for h in range(4):
            em.op("y", lambda e, _d=att[h, :, :], _s=att_sb[h][:, :]:
                  e.dma_start(out=_d, in_=_s), waits=[(vv, fin)], inc=st, inc_n=16)
        em.wait("y", st, st.v)
        em.run_block()
    return nc


def a1_in_maps(inputs):
    x = np.asarray(inputs["L0_attn_x"])
    wqkv = np.asarray(inputs["L0_attn_weight_qkv"])
    masks = host_masks()
    idn = ident_np()
    maps = []
    for c in range(8):
        b, g = c // 4, c % 4
        xb = np.ascontiguousarray(x[b].T)
        base = g * 768
        wq_t = np.ascontiguousarray(wqkv[base:base + 512].T)
        wk_t = np.ascontiguousarray(wqkv[base + 512:base + 640].T)
        wv_t = np.ascontiguousarray(wqkv[base + 640:base + 768].T)
        maps.append({"xb": xb, "wq_t": wq_t, "wk_t": wk_t, "wv_t": wv_t,
                     "masks": masks, "ident": idn})
    return maps



EPS = 1e-5


def build_a2():
    nc = bass.Bass("TRN2", target_bir_lowering=False)
    KC = D // 128
    T_ = 1024
    IE = 1408
    ICH = IE // 128
    ISH = 2816
    ISCH = ISH // 128

    att_in = nc.dram_tensor("att_in", [D, T_], BF16, kind="ExternalInput")
    x_res = nc.dram_tensor("x_res", [D, T_], F32, kind="ExternalInput")
    wproj_t = nc.dram_tensor("wproj_t", [D, D], F32, kind="ExternalInput")
    router_t = nc.dram_tensor("router_t", [D, 8], F32, kind="ExternalInput")
    gate_w = nc.dram_tensor("gate_w", [8, D, IE], F32, kind="ExternalInput")
    up_w = nc.dram_tensor("up_w", [8, D, IE], F32, kind="ExternalInput")
    down_w = nc.dram_tensor("down_w", [8, IE, D], F32, kind="ExternalInput")
    sg_w = nc.dram_tensor("sg_w", [D, ISH], F32, kind="ExternalInput")
    su_w = nc.dram_tensor("su_w", [D, ISH], F32, kind="ExternalInput")
    sd_w = nc.dram_tensor("sd_w", [ISH, D], F32, kind="ExternalInput")
    sg_b = nc.dram_tensor("sg_b", [ISH], F32, kind="ExternalInput")
    su_b = nc.dram_tensor("su_b", [ISH], F32, kind="ExternalInput")
    sd_b = nc.dram_tensor("sd_b", [D], F32, kind="ExternalInput")
    wq1 = nc.dram_tensor("wq1", [D, D], F32, kind="ExternalInput")
    wk1 = nc.dram_tensor("wk1", [D, D], F32, kind="ExternalInput")
    wv1 = nc.dram_tensor("wv1", [D, D], F32, kind="ExternalInput")
    ones1 = nc.dram_tensor("ones1", [1, 128], BF16, kind="ExternalInput")
    ones128 = nc.dram_tensor("ones128", [128, 1], BF16, kind="ExternalInput")
    sel = nc.dram_tensor("sel", [8, 8, 128], BF16, kind="ExternalInput")
    ident32 = nc.dram_tensor("ident32", [128, 128], F32, kind="ExternalInput")

    x2_out = nc.dram_tensor("x2_out", [D, T_], F32, kind="ExternalOutput")
    q1_out = nc.dram_tensor("q1_out", [D, T_], BF16, kind="ExternalOutput")
    k1_out = nc.dram_tensor("k1_out", [D, T_], BF16, kind="ExternalOutput")
    v1_out = nc.dram_tensor("v1_out", [D, T_], BF16, kind="ExternalOutput")

    ctx = ExitStack()
    with ctx:
        em = Em(nc, ctx)
        stream = sb(ctx, nc, "stream", [128, KC * T_], F32)
        xbf = sb(ctx, nc, "xbf", [128, KC * T_])
        att_sb = sb(ctx, nc, "att_sb", [128, 8 * T_])
        h_sb = sb(ctx, nc, "h_sb", [128, ICH * T_])
        cbc = sb(ctx, nc, "cbc", [128, T_])
        wb = [sb(ctx, nc, f"wb{i}", [128, 16 * 128]) for i in range(6)]
        s_ch = sb(ctx, nc, "s_ch", [128, T_])
        u_ch = sb(ctx, nc, "u_ch", [128, T_], F32)
        sq_ch = [sb(ctx, nc, f"sq{i}", [128, T_]) for i in range(2)]
        sq2 = [sb(ctx, nc, f"sq2_{i}", [128, T_]) for i in range(2)]
        rwt = sb(ctx, nc, "rwt", [128, KC * 8])
        comb = sb(ctx, nc, "comb", [128, 8], F32)
        comb_fm = sb(ctx, nc, "comb_fm", [8, T_])
        rstat = sb(ctx, nc, "rstat", [128, 12], F32)
        lnrow = sb(ctx, nc, "lnrow", [1, 4 * T_], F32)
        lnrow_bf = sb(ctx, nc, "lnrow_bf", [1, 2 * T_])
        o1 = sb(ctx, nc, "o1", [1, 128])
        o128 = sb(ctx, nc, "o128", [128, 1])
        sel_sb = sb(ctx, nc, "sel_sb", [8, 8 * 128])
        id32 = sb(ctx, nc, "id32", [128, 128], F32)
        sgb = sb(ctx, nc, "sgb", [128, ISCH], F32)
        sub = sb(ctx, nc, "sub", [128, ISCH], F32)
        sdb = sb(ctx, nc, "sdb", [128, KC], F32)
        stg = [sb(ctx, nc, f"stg{i}", [128, T_]) for i in range(2)]
        pool = ps(ctx, nc, "pool", [128, 4096])
        A = pool[:, 0:1024]
        Bk = pool[:, 1024:2048]
        C = pool[:, 2048:3072]
        Dk = pool[:, 3072:4096]

        ldc = em.sem("ldc")
        ldg = em.sem("ldg")
        wsem = [em.sem(f"ws{i}") for i in range(6)]
        st = em.sem("st")
        tq = em.sem("tq")
        vv = em.sem("vv")
        ss = em.sem("ss")
        em.set_autochain("v", vv)
        em.set_autochain("s", ss)

        em.dma("y", stream[:, :].rearrange("p (kc t) -> p kc t", kc=KC),
               x_res.ap().rearrange("(kc p) t -> p kc t", p=128), ldc)
        for i in range(ISCH):
            em.dma("y", sgb[:, i:i + 1], sg_b.ap().rearrange("(i p) -> i p", p=128)[i:i + 1, :].transpose([1, 0]), ldc)
            em.dma("y", sub[:, i:i + 1], su_b.ap().rearrange("(i p) -> i p", p=128)[i:i + 1, :].transpose([1, 0]), ldc)
        for i in range(KC):
            em.dma("y", sdb[:, i:i + 1], sd_b.ap().rearrange("(i p) -> i p", p=128)[i:i + 1, :].transpose([1, 0]), ldc)
        em.dma("y", o1[:, :], ones1[:, :], ldc)
        em.dma("y", o128[:, :], ones128[:, :], ldc)
        em.dma("y", sel_sb[:, :].rearrange("p (e m) -> p e m", e=8),
               sel.ap().transpose([1, 0, 2]), ldc)
        v_ldc = em.dma("y", id32[:, :], ident32[:, :], ldc)
        v_rwt = em.dma("g", rwt[:, :].rearrange("p (kc m) -> p kc m", kc=KC),
                       router_t.ap().rearrange("(kc p) m -> p kc m", p=128), ldg)
        em.wait("t", ldc, v_ldc)
        em.wait("v", ldc, v_ldc)
        em.wait("s", ldc, v_ldc)

        wuse = [0] * 6

        def wload(i, dram_slice):
            kcs = dram_slice.shape[0] // 128
            return em.dma("g", wb[i][:, 0:kcs * 128].rearrange("p (kc m) -> p kc m", kc=kcs),
                          dram_slice.rearrange("(kc p) m -> p kc m", p=128),
                          wsem[i], waits=[(tq, wuse[i])])

        strv = stream[:, :].rearrange("p (kc t) -> p kc t", kc=KC)
        xbv = xbf[:, :].rearrange("p (kc t) -> p kc t", kc=KC)
        attv = att_sb[:, :].rearrange("p (kc t) -> p kc t", kc=8)

        def wbv(i, kcs=16):
            return wb[i][:, 0:kcs * 128].rearrange("p (kc m) -> p kc m", kc=kcs)

        lda = em.sem("lda")
        for kh in range(2):
            v_att = em.dma("g", attv[:, :, :],
                           att_in.ap().rearrange("(kc p) t -> p kc t", p=128)[:, kh * 8:(kh + 1) * 8, :],
                           lda, waits=[(vv, vv.v)])
            em.wait("t", lda, v_att)
            for oc in range(KC):
                wi = oc % 2
                v_w = wload(wi, wproj_t.ap()[kh * 1024:(kh + 1) * 1024, oc * 128:(oc + 1) * 128])
                dst = A if oc % 2 == 0 else Bk
                em.wait("t", wsem[wi], v_w)
                em.wait("t", vv, vv.v)
                for hf in range(2):
                    for kc in range(8):
                        em.op("t", lambda e, _d=dst[:, hf * 512:(hf + 1) * 512],
                              _w=wbv(wi, 8)[:, kc, :],
                              _x=attv[:, kc, hf * 512:(hf + 1) * 512],
                              _s=(kc == 0), _p=(kc == 7):
                              e.matmul(_d, _w, _x, start=_s, stop=_p),
                              inc=tq if (kc == 7 and hf == 1) else None)
                wuse[wi] = tq.v
                em.op("v", lambda e, _d=strv[:, oc, :], _s=dst[:, :]:
                      e.tensor_tensor(_d, _d, _s, ALU.add), waits=[(tq, tq.v)], inc=vv)

        def layernorm():
            em.wait("s", vv, vv.v)
            squse = [0, 0]
            for kc in range(KC):
                b2 = kc % 2
                em.op("v", lambda e, _d=sq_ch[b2][:, :], _s=strv[:, kc, :]:
                      e.tensor_copy(_d, _s), waits=[(tq, squse[b2])], inc=vv)
                em.op("s", lambda e, _d=sq2[b2][:, :], _s=strv[:, kc, :]:
                      e.activation(_d, _s, AF.Square, bias=0.0, scale=1.0),
                      waits=[(tq, squse[b2])], inc=ss)
                for hf in range(2):
                    em.op("t", lambda e, _d=pool[0:1, 2048 + hf * 512:2048 + (hf + 1) * 512],
                          _o=o128[:, :], _x=sq_ch[b2][:, hf * 512:(hf + 1) * 512],
                          _s=(kc == 0), _p=(kc == KC - 1):
                          e.matmul(_d, _o, _x, start=_s, stop=_p),
                          waits=[(vv, vv.v)] if hf == 0 else (),
                          inc=tq if hf == 1 else None)
                for hf in range(2):
                    em.op("t", lambda e, _d=pool[0:1, 3072 + hf * 512:3072 + (hf + 1) * 512],
                          _o=o128[:, :], _x=sq2[b2][:, hf * 512:(hf + 1) * 512],
                          _s=(kc == 0), _p=(kc == KC - 1):
                          e.matmul(_d, _o, _x, start=_s, stop=_p),
                          waits=[(ss, ss.v)] if hf == 0 else (),
                          inc=tq if hf == 1 else None)
                squse[b2] = tq.v
            Ar, Br = lnrow[:, 0:T_], lnrow[:, T_:2 * T_]
            Cr, Dr = lnrow[:, 2 * T_:3 * T_], lnrow[:, 3 * T_:4 * T_]
            em.op("v", lambda e: e.tensor_copy(Ar, pool[0:1, 2048:3072]),
                  waits=[(tq, tq.v)], inc=vv)
            em.op("v", lambda e: e.tensor_copy(Br, pool[0:1, 3072:4096]), inc=vv)
            em.wait("v", vv, vv.v)
            em.op("v", lambda e: e.tensor_scalar_mul(Cr, Ar, 1.0 / D), inc=vv)
            em.wait("v", vv, vv.v)
            em.op("v", lambda e: e.tensor_scalar_mul(Ar, Br, 1.0 / D), inc=vv)
            em.wait("v", vv, vv.v)
            em.op("v", lambda e: e.tensor_mul(Br, Cr, Cr), inc=vv)
            em.wait("v", vv, vv.v)
            em.op("v", lambda e: e.tensor_sub(Ar, Ar, Br), inc=vv)
            em.wait("v", vv, vv.v)
            em.op("v", lambda e: e.tensor_scalar_add(Ar, Ar, EPS), inc=vv)
            em.op("s", lambda e: e.activation(Br, Ar, AF.Ln, bias=0.0, scale=1.0),
                  waits=[(vv, vv.v)], inc=ss)
            em.wait("s", ss, ss.v)
            em.op("s", lambda e: e.activation(Dr, Br, AF.Exp, bias=0.0, scale=-0.5),
                  inc=ss)
            em.op("v", lambda e: e.tensor_mul(Ar, Cr, Dr),
                  waits=[(ss, ss.v)], inc=vv)
            em.wait("v", vv, vv.v)
            em.op("v", lambda e: e.tensor_copy(lnrow_bf[:, 0:T_], Dr), inc=vv)
            em.op("v", lambda e: e.tensor_copy(lnrow_bf[:, T_:2 * T_], Ar), inc=vv)
            for hf in range(2):
                em.op("t", lambda e, _d=C[:, hf * 512:(hf + 1) * 512], _r=lnrow_bf[:, hf * 512:(hf + 1) * 512]:
                      e.matmul(_d, o1[:, :], _r, start=True, stop=True),
                      waits=[(vv, vv.v)] if hf == 0 else (), inc=tq if hf == 1 else None)
                em.op("t", lambda e, _d=Dk[:, hf * 512:(hf + 1) * 512], _r=lnrow_bf[:, T_ + hf * 512:T_ + (hf + 1) * 512]:
                      e.matmul(_d, o1[:, :], _r, start=True, stop=True),
                      inc=tq if hf == 1 else None)
            for kc in range(KC):
                em.op("v", lambda e, _x=strv[:, kc, :], _a=C[:, :]:
                      e.tensor_mul(_x, _x, _a), waits=[(tq, tq.v)] if kc == 0 else (), inc=vv)
                em.wait("v", vv, vv.v)
                em.op("v", lambda e, _x=strv[:, kc, :], _b=Dk[:, :]:
                      e.tensor_sub(_x, _x, _b), inc=vv)
                em.wait("v", vv, vv.v)
                em.op("v", lambda e, _d=xbv[:, kc, :], _s=strv[:, kc, :]:
                      e.tensor_copy(_d, _s), inc=vv)
            em.wait("t", vv, vv.v)

        layernorm()

        for tt in range(8):
            em.wait("t", ldg, v_rwt)
            for kc in range(KC):
                em.op("t", lambda e, _d=C[:, 0:8],
                      _x=xbv[:, kc, tt * 128:(tt + 1) * 128],
                      _w=rwt[:, :].rearrange("p (kc m) -> p kc m", kc=KC)[:, kc, :],
                      _s=(kc == 0), _p=(kc == KC - 1):
                      e.matmul(_d, _x, _w, start=_s, stop=_p),
                      inc=tq if kc == KC - 1 else None)
            em.op("v", lambda e: e.reduce_max(rstat[:, 0:1], C[:, 0:8], axis=AX.X),
                  waits=[(tq, tq.v)], inc=vv)
            em.wait("v", vv, vv.v)
            em.op("v", lambda e: e.tensor_scalar_mul(rstat[:, 1:2], rstat[:, 0:1], -1.0), inc=vv)
            em.op("s", lambda e: e.activation(comb[:, 0:8], C[:, 0:8], AF.Exp,
                                              bias=rstat[:, 1:2], scale=1.0),
                  waits=[(vv, vv.v), (tq, tq.v)], inc=ss)
            em.op("v", lambda e: e.reduce_max(rstat[:, 2:3], comb[:, 0:8], axis=AX.X),
                  waits=[(ss, ss.v)], inc=vv)
            em.wait("v", vv, vv.v)
            em.op("v", lambda e: e.tensor_single_scalar(u_ch[:, 0:8], comb[:, 0:8], rstat[:, 2:3], ALU.is_equal), inc=vv)
            em.wait("v", vv, vv.v)
            em.op("v", lambda e: e.scalar_tensor_tensor(u_ch[:, 8:16], u_ch[:, 0:8], -1.0e30,
                                                        comb[:, 0:8], ALU.mult, ALU.add), inc=vv)
            em.wait("v", vv, vv.v)
            em.op("v", lambda e: e.reduce_max(rstat[:, 3:4], u_ch[:, 8:16], axis=AX.X), inc=vv)
            em.wait("v", vv, vv.v)
            em.op("v", lambda e: e.tensor_single_scalar(u_ch[:, 8:16], comb[:, 0:8], rstat[:, 3:4], ALU.is_equal), inc=vv)
            em.op("v", lambda e: e.tensor_add(rstat[:, 4:5], rstat[:, 2:3], rstat[:, 3:4]), inc=vv)
            em.wait("v", vv, vv.v)
            em.op("v", lambda e: e.reciprocal(rstat[:, 5:6], rstat[:, 4:5]), inc=vv)
            em.wait("v", vv, vv.v)
            em.op("v", lambda e: e.tensor_mul(rstat[:, 6:7], rstat[:, 2:3], rstat[:, 5:6]), inc=vv)
            em.op("v", lambda e: e.tensor_mul(rstat[:, 7:8], rstat[:, 3:4], rstat[:, 5:6]), inc=vv)
            em.wait("v", vv, vv.v)
            em.op("v", lambda e: e.tensor_single_scalar(u_ch[:, 0:8], u_ch[:, 0:8], rstat[:, 6:7], ALU.mult), inc=vv)
            em.wait("v", vv, vv.v)
            em.op("v", lambda e: e.scalar_tensor_tensor(comb[:, 0:8], u_ch[:, 8:16], rstat[:, 7:8],
                                                        u_ch[:, 0:8], ALU.mult, ALU.add), inc=vv)
            em.op("t", lambda e, _d=Dk[0:8, 0:128], _s=comb[:, 0:8]:
                  e.transpose(_d, _s, id32[:, :]),
                  waits=[(vv, vv.v)], inc=tq)
            em.op("v", lambda e, _d=comb_fm[0:8, tt * 128:(tt + 1) * 128], _s=Dk[0:8, 0:128]:
                  e.tensor_copy(_d, _s), waits=[(tq, tq.v)], inc=vv)
        comb_done = vv.v

        def gu_half(dst, wi, hf):
            for kc in range(KC):
                em.op("t", lambda e, _d=dst[:, hf * 512:(hf + 1) * 512],
                      _w=wbv(wi)[:, kc, :],
                      _x=xbv[:, kc, hf * 512:(hf + 1) * 512],
                      _s=(kc == 0), _p=(kc == KC - 1):
                      e.matmul(_d, _w, _x, start=_s, stop=_p),
                      inc=tq if kc == KC - 1 else None)

        for e_ in range(8):
            for hf in range(2):
                em.op("t", lambda e, _d=C[:, hf * 512:(hf + 1) * 512],
                      _s=sel_sb[:, :].rearrange("p (e m) -> p e m", e=8)[:, e_, :],
                      _r=comb_fm[0:8, hf * 512:(hf + 1) * 512]:
                      e.matmul(_d, _s, _r, start=True, stop=True),
                      waits=[(vv, comb_done if e_ == 0 else vv.v)] if hf == 0 else (),
                      inc=tq if hf == 1 else None)
            em.op("v", lambda e: e.tensor_copy(cbc[:, :], C[:, :]), waits=[(tq, tq.v)], inc=vv)
            cb_ready = vv.v
            for ii in range(ICH):
                v_wg = wload(2, gate_w.ap()[e_, :, ii * 128:(ii + 1) * 128])
                v_wu = wload(3, up_w.ap()[e_, :, ii * 128:(ii + 1) * 128])
                em.wait("t", wsem[2], v_wg)
                em.wait("t", wsem[3], v_wu)
                em.wait("t", ss, ss.v)
                em.wait("t", vv, vv.v)
                gu_half(A, 2, 0); gu_half(A, 2, 1)
                wuse[2] = tq.v
                g_done = tq.v
                gu_half(Bk, 3, 0); gu_half(Bk, 3, 1)
                wuse[3] = tq.v
                em.op("s", lambda e: e.activation(s_ch[:, :], A[:, :], AF.Silu, bias=0.0, scale=1.0),
                      waits=[(tq, g_done), (vv, vv.v)], inc=ss)
                em.op("v", lambda e: e.tensor_tensor(u_ch[:, :], s_ch[:, :], Bk[:, :], ALU.mult),
                      waits=[(ss, ss.v), (tq, tq.v), (vv, cb_ready)], inc=vv)
                em.wait("v", vv, vv.v)
                em.op("v", lambda e, _d=h_sb[:, ii * T_:(ii + 1) * T_]:
                      e.tensor_tensor(_d, u_ch[:, :], cbc[:, :], ALU.mult), inc=vv)
            h_done = vv.v
            for oc in range(KC):
                wdi = 4 + (oc % 2)
                v_wd = wload(wdi, down_w.ap()[e_, :, oc * 128:(oc + 1) * 128])
                dst = C if oc % 2 == 0 else Dk
                em.wait("t", wsem[wdi], v_wd)
                em.wait("t", vv, h_done if oc == 0 else vv.v)
                for hf in range(2):
                    for ii in range(ICH):
                        em.op("t", lambda e, _d=dst[:, hf * 512:(hf + 1) * 512],
                              _w=wbv(wdi, ICH)[:, ii, :],
                              _h=h_sb[:, ii * T_ + hf * 512: ii * T_ + (hf + 1) * 512],
                              _s=(ii == 0), _p=(ii == ICH - 1):
                              e.matmul(_d, _w, _h, start=_s, stop=_p),
                              inc=tq if (ii == ICH - 1 and hf == 1) else None)
                wuse[wdi] = tq.v
                em.op("v", lambda e, _x=strv[:, oc, :], _s=dst[:, :]:
                      e.tensor_tensor(_x, _x, _s, ALU.add), waits=[(tq, tq.v)], inc=vv)

        for half in range(2):
            ib0 = half * 11
            for ii in range(11):
                ic = ib0 + ii
                v_wg = wload(2, sg_w.ap()[:, ic * 128:(ic + 1) * 128])
                v_wu = wload(3, su_w.ap()[:, ic * 128:(ic + 1) * 128])
                em.wait("t", wsem[2], v_wg)
                em.wait("t", wsem[3], v_wu)
                em.wait("t", ss, ss.v)
                em.wait("t", vv, vv.v)
                gu_half(A, 2, 0); gu_half(A, 2, 1)
                wuse[2] = tq.v
                g_done = tq.v
                gu_half(Bk, 3, 0); gu_half(Bk, 3, 1)
                wuse[3] = tq.v
                em.op("s", lambda e, _b=sgb[:, ic:ic + 1]:
                      e.activation(s_ch[:, :], A[:, :], AF.Silu, bias=_b, scale=1.0),
                      waits=[(tq, g_done), (vv, vv.v)], inc=ss)
                em.op("v", lambda e, _b=sub[:, ic:ic + 1]:
                      e.tensor_scalar_add(u_ch[:, :], Bk[:, :], _b),
                      waits=[(tq, tq.v)], inc=vv)
                em.wait("v", vv, vv.v)
                em.op("v", lambda e, _d=h_sb[:, ii * T_:(ii + 1) * T_]:
                      e.tensor_tensor(_d, s_ch[:, :], u_ch[:, :], ALU.mult),
                      waits=[(ss, ss.v)], inc=vv)
            h_done = vv.v
            for oc in range(KC):
                wdi = 4 + (oc % 2)
                v_wd = wload(wdi, sd_w.ap()[ib0 * 128:(ib0 + 11) * 128, oc * 128:(oc + 1) * 128])
                dst = C if oc % 2 == 0 else Dk
                em.wait("t", wsem[wdi], v_wd)
                em.wait("t", vv, h_done if oc == 0 else vv.v)
                for hf in range(2):
                    for ii in range(11):
                        em.op("t", lambda e, _d=dst[:, hf * 512:(hf + 1) * 512],
                              _w=wbv(wdi, 11)[:, ii, :],
                              _h=h_sb[:, ii * T_ + hf * 512: ii * T_ + (hf + 1) * 512],
                              _s=(ii == 0), _p=(ii == 11 - 1):
                              e.matmul(_d, _w, _h, start=_s, stop=_p),
                              inc=tq if (ii == 11 - 1 and hf == 1) else None)
                wuse[wdi] = tq.v
                em.op("v", lambda e, _x=strv[:, oc, :], _s=dst[:, :]:
                      e.tensor_tensor(_x, _x, _s, ALU.add), waits=[(tq, tq.v)], inc=vv)
        for oc in range(KC):
            em.op("v", lambda e, _x=strv[:, oc, :], _b=sdb[:, oc:oc + 1]:
                  e.tensor_scalar_add(_x, _x, _b), inc=vv)
        em.wait("t", vv, vv.v)

        layernorm()
        x2_ready = vv.v
        for kc in range(KC):
            em.op("y", lambda e, _d=x2_out.ap().rearrange("(kc p) t -> p kc t", p=128)[:, kc, :],
                  _s=strv[:, kc, :]:
                  e.dma_start(out=_d, in_=_s), waits=[(vv, x2_ready)], inc=st, inc_n=16)

        stsem = [em.sem("stg0"), em.sem("stg1")]
        stguse = [0, 0]
        for wt, outt in ((wq1, q1_out), (wk1, k1_out), (wv1, v1_out)):
            for oc in range(KC):
                wi = oc % 2
                v_w = wload(wi, wt.ap()[:, oc * 128:(oc + 1) * 128])
                dst = A if oc % 2 == 0 else Bk
                sgi = oc % 2
                em.wait("t", wsem[wi], v_w)
                em.wait("t", vv, vv.v)
                for hf in range(2):
                    for kc in range(KC):
                        em.op("t", lambda e, _d=dst[:, hf * 512:(hf + 1) * 512],
                              _w=wbv(wi)[:, kc, :],
                              _x=xbv[:, kc, hf * 512:(hf + 1) * 512],
                              _s=(kc == 0), _p=(kc == KC - 1):
                              e.matmul(_d, _w, _x, start=_s, stop=_p),
                              inc=tq if (kc == KC - 1 and hf == 1) else None)
                wuse[wi] = tq.v
                em.op("v", lambda e, _d=stg[sgi][:, :], _s=dst[:, :]:
                      e.tensor_copy(_d, _s), waits=[(tq, tq.v), (stsem[sgi], stguse[sgi])], inc=vv)
                stguse[sgi] = em.dma("y", outt.ap().rearrange("(kc p) t -> p kc t", p=128)[:, oc, :],
                                     stg[sgi][:, :], stsem[sgi], waits=[(vv, vv.v)])
        em.wait("y", st, st.v)
        em.wait("y", stsem[0], stguse[0])
        em.wait("y", stsem[1], stguse[1])
        em.run_block()
    return nc


def a2_in_maps(inputs, att_full):
    x = np.asarray(inputs["L0_attn_x"])
    maps = []
    common = {
        "wproj_t": np.ascontiguousarray(np.asarray(inputs["L0_attn_weight_proj"]).T),
        "router_t": np.ascontiguousarray(np.asarray(inputs["L0_ffn_router_w"]).T),
        "gate_w": np.asarray(inputs["L0_ffn_gate_w"]),
        "up_w": np.asarray(inputs["L0_ffn_up_w"]),
        "down_w": np.asarray(inputs["L0_ffn_down_w"]),
        "sg_w": np.asarray(inputs["L0_ffn_s_gate_w"]),
        "su_w": np.asarray(inputs["L0_ffn_s_up_w"]),
        "sd_w": np.asarray(inputs["L0_ffn_s_down_w"]),
        "sg_b": np.asarray(inputs["L0_ffn_s_gate_b"]),
        "su_b": np.asarray(inputs["L0_ffn_s_up_b"]),
        "sd_b": np.asarray(inputs["L0_ffn_s_down_b"]),
        "wq1": np.asarray(inputs["L1_attn_Wq"]),
        "wk1": np.asarray(inputs["L1_attn_Wk"]),
        "wv1": np.asarray(inputs["L1_attn_Wv"]),
        "ones1": np.ones((1, 128), BF),
        "ones128": np.ones((128, 1), BF),
        "sel": np.eye(8, dtype=BF)[:, :, None] * np.ones((1, 1, 128), BF),
        "ident32": np.eye(128, dtype=np.float32),
    }
    for c in range(8):
        b = c // 4
        s0, s1 = stripes_of_core(c)
        cols = np.r_[s0 * 512:(s0 + 1) * 512, s1 * 512:(s1 + 1) * 512]
        m = dict(common)
        m["att_in"] = np.ascontiguousarray(att_full[b][:, cols])
        m["x_res"] = np.ascontiguousarray(x[b].T[:, cols].astype(np.float32))
        maps.append(m)
    return maps



def build_b1():
    nc = bass.Bass("TRN2", target_bir_lowering=False)
    H = 16
    SEG = 512

    qf_d = nc.dram_tensor("qf", [D, S], BF16, kind="ExternalInput")
    kf_d = nc.dram_tensor("kf", [D, S], BF16, kind="ExternalInput")
    vt_d = nc.dram_tensor("vt", [S, D], BF16, kind="ExternalInput")
    betasT = nc.dram_tensor("betasT", [128, 16], F32, kind="ExternalInput")
    tri = nc.dram_tensor("tri", [128, 128], F32, kind="ExternalInput")
    ident = nc.dram_tensor("ident", [128, 128], BF16, kind="ExternalInput")
    ones1b = nc.dram_tensor("ones1b", [1, 128], BF16, kind="ExternalInput")
    att_d = nc.dram_tensor("att", [D, S], BF16, kind="ExternalOutput")

    ctx = ExitStack()
    with ctx:
        em = Em(nc, ctx)
        qseg = sb(ctx, nc, "qseg", [128, H * 1024])
        kseg = sb(ctx, nc, "kseg", [128, H * 1024])
        vseg = sb(ctx, nc, "vseg", [128, 8 * D])
        mem_aug = sb(ctx, nc, "mem_aug", [128, H * 129], F32)
        mem_bf = sb(ctx, nc, "mem_bf", [128, H * 129])
        sigk = sb(ctx, nc, "sigk", [128, 512])
        sigq = sb(ctx, nc, "sigq", [128, 512])
        sigkT = sb(ctx, nc, "sigkT", [128, 512])
        sck = sb(ctx, nc, "sck", [128, 512], F32)
        sce = sb(ctx, nc, "sce", [128, 512])
        delt = sb(ctx, nc, "delt", [128, 512])
        p_sb = sb(ctx, nc, "p_sb", [128, 512])
        pt2 = sb(ctx, nc, "pt2", [128, 512])
        adot = sb(ctx, nc, "adot", [128, 512])
        astg = [sb(ctx, nc, f"astg{i}", [128, 512]) for i in range(2)]
        rowr = sb(ctx, nc, "rowr", [1, 1024], F32)
        rowb = sb(ctx, nc, "rowb", [1, 512])
        beta_sb = sb(ctx, nc, "beta_sb", [128, 16], F32)
        omb_sb = sb(ctx, nc, "omb_sb", [128, 16], F32)
        tri_sb = sb(ctx, nc, "tri_sb", [128, 128], F32)
        idn = sb(ctx, nc, "idn", [128, 128])
        o1 = sb(ctx, nc, "o1", [1, 128])
        stat = sb(ctx, nc, "stat", [128, 8], F32)
        pool = ps(ctx, nc, "pool", [128, 4096])
        sc_ps = pool[:, 0:512]
        ptp = pool[:, 512:1024]
        av_ps = pool[:, 1024:1152]
        md_ps = pool[:, 1152:1280]
        attm_ps = pool[:, 1536:2048]
        bc_ps = pool[:, 2048:2560]
        zq_ps = pool[0:1, 2560:3072]
        ds_ps = [pool[:, 3072:3201], pool[:, 3584:3713]]
        skT_ps = pool[:, 3328:3584]

        ldc = em.sem("ldc")
        lds = [em.sem("lds0"), em.sem("lds1")]
        st = em.sem("st")
        stsem = [em.sem("stg0"), em.sem("stg1")]
        tq = em.sem("tq")
        vv = em.sem("vv")
        ss = em.sem("ss")
        em.set_autochain("v", vv)
        em.set_autochain("s", ss)

        ptp_bf = ptp.tensor.bitcast(BF16)
        skT_bf = skT_ps.tensor.bitcast(BF16)

        em.dma("y", beta_sb[:, :], betasT[:, :], ldc)
        em.dma("y", tri_sb[:, :], tri[:, :], ldc)
        em.dma("y", idn[:, :], ident[:, :], ldc)
        v_ldc = em.dma("y", o1[:, :], ones1b[:, :], ldc)
        em.wait("t", ldc, v_ldc)
        em.wait("v", ldc, v_ldc)
        em.wait("s", ldc, v_ldc)
        em.op("s", lambda e: e.activation(beta_sb[:, :], beta_sb[:, :], AF.Sigmoid, bias=0.0, scale=1.0))
        em.op("v", lambda e: e.scalar_tensor_tensor(omb_sb[:, :], beta_sb[:, :], -1.0, beta_sb[:, :],
                                                    ALU.mult, ALU.bypass), waits=[(ss, ss.v)])
        em.op("v", lambda e: e.tensor_scalar_add(omb_sb[:, :], omb_sb[:, :], 1.0))
        em.op("v", lambda e: e.memset(mem_aug[:, :], 0.0))
        for h in range(H):
            em.op("v", lambda e, _c=mem_aug[:, h * 129 + 128:h * 129 + 129]:
                  e.memset(_c, 1.0 / 128.0))

        qsv = qseg[:, :].rearrange("p (h t) -> p h t", h=H)
        ksv = kseg[:, :].rearrange("p (h t) -> p h t", h=H)
        vsv = vseg[:, :].rearrange("p (s f) -> p s f", s=8)

        def sigma(dst, src_ap):
            em.op("v", lambda e: e.tensor_scalar_min(sck[:, :], src_ap, 0.0),
                  waits=[(tq, tq.v), (ss, ss.v)])
            em.op("s", lambda e: e.activation(sce[:, :], sck[:, :], AF.Exp, bias=0.0, scale=1.0),
                  waits=[(vv, vv.v)])
            em.op("v", lambda e: e.scalar_tensor_tensor(dst, src_ap, 0.0, sce[:, :],
                                                        ALU.max, ALU.add), waits=[(ss, ss.v)])

        seg_use = [0, 0]
        seg_use_v = [0, 0]
        stg_use = [0, 0]
        for s_ in range(8):
            par = s_ % 2
            tsl = slice(s_ * 512, (s_ + 1) * 512)
            v_q = em.dma("g", qsv[:, :, par * 512:(par + 1) * 512],
                         qf_d.ap().rearrange("(h p) t -> p h t", p=128)[:, :, tsl],
                         lds[par], waits=[(tq, seg_use[par]), (vv, seg_use_v[par])])
            v_k = em.dma("g", ksv[:, :, par * 512:(par + 1) * 512],
                         kf_d.ap().rearrange("(h p) t -> p h t", p=128)[:, :, tsl],
                         lds[par], waits=[])
            v_v = em.dma("g", vsv[:, par * 4:(par + 1) * 4, :],
                         vt_d.ap()[tsl, :].rearrange("(sub p) f -> p sub f", p=128),
                         lds[par], waits=[])
            em.wait("t", lds[par], v_v)
            em.wait("v", lds[par], v_v)
            for h in range(H):
                kb0 = h * 1024 + par * 512
                kk = kseg[:, kb0:kb0 + 512]
                qq = qseg[:, kb0:kb0 + 512]
                mb = mem_bf[:, h * 129:(h + 1) * 129]
                ma = mem_aug[:, h * 129:(h + 1) * 129]
                em.op("v", lambda e, _d=mb, _s=ma: e.tensor_copy(_d, _s),
                      waits=[(tq, tq.v)])
                sigma(sigk[:, :], kk)
                sigma(sigq[:, :], qq)
                sig_done = vv.v
                em.op("t", lambda e, _z=mem_bf[:, h * 129 + 128:h * 129 + 129]:
                      e.matmul(zq_ps, _z, sigq[:, :], start=True, stop=True),
                      waits=[(vv, sig_done)], inc=tq)
                em.op("v", lambda e: e.tensor_copy(rowr[:, 0:512], zq_ps), waits=[(tq, tq.v)])
                em.op("v", lambda e: e.reciprocal(rowr[:, 512:1024], rowr[:, 0:512]),
                      waits=[(vv, vv.v)])
                em.op("v", lambda e: e.tensor_copy(rowb[:, :], rowr[:, 512:1024]))
                em.op("t", lambda e: e.matmul(bc_ps, o1[:, :], rowb[:, :], start=True, stop=True),
                      waits=[(vv, vv.v)], inc=tq)
                em.op("v", lambda e: e.tensor_tensor(sigq[:, :], sigq[:, :], bc_ps, ALU.mult),
                      waits=[(tq, tq.v)])
                sq_done = vv.v
                em.op("t", lambda e, _m=mem_bf[:, h * 129:h * 129 + 128]:
                      e.matmul(attm_ps, _m, sigq[:, :], start=True, stop=True),
                      waits=[(vv, sq_done)], inc=tq)
                attm_done = tq.v
                for sub in range(4):
                    dsp = ds_ps[sub % 2]
                    em.op("t", lambda e, _l=sigk[:, sub * 128:(sub + 1) * 128], _m=mb, _d=dsp:
                          e.matmul(_d, _l, _m, start=True, stop=True),
                          waits=[(vv, vv.v)], inc=tq)
                    em.op("v", lambda e, _d=dsp: e.reciprocal(stat[:, 0:1], _d[:, 128:129]),
                          waits=[(tq, tq.v)])
                    em.op("v", lambda e: e.tensor_scalar_mul(stat[:, 1:2], stat[:, 0:1], -1.0),
                          waits=[(vv, vv.v)])
                    em.op("v", lambda e, _d=delt[:, sub * 128:(sub + 1) * 128], _p=dsp,
                          _v=vseg[:, (par * 4 + sub) * D + h * 128:(par * 4 + sub) * D + (h + 1) * 128]:
                          e.scalar_tensor_tensor(_d, _p[:, 0:128], stat[:, 1:2], _v,
                                                 ALU.mult, ALU.add),
                          waits=[(vv, vv.v), (tq, tq.v)])
                delta_done = vv.v
                sgi = (s_ * H + h) % 2
                for sub in range(4):
                    nkb = sub + 1
                    em.op("t", lambda e, _q=qseg[:, kb0 + sub * 128:kb0 + (sub + 1) * 128],
                          _k=kseg[:, kb0:kb0 + nkb * 128], _d=pool[:, 0:nkb * 128]:
                          e.matmul(_d, _q, _k, start=True, stop=True),
                          waits=[(ss, ss.v), (vv, vv.v)], inc=tq)
                    em.op("v", lambda e, _d=pool[:, (nkb - 1) * 128:nkb * 128]:
                          e.tensor_tensor(_d, _d, tri_sb[:, :], ALU.add), waits=[(tq, tq.v)])
                    em.op("v", lambda e, _s=pool[:, 0:nkb * 128]:
                          e.reduce_max(stat[:, 2:3], _s, axis=AX.X), waits=[(vv, vv.v)])
                    em.op("v", lambda e: e.tensor_scalar_mul(stat[:, 3:4], stat[:, 2:3], -SCALE0),
                          waits=[(vv, vv.v)])
                    em.op("s", lambda e, _d=p_sb[:, 0:nkb * 128], _s=pool[:, 0:nkb * 128]:
                          e.activation(_d, _s, AF.Exp, bias=stat[:, 3:4], scale=SCALE0,
                                       accum_out=stat[:, 4:5]),
                          waits=[(vv, vv.v), (tq, tq.v)])
                    em.op("v", lambda e: e.reciprocal(stat[:, 5:6], stat[:, 4:5]),
                          waits=[(ss, ss.v)])
                    em.op("v", lambda e, _p=p_sb[:, 0:nkb * 128]:
                          e.tensor_single_scalar(_p, _p, stat[:, 5:6], ALU.mult),
                          waits=[(vv, vv.v)])
                    pn = vv.v
                    for kb in range(nkb):
                        em.op("t", lambda e, _d=ptp_bf[:, 1024 + kb * 128:1024 + (kb + 1) * 128],
                              _s=p_sb[:, kb * 128:(kb + 1) * 128]:
                              e.transpose(_d, _s, idn[:, :]),
                              waits=[(vv, pn)] if kb == 0 else (),
                              inc=tq if kb == nkb - 1 else None)
                    em.op("v", lambda e, _d=pt2[:, 0:nkb * 128], _s=ptp_bf[:, 1024:1024 + nkb * 128]:
                          e.tensor_copy(_d, _s), waits=[(tq, tq.v)])
                    ptd = vv.v
                    for kb in range(nkb):
                        em.op("t", lambda e, _v=vseg[:, (par * 4 + kb) * D + h * 128:(par * 4 + kb) * D + (h + 1) * 128],
                              _p=pt2[:, kb * 128:(kb + 1) * 128],
                              _s=(kb == 0), _e=(kb == nkb - 1):
                              e.matmul(av_ps, _v, _p, start=_s, stop=_e),
                              waits=[(vv, ptd)] if kb == 0 else (),
                              inc=tq if kb == nkb - 1 else None)
                    em.op("v", lambda e, _d=adot[:, sub * 128:(sub + 1) * 128],
                          _o=omb_sb[:, h:h + 1]:
                          e.tensor_single_scalar(_d, av_ps, _o, ALU.mult),
                          waits=[(tq, tq.v)])
                    em.op("v", lambda e, _d=astg[sgi][:, sub * 128:(sub + 1) * 128],
                          _a=pool[:, 1536 + sub * 128:1536 + (sub + 1) * 128],
                          _o=adot[:, sub * 128:(sub + 1) * 128],
                          _bt=beta_sb[:, h:h + 1]:
                          e.scalar_tensor_tensor(_d, _a, _bt, _o,
                                                 ALU.mult, ALU.add),
                          waits=[(tq, attm_done), (stsem[sgi], stg_use[sgi]), (vv, vv.v)])
                stg_use[sgi] = em.dma("y", att_d.ap()[h * 128:(h + 1) * 128, tsl],
                                      astg[sgi][:, :], stsem[sgi], waits=[(vv, vv.v)])
                for sub in range(4):
                    em.op("t", lambda e, _d=skT_bf[:, 6656 + sub * 128:6656 + (sub + 1) * 128],
                          _s=sigk[:, sub * 128:(sub + 1) * 128]:
                          e.transpose(_d, _s, idn[:, :]),
                          waits=[(vv, delta_done)] if sub == 0 else (),
                          inc=tq if sub == 3 else None)
                em.op("v", lambda e: e.tensor_copy(sigkT[:, :], skT_bf[:, 6656:6656 + 512]),
                      waits=[(tq, tq.v)])
                em.wait("t", vv, vv.v)
                for sub in range(4):
                    em.op("t", lambda e, _l=sigkT[:, sub * 128:(sub + 1) * 128],
                          _r=delt[:, sub * 128:(sub + 1) * 128],
                          _s=(sub == 0), _e=(sub == 3):
                          e.matmul(md_ps, _l, _r, start=_s, stop=_e),
                          inc=tq if sub == 3 else None)
                em.op("v", lambda e, _m=mem_aug[:, h * 129:h * 129 + 128]:
                      e.tensor_tensor(_m, _m, md_ps, ALU.add), waits=[(tq, tq.v)])
                em.op("v", lambda e: e.reduce_sum(stat[:, 6:7], sigk[:, :], axis=AX.X))
                em.op("v", lambda e, _z=mem_aug[:, h * 129 + 128:h * 129 + 129]:
                      e.tensor_tensor(_z, _z, stat[:, 6:7], ALU.add), waits=[(vv, vv.v)])
            seg_use[par] = tq.v
            seg_use_v[par] = vv.v
        em.wait("y", stsem[0], stg_use[0])
        em.wait("y", stsem[1], stg_use[1])
        em.run_block()
    return nc



def build_b2():
    nc = bass.Bass("TRN2", target_bir_lowering=False)
    KC = D // 128
    T_ = 1024
    I1 = 7168
    IC = I1 // 128
    R = 64
    SCL = 16.0 / 64.0

    att_in = nc.dram_tensor("att_in", [D, T_], BF16, kind="ExternalInput")
    x2_in = nc.dram_tensor("x2_in", [D, T_], F32, kind="ExternalInput")
    wo = nc.dram_tensor("wo", [D, D], F32, kind="ExternalInput")
    conf_t = nc.dram_tensor("conf_t", [D, 8], F32, kind="ExternalInput")
    conf_b = nc.dram_tensor("conf_b", [1, 8], F32, kind="ExternalInput")
    wealth = nc.dram_tensor("wealth", [1, 8], F32, kind="ExternalInput")
    bgate = nc.dram_tensor("bgate", [D, I1], F32, kind="ExternalInput")
    bup = nc.dram_tensor("bup", [D, I1], F32, kind="ExternalInput")
    bdown = nc.dram_tensor("bdown", [I1, D], F32, kind="ExternalInput")
    gA = nc.dram_tensor("gA", [8, D, R], F32, kind="ExternalInput")
    gB = nc.dram_tensor("gB", [8, R, I1], F32, kind="ExternalInput")
    uA = nc.dram_tensor("uA", [8, D, R], F32, kind="ExternalInput")
    uB = nc.dram_tensor("uB", [8, R, I1], F32, kind="ExternalInput")
    dA = nc.dram_tensor("dA", [8, I1, R], F32, kind="ExternalInput")
    dB = nc.dram_tensor("dB", [8, R, D], F32, kind="ExternalInput")
    ones1 = nc.dram_tensor("ones1", [1, 128], BF16, kind="ExternalInput")
    ones128 = nc.dram_tensor("ones128", [128, 1], BF16, kind="ExternalInput")
    sel = nc.dram_tensor("sel", [8, 8, 128], BF16, kind="ExternalInput")
    ident32 = nc.dram_tensor("ident32", [128, 128], F32, kind="ExternalInput")

    out_d = nc.dram_tensor("out", [D, T_], F32, kind="ExternalOutput")
    x3_dram = nc.dram_tensor("x3_sp", [D, T_], F32, kind="Internal")
    hb_dram = nc.dram_tensor("hb_sp", [IC, 128, T_], BF16, kind="Internal")

    ctx = ExitStack()
    with ctx:
        em = Em(nc, ctx)
        AR = sb(ctx, nc, "AR", [128, 94 * 1024])
        ARf = AR.bitcast(F32)
        xbf = AR[:, 0:16 * 1024]
        stream = ARf[:, 8192:8192 + 16 * 1024]
        z_g = AR[0:64, 16384:16384 + 8192]
        z_u = AR[0:64, 24576:24576 + 8192]
        base_g = AR[:, 32768:32768 + 1024]
        base_u = AR[:, 33792:33792 + 1024]
        gb_b = [AR[0:64, 34816 + i * 1024:34816 + (i + 1) * 1024] for i in range(2)]
        ub_b = [AR[0:64, 36864 + i * 1024:36864 + (i + 1) * 1024] for i in range(2)]
        dA_b = [AR[:, 38912 + i * 512:38912 + (i + 1) * 512] for i in range(2)]
        suw = AR[:, 39936:39936 + 1024]
        hbar = ARf[:, 20480:20480 + 1024]
        s_ch = AR[:, 43008:43008 + 1024]
        g_ch = AR[:, 44032:44032 + 1024]
        u_ch = AR[:, 45056:45056 + 1024]
        att_sb = AR[:, 49152:49152 + 8192]
        y_sb = AR[0:64, 49152:49152 + 8192]
        wd = AR[:, 57344:57344 + 14336]
        wb = [AR[:, 71680 + i * 2048:71680 + (i + 1) * 2048] for i in range(4)]
        stg = [AR[:, 79872 + i * 1024:79872 + (i + 1) * 1024] for i in range(2)]
        lnrow = ARf[:, 40960:40960 + 4096][0:1, :]
        lnrow_bf = AR[:, 90112:90112 + 2048][0:1, :]
        cbc8 = AR[:, 81920:81920 + 8192]
        comb_fm = AR[0:8, 92160:92160 + 1024]
        hb_st = [AR[:, 93184 + i * 1024:93184 + (i + 1) * 1024] for i in range(2)]
        rwt = sb(ctx, nc, "rwt", [128, KC * 8])
        comb = sb(ctx, nc, "comb", [128, 8], F32)
        rstat = sb(ctx, nc, "rstat", [128, 12], F32)
        rows8 = sb(ctx, nc, "rows8", [1, 16])
        rows8f = sb(ctx, nc, "rows8f", [1, 16], F32)
        o1 = sb(ctx, nc, "o1", [1, 128])
        o128 = sb(ctx, nc, "o128", [128, 1])
        sel_sb = sb(ctx, nc, "sel_sb", [8, 8 * 128])
        id32 = sb(ctx, nc, "id32", [128, 128], F32)
        sq_ch = [sb(ctx, nc, f"sq{i}", [128, T_]) for i in range(2)]
        sq2 = [sb(ctx, nc, f"sq2_{i}", [128, T_]) for i in range(2)]
        x3c = [sb(ctx, nc, f"x3c{i}", [128, T_], F32) for i in range(2)]
        pool = ps(ctx, nc, "pool", [128, 4096])
        A = pool[:, 0:1024]
        Bk = pool[:, 1024:2048]
        C = pool[:, 2048:3072]
        Dk = pool[:, 3072:4096]

        ldc = em.sem("ldc")
        ldg = em.sem("ldg")
        lda = em.sem("lda")
        ldh = [em.sem("ldh0"), em.sem("ldh1")]
        wsem = [em.sem(f"ws{i}") for i in range(4)]
        gsem = [em.sem(f"gs{i}") for i in range(2)]
        usem = [em.sem(f"us{i}") for i in range(2)]
        asem = [em.sem(f"as{i}") for i in range(2)]
        dsem = em.sem("ds")
        x3sem = [em.sem(f"x3s{i}") for i in range(2)]
        spl = em.sem("spl")
        st = em.sem("st")
        stsem = [em.sem("stg0"), em.sem("stg1")]
        tq = em.sem("tq")
        vv = em.sem("vv")
        ss = em.sem("ss")
        em.set_autochain("v", vv)
        em.set_autochain("s", ss)

        em.dma("y", stream.rearrange("p (kc t) -> p kc t", kc=KC),
               x2_in.ap().rearrange("(kc p) t -> p kc t", p=128), ldc)
        em.dma("y", rows8f[:, 0:8], conf_b[:, :], ldc)
        em.dma("y", rows8f[:, 8:16], wealth[:, :], ldc)
        em.dma("y", o1[:, :], ones1[:, :], ldc)
        em.dma("y", o128[:, :], ones128[:, :], ldc)
        em.dma("y", sel_sb[:, :].rearrange("p (e m) -> p e m", e=8),
               sel.ap().transpose([1, 0, 2]), ldc)
        v_ldc = em.dma("y", id32[:, :], ident32[:, :], ldc)
        v_rwt = em.dma("g", rwt[:, :].rearrange("p (kc m) -> p kc m", kc=KC),
                       conf_t.ap().rearrange("(kc p) m -> p kc m", p=128), ldg)
        em.wait("t", ldc, v_ldc)
        em.wait("v", ldc, v_ldc)
        em.wait("s", ldc, v_ldc)

        wuse = [0] * 4

        def wload(i, dram_slice, sem=None):
            sem = wsem[i] if sem is None else sem
            kcs = dram_slice.shape[0] // 128
            m_ = dram_slice.shape[1]
            return em.dma("g", wb[i][:, 0:kcs * m_].rearrange("p (kc m) -> p kc m", kc=kcs),
                          dram_slice.rearrange("(kc p) m -> p kc m", p=128),
                          sem, waits=[(tq, wuse[i])])

        strv = stream.rearrange("p (kc t) -> p kc t", kc=KC)
        xbv = xbf.rearrange("p (kc t) -> p kc t", kc=KC)
        attv = att_sb.rearrange("p (kc t) -> p kc t", kc=8)

        def wbv(i, kcs=16):
            return wb[i].rearrange("p (kc m) -> p kc m", kc=16)[:, 0:kcs, :]

        for kh in range(2):
            v_att = em.dma("g", attv[:, :, :],
                           att_in.ap().rearrange("(kc p) t -> p kc t", p=128)[:, kh * 8:(kh + 1) * 8, :],
                           lda, waits=[(vv, vv.v)])
            em.wait("t", lda, v_att)
            for oc in range(KC):
                wi = oc % 2
                v_w = wload(wi, wo.ap()[kh * 1024:(kh + 1) * 1024, oc * 128:(oc + 1) * 128])
                dst = A if oc % 2 == 0 else Bk
                em.wait("t", wsem[wi], v_w)
                em.wait("t", vv, vv.v)
                for hf in range(2):
                    for kc in range(8):
                        em.op("t", lambda e, _d=pool[:, (oc % 2) * 1024 + hf * 512:(oc % 2) * 1024 + (hf + 1) * 512],
                              _w=wbv(wi, 8)[:, kc, :],
                              _x=attv[:, kc, hf * 512:(hf + 1) * 512],
                              _s=(kc == 0), _p=(kc == 7):
                              e.matmul(_d, _w, _x, start=_s, stop=_p),
                              inc=tq if (kc == 7 and hf == 1) else None)
                wuse[wi] = tq.v
                em.op("v", lambda e, _d=strv[:, oc, :], _s=dst:
                      e.tensor_tensor(_d, _d, _s, ALU.add), waits=[(tq, tq.v)], inc=vv)

        def layernorm():
            em.wait("s", vv, vv.v)
            squse = [0, 0]
            for kc in range(KC):
                b2_ = kc % 2
                em.op("v", lambda e, _d=sq_ch[b2_][:, :], _s=strv[:, kc, :]:
                      e.tensor_copy(_d, _s), waits=[(tq, squse[b2_])])
                em.op("s", lambda e, _d=sq2[b2_][:, :], _s=strv[:, kc, :]:
                      e.activation(_d, _s, AF.Square, bias=0.0, scale=1.0),
                      waits=[(tq, squse[b2_])])
                for hf in range(2):
                    em.op("t", lambda e, _d=pool[0:1, 2048 + hf * 512:2048 + (hf + 1) * 512],
                          _o=o128[:, :], _x=sq_ch[b2_][:, hf * 512:(hf + 1) * 512],
                          _s=(kc == 0), _p=(kc == KC - 1):
                          e.matmul(_d, _o, _x, start=_s, stop=_p),
                          waits=[(vv, vv.v)] if hf == 0 else (),
                          inc=tq if hf == 1 else None)
                for hf in range(2):
                    em.op("t", lambda e, _d=pool[0:1, 3072 + hf * 512:3072 + (hf + 1) * 512],
                          _o=o128[:, :], _x=sq2[b2_][:, hf * 512:(hf + 1) * 512],
                          _s=(kc == 0), _p=(kc == KC - 1):
                          e.matmul(_d, _o, _x, start=_s, stop=_p),
                          waits=[(ss, ss.v)] if hf == 0 else (),
                          inc=tq if hf == 1 else None)
                squse[b2_] = tq.v
            Ar, Br = lnrow[:, 0:T_], lnrow[:, T_:2 * T_]
            Cr, Dr = lnrow[:, 2 * T_:3 * T_], lnrow[:, 3 * T_:4 * T_]
            em.op("v", lambda e: e.tensor_copy(Ar, pool[0:1, 2048:3072]), waits=[(tq, tq.v)])
            em.op("v", lambda e: e.tensor_copy(Br, pool[0:1, 3072:4096]))
            em.op("v", lambda e: e.tensor_scalar_mul(Cr, Ar, 1.0 / D))
            em.op("v", lambda e: e.tensor_scalar_mul(Ar, Br, 1.0 / D))
            em.op("v", lambda e: e.tensor_mul(Br, Cr, Cr))
            em.op("v", lambda e: e.tensor_sub(Ar, Ar, Br))
            em.op("v", lambda e: e.tensor_scalar_add(Ar, Ar, EPS))
            em.op("s", lambda e: e.activation(Br, Ar, AF.Ln, bias=0.0, scale=1.0),
                  waits=[(vv, vv.v)])
            em.op("s", lambda e: e.activation(Dr, Br, AF.Exp, bias=0.0, scale=-0.5))
            em.op("v", lambda e: e.tensor_mul(Ar, Cr, Dr), waits=[(ss, ss.v)])
            em.op("v", lambda e: e.tensor_copy(lnrow_bf[:, 0:T_], Dr))
            em.op("v", lambda e: e.tensor_copy(lnrow_bf[:, T_:2 * T_], Ar))
            for hf in range(2):
                em.op("t", lambda e, _d=C[:, hf * 512:(hf + 1) * 512], _r=lnrow_bf[:, hf * 512:(hf + 1) * 512]:
                      e.matmul(_d, o1[:, :], _r, start=True, stop=True),
                      waits=[(vv, vv.v)] if hf == 0 else (), inc=tq if hf == 1 else None)
                em.op("t", lambda e, _d=Dk[:, hf * 512:(hf + 1) * 512], _r=lnrow_bf[:, T_ + hf * 512:T_ + (hf + 1) * 512]:
                      e.matmul(_d, o1[:, :], _r, start=True, stop=True),
                      inc=tq if hf == 1 else None)
            for kc in range(KC):
                em.op("v", lambda e, _x=strv[:, kc, :]:
                      e.tensor_mul(_x, _x, C), waits=[(tq, tq.v)] if kc == 0 else ())
                em.op("v", lambda e, _x=strv[:, kc, :]:
                      e.tensor_sub(_x, _x, Dk))
                em.op("v", lambda e, _d=xbv[:, kc, :], _s=strv[:, kc, :]:
                      e.tensor_copy(_d, _s))
            em.wait("t", vv, vv.v)

        layernorm()

        v_spl = 0
        for kc in range(KC):
            v_spl = em.dma("y", x3_dram.ap().rearrange("(kc p) t -> p kc t", p=128)[:, kc, :],
                           strv[:, kc, :], spl, waits=[(vv, vv.v)] if kc == 0 else ())

        em.op("v", lambda e: e.tensor_copy(rows8[:, 0:8], rows8f[:, 0:8]))
        em.op("v", lambda e: e.tensor_copy(rows8[:, 8:16], rows8f[:, 8:16]))
        for tt in range(8):
            em.wait("t", ldg, v_rwt)
            for kc in range(KC):
                em.op("t", lambda e, _d=C[:, 0:8],
                      _x=xbv[:, kc, tt * 128:(tt + 1) * 128],
                      _w=rwt[:, :].rearrange("p (kc m) -> p kc m", kc=KC)[:, kc, :],
                      _s=(kc == 0), _p=(kc == KC - 1):
                      e.matmul(_d, _x, _w, start=_s, stop=_p),
                      inc=tq if kc == KC - 1 else None)
            em.op("t", lambda e: e.matmul(Dk[:, 0:8], o1[:, :], rows8[:, 0:8], start=True, stop=True),
                  waits=[(vv, vv.v)], inc=tq)
            em.op("t", lambda e: e.matmul(Dk[:, 512:520], o1[:, :], rows8[:, 8:16], start=True, stop=True),
                  inc=tq)
            em.op("v", lambda e: e.tensor_copy(comb[:, 0:8], C[:, 0:8]), waits=[(tq, tq.v)])
            em.op("v", lambda e: e.tensor_tensor(comb[:, 0:8], comb[:, 0:8], Dk[:, 0:8], ALU.add))
            em.op("s", lambda e: e.activation(comb[:, 0:8], comb[:, 0:8], AF.Sigmoid, bias=0.0, scale=1.0),
                  waits=[(vv, vv.v)])
            em.op("v", lambda e: e.tensor_tensor(comb[:, 0:8], comb[:, 0:8], Dk[:, 512:520], ALU.mult),
                  waits=[(ss, ss.v)])
            em.op("v", lambda e: e.reduce_max(rstat[:, 2:3], comb[:, 0:8], axis=AX.X))
            eqs = x3c[0]
            em.op("v", lambda e: e.tensor_single_scalar(eqs[:, 0:8], comb[:, 0:8], rstat[:, 2:3], ALU.is_equal))
            em.op("v", lambda e: e.scalar_tensor_tensor(eqs[:, 8:16], eqs[:, 0:8], -1.0e30,
                                                        comb[:, 0:8], ALU.mult, ALU.add))
            em.op("v", lambda e: e.reduce_max(rstat[:, 3:4], eqs[:, 8:16], axis=AX.X))
            em.op("v", lambda e: e.tensor_single_scalar(eqs[:, 8:16], comb[:, 0:8], rstat[:, 3:4], ALU.is_equal))
            em.op("v", lambda e: e.tensor_sub(rstat[:, 4:5], rstat[:, 3:4], rstat[:, 2:3]))
            em.op("s", lambda e: e.activation(rstat[:, 5:6], rstat[:, 4:5], AF.Exp, bias=0.0, scale=1.0),
                  waits=[(vv, vv.v)])
            em.op("v", lambda e: e.tensor_scalar_add(rstat[:, 6:7], rstat[:, 5:6], 1.0),
                  waits=[(ss, ss.v)])
            em.op("v", lambda e: e.reciprocal(rstat[:, 7:8], rstat[:, 6:7]))
            em.op("v", lambda e: e.tensor_mul(rstat[:, 8:9], rstat[:, 5:6], rstat[:, 7:8]))
            em.op("v", lambda e: e.tensor_single_scalar(eqs[:, 0:8], eqs[:, 0:8], rstat[:, 7:8], ALU.mult))
            em.op("v", lambda e: e.scalar_tensor_tensor(comb[:, 0:8], eqs[:, 8:16], rstat[:, 8:9],
                                                        eqs[:, 0:8], ALU.mult, ALU.add))
            em.op("t", lambda e, _d=Dk[0:8, 0:128], _s=comb[:, 0:8]:
                  e.transpose(_d, _s, id32[:, :]), waits=[(vv, vv.v)], inc=tq)
            em.op("v", lambda e, _d=comb_fm[:, tt * 128:(tt + 1) * 128], _s=Dk[0:8, 0:128]:
                  e.tensor_copy(_d, _s), waits=[(tq, tq.v)])
        comb_done = vv.v
        for e_ in range(8):
            for hf in range(2):
                em.op("t", lambda e, _d=C[:, hf * 512:(hf + 1) * 512],
                      _s=sel_sb[:, :].rearrange("p (e m) -> p e m", e=8)[:, e_, :],
                      _r=comb_fm[:, hf * 512:(hf + 1) * 512]:
                      e.matmul(_d, _s, _r, start=True, stop=True),
                      waits=[(vv, vv.v)] if hf == 0 else (), inc=tq if hf == 1 else None)
            em.op("v", lambda e, _d=cbc8[:, e_ * 1024:(e_ + 1) * 1024]:
                  e.tensor_copy(_d, C), waits=[(tq, tq.v)])

        em.wait("t", spl, v_spl)
        em.wait("v", spl, v_spl)
        em.wait("g", spl, v_spl)
        for e_ in range(8):
            for which, zdst, wAd in (("g", z_g, gA), ("u", z_u, uA)):
                wi = e_ % 2
                v_w = wload(wi, wAd.ap()[e_, :, :])
                em.wait("t", wsem[wi], v_w)
                em.wait("t", vv, vv.v)
                for hf in range(2):
                    for kc in range(KC):
                        em.op("t", lambda e, _d=C[0:64, hf * 512:(hf + 1) * 512],
                              _w=wb[wi][:, 0:16 * 64].rearrange("p (kc m) -> p kc m", kc=16)[:, kc, :],
                              _x=xbv[:, kc, hf * 512:(hf + 1) * 512],
                              _s=(kc == 0), _p=(kc == KC - 1):
                              e.matmul(_d, _w, _x, start=_s, stop=_p),
                              inc=tq if (kc == KC - 1 and hf == 1) else None)
                wuse[wi] = tq.v
                em.op("v", lambda e, _d=zdst[:, e_ * 1024:(e_ + 1) * 1024], _s=C[0:64, :]:
                      e.tensor_copy(_d, _s), waits=[(tq, tq.v)])
        z_done = vv.v

        em.op("v", lambda e: e.memset(y_sb[:, :], 0.0), waits=[(lda, 32), (tq, tq.v)])
        hb_dma = None
        for i in range(IC):
            par = i % 2
            v_bg = wload(par, bgate.ap()[:, i * 128:(i + 1) * 128])
            v_bu = wload(2 + par, bup.ap()[:, i * 128:(i + 1) * 128], sem=wsem[2 + par])
            v_gb = em.dma("g", gb_b[par].rearrange("q (e m) -> q e m", e=8),
                          gB.ap()[:, :, i * 128:(i + 1) * 128].transpose([1, 0, 2]),
                          gsem[par], waits=[(tq, wuse[par])])
            v_ub = em.dma("g", ub_b[par].rearrange("q (e m) -> q e m", e=8),
                          uB.ap()[:, :, i * 128:(i + 1) * 128].transpose([1, 0, 2]),
                          usem[par], waits=[(tq, wuse[par])])
            v_da = em.dma("g", dA_b[par].rearrange("p (e m) -> p e m", e=8),
                          dA.ap()[:, i * 128:(i + 1) * 128, :].transpose([1, 0, 2]),
                          asem[par], waits=[(tq, wuse[par])])
            em.wait("t", wsem[par], v_bg)
            em.wait("t", wsem[2 + par], v_bu)
            em.wait("t", gsem[par], v_gb)
            em.wait("t", usem[par], v_ub)
            em.wait("t", asem[par], v_da)
            em.wait("t", vv, vv.v)
            for hf in range(2):
                for kc in range(KC):
                    em.op("t", lambda e, _d=A[:, hf * 512:(hf + 1) * 512],
                          _w=wbv(par)[:, kc, :], _x=xbv[:, kc, hf * 512:(hf + 1) * 512],
                          _s=(kc == 0), _p=(kc == KC - 1):
                          e.matmul(_d, _w, _x, start=_s, stop=_p),
                          inc=tq if kc == KC - 1 else None)
            wuse[par] = tq.v
            bg_done = tq.v
            for hf in range(2):
                for kc in range(KC):
                    em.op("t", lambda e, _d=Bk[:, hf * 512:(hf + 1) * 512],
                          _w=wbv(2 + par)[:, kc, :], _x=xbv[:, kc, hf * 512:(hf + 1) * 512],
                          _s=(kc == 0), _p=(kc == KC - 1):
                          e.matmul(_d, _w, _x, start=_s, stop=_p),
                          inc=tq if kc == KC - 1 else None)
            wuse[2 + par] = tq.v
            em.op("v", lambda e: e.tensor_copy(base_g, A), waits=[(tq, bg_done)])
            em.op("v", lambda e: e.tensor_copy(base_u, Bk), waits=[(tq, tq.v)])
            base_done = vv.v
            for e_ in range(8):
                em.wait("t", vv, vv.v if e_ == 0 else vv.v)
                for hf in range(2):
                    em.op("t", lambda e, _d=C[:, hf * 512:(hf + 1) * 512],
                          _w=gb_b[par].rearrange("q (e m) -> q e m", e=8)[:, e_, :],
                          _z=z_g[:, e_ * 1024 + hf * 512:e_ * 1024 + (hf + 1) * 512]:
                          e.matmul(_d, _w, _z, start=True, stop=True),
                          inc=tq if hf == 1 else None)
                for hf in range(2):
                    em.op("t", lambda e, _d=Dk[:, hf * 512:(hf + 1) * 512],
                          _w=ub_b[par].rearrange("q (e m) -> q e m", e=8)[:, e_, :],
                          _z=z_u[:, e_ * 1024 + hf * 512:e_ * 1024 + (hf + 1) * 512]:
                          e.matmul(_d, _w, _z, start=True, stop=True),
                          inc=tq if hf == 1 else None)
                lora_done = tq.v
                em.op("v", lambda e: e.scalar_tensor_tensor(g_ch[:, :], C, SCL, base_g,
                                                            ALU.mult, ALU.add),
                      waits=[(tq, lora_done), (vv, base_done)])
                em.op("s", lambda e: e.activation(s_ch[:, :], g_ch[:, :], AF.Silu, bias=0.0, scale=1.0),
                      waits=[(vv, vv.v)])
                em.op("v", lambda e: e.scalar_tensor_tensor(u_ch[:, :], Dk, SCL, base_u,
                                                            ALU.mult, ALU.add))
                em.op("v", lambda e: e.tensor_tensor(u_ch[:, :], s_ch[:, :], u_ch[:, :], ALU.mult),
                      waits=[(ss, ss.v)])
                em.op("v", lambda e, _c=cbc8[:, e_ * 1024:(e_ + 1) * 1024]:
                      e.tensor_tensor(suw[:, :], u_ch[:, :], _c, ALU.mult))
                if e_ == 0:
                    em.op("v", lambda e: e.tensor_copy(hbar, suw[:, :]),
                          waits=[hb_dma] if hb_dma else ())
                else:
                    em.op("v", lambda e: e.tensor_tensor(hbar, hbar, suw[:, :], ALU.add))
                for hf in range(2):
                    em.op("t", lambda e, _d=A[0:64, hf * 512:(hf + 1) * 512],
                          _w=dA_b[par].rearrange("p (e m) -> p e m", e=8)[:, e_, :],
                          _r=suw[:, hf * 512:(hf + 1) * 512],
                          _s=True, _p=True:
                          e.matmul(_d, _w, _r, start=True, stop=True),
                          waits=[(vv, vv.v)] if hf == 0 else (),
                          inc=tq if hf == 1 else None)
                em.op("v", lambda e, _d=y_sb[:, e_ * 1024:(e_ + 1) * 1024]:
                      e.tensor_tensor(_d, _d, A[0:64, :], ALU.add), waits=[(tq, tq.v)])
            hb_dma = (ldh[par], em.dma("g", hb_dram.ap()[i, :, :], hbar, ldh[par],
                                       waits=[(vv, vv.v)]))
        em.op("v", lambda e: e.tensor_scalar_mul(y_sb[:, :], y_sb[:, :], SCL),
              waits=[(tq, tq.v)])
        y_scaled = vv.v
        em.wait("g", ldh[0], ldh[0].v)
        em.wait("g", ldh[1], ldh[1].v)

        hb_use = [0, 0]
        wd_use = 0
        db_use = 0
        x3use = [vv.v, vv.v]
        ldh_fin = [ldh[0].v, ldh[1].v]
        em.wait("v", ldh[0], ldh_fin[0])
        em.wait("v", ldh[1], ldh_fin[1])
        for ocp in range(8):
            v_wd = em.dma("g", wd.rearrange("p (ic m) -> p ic m", ic=IC),
                          bdown.ap()[:, ocp * 256:(ocp + 1) * 256].rearrange("(ic p) m -> p ic m", p=128),
                          dsem, waits=[(tq, wd_use)])
            v_db = em.dma("g", wb[3].rearrange("p (e m) -> p e m", e=8)[0:64, :, 0:256],
                          dB.ap()[:, :, ocp * 256:(ocp + 1) * 256].transpose([1, 0, 2]),
                          wsem[3], waits=[(tq, db_use)])
            em.wait("t", dsem, v_wd)
            em.wait("t", wsem[3], v_db)
            em.wait("t", vv, vv.v)
            for i in range(IC):
                par = i % 2
                v_hb = em.dma("g", hb_st[par], hb_dram.ap()[i, :, :], ldh[par],
                              waits=[(tq, hb_use[par])])
                em.wait("t", ldh[par], v_hb)
                for sub in range(4):
                    oc2, hf = sub // 2, sub % 2
                    em.op("t", lambda e, _d=pool[:, 2048 + sub * 512:2048 + (sub + 1) * 512],
                          _w=wd.rearrange("p (ic m) -> p ic m", ic=IC)[:, i, oc2 * 128:(oc2 + 1) * 128],
                          _h=hb_st[par][:, hf * 512:(hf + 1) * 512],
                          _s=(i == 0):
                          e.matmul(_d, _w, _h, start=_s, stop=False),
                          inc=tq if sub == 3 else None)
                hb_use[par] = tq.v
            for sub in range(4):
                oc2, hf = sub // 2, sub % 2
                for e_ in range(8):
                    em.op("t", lambda e, _d=pool[:, 2048 + sub * 512:2048 + (sub + 1) * 512],
                          _w=wb[3].rearrange("p (e m) -> p e m", e=8)[0:64, e_, oc2 * 128:(oc2 + 1) * 128],
                          _y=y_sb[:, e_ * 1024 + hf * 512:e_ * 1024 + (hf + 1) * 512],
                          _p=(e_ == 7):
                          e.matmul(_d, _w, _y, start=False, stop=_p),
                          inc=tq if e_ == 7 else None)
            wd_use = tq.v
            db_use = tq.v
            mm_done = tq.v
            for oc2 in range(2):
                oc = ocp * 2 + oc2
                x3b = x3c[oc % 2]
                v_x3 = em.dma("y", x3b[:, :],
                              x3_dram.ap().rearrange("(kc p) t -> p kc t", p=128)[:, oc, :],
                              x3sem[oc % 2], waits=[(vv, x3use[oc % 2]), (spl, v_spl)])
                em.wait("v", x3sem[oc % 2], v_x3)
                for hf in range(2):
                    sub = oc2 * 2 + hf
                    em.op("v", lambda e, _d=strv[:, oc, hf * 512:(hf + 1) * 512],
                          _p=pool[:, 2048 + sub * 512:2048 + (sub + 1) * 512],
                          _x=x3b[:, hf * 512:(hf + 1) * 512]:
                          e.tensor_tensor(_d, _p, _x, ALU.add), waits=[(tq, mm_done)])
                x3use[oc % 2] = vv.v

        layernorm()
        fin = vv.v
        for kc in range(KC):
            em.op("y", lambda e, _d=out_d.ap().rearrange("(kc p) t -> p kc t", p=128)[:, kc, :],
                  _s=strv[:, kc, :]:
                  e.dma_start(out=_d, in_=_s), waits=[(vv, fin)], inc=st, inc_n=16)
        em.wait("y", st, st.v)
        em.run_block()
    return nc
from concourse.bass_utils import run_bass_kernel_spmd

_NCACHE = {}


def _get_nc(name, builder):
    if name not in _NCACHE:
        _NCACHE[name] = builder()
    return _NCACHE[name]


def _own_cols(c):
    a, b_ = stripes_of_core(c)
    return np.r_[a * 512:(a + 1) * 512, b_ * 512:(b_ + 1) * 512]


def kernel(**inputs):
    cores = list(range(8))

    nc1 = _get_nc("a1", build_a1)
    r1 = run_bass_kernel_spmd(nc1, a1_in_maps(inputs), core_ids=cores)
    att_full = np.stack([
        np.concatenate([r1.results[b * 4 + g]["att"].reshape(512, S)
                        for g in range(4)], axis=0)
        for b in range(B)])

    nc2 = _get_nc("a2", build_a2)
    r2 = run_bass_kernel_spmd(nc2, a2_in_maps(inputs, att_full), core_ids=cores)

    qf = np.zeros((B, D, S), BF)
    kf = np.zeros((B, D, S), BF)
    vt = np.zeros((B, S, D), BF)
    x2 = [None] * 8
    for c in range(8):
        b = c // 4
        cols = _own_cols(c)
        qf[b][:, cols] = r2.results[c]["q1_out"]
        kf[b][:, cols] = r2.results[c]["k1_out"]
        vt[b][cols, :] = r2.results[c]["v1_out"].T
        x2[c] = r2.results[c]["x2_out"]

    nc3 = _get_nc("b1", build_b1)
    tri = host_masks()[0][:, 0:128].copy()
    betasT = np.ascontiguousarray(np.asarray(inputs["L1_attn_betas"])[0, :, 0, :].T)
    maps3 = []
    for c in range(8):
        b = c // 4
        maps3.append({"qf": qf[b], "kf": kf[b], "vt": vt[b], "betasT": betasT,
                      "tri": tri, "ident": ident_np(),
                      "ones1b": np.ones((1, 128), BF)})
    r3 = run_bass_kernel_spmd(nc3, maps3, core_ids=cores)
    T = np.arange(S)
    seg_i, t_i = T // 512, T % 512
    a_i, b_i = t_i // 32, t_i % 32
    f = np.arange(D)
    j_i, v_i = f // 128, f % 128
    _rows = a_i[None, :] * 128 + v_i[:, None]
    _cols = seg_i[None, :] * 512 + b_i[None, :] * 16 + j_i[:, None]
    att1 = [np.ascontiguousarray(r3.results[0]["att"][_rows, _cols]),
            np.ascontiguousarray(r3.results[4]["att"][_rows, _cols])]

    nc4 = _get_nc("b2", build_b2)
    common4 = {
        "wo": np.asarray(inputs["L1_attn_Wo"]),
        "conf_t": np.ascontiguousarray(np.asarray(inputs["L1_ffn_conf_W"]).T),
        "conf_b": np.asarray(inputs["L1_ffn_conf_b"]).reshape(1, 8),
        "wealth": np.asarray(inputs["L1_ffn_expert_wealth"]).reshape(1, 8),
        "bgate": np.asarray(inputs["L1_ffn_base_gate"]),
        "bup": np.asarray(inputs["L1_ffn_base_up"]),
        "bdown": np.asarray(inputs["L1_ffn_base_down"]),
        "gA": np.asarray(inputs["L1_ffn_gate_A"]), "gB": np.asarray(inputs["L1_ffn_gate_B"]),
        "uA": np.asarray(inputs["L1_ffn_up_A"]), "uB": np.asarray(inputs["L1_ffn_up_B"]),
        "dA": np.asarray(inputs["L1_ffn_down_A"]), "dB": np.asarray(inputs["L1_ffn_down_B"]),
        "ones1": np.ones((1, 128), BF), "ones128": np.ones((128, 1), BF),
        "sel": np.ascontiguousarray(np.eye(8, dtype=BF)[:, :, None] * np.ones((1, 1, 128), BF)),
        "ident32": np.eye(128, dtype=np.float32),
    }
    maps4 = []
    for c in range(8):
        b = c // 4
        cols = _own_cols(c)
        m = dict(common4)
        m["att_in"] = np.ascontiguousarray(att1[b][:, cols])
        m["x2_in"] = x2[c]
        maps4.append(m)
    r4 = run_bass_kernel_spmd(nc4, maps4, core_ids=cores)

    out = np.zeros((B, S, D), np.float32)
    for c in range(8):
        b = c // 4
        cols = _own_cols(c)
        out[b][cols, :] = r4.results[c]["out"].T
    return out


# version 03
# speedup vs baseline: 1.0837x; 1.0837x over previous
import math
import numpy as np
import ml_dtypes
from contextlib import ExitStack

import concourse.bass as bass
import concourse.mybir as mybir

F32 = mybir.dt.float32
BF16 = mybir.dt.bfloat16
AF = mybir.ActivationFunctionType
ALU = mybir.AluOpType
AX = mybir.AxisListType

BF = ml_dtypes.bfloat16

D = 2048
S = 4096
B = 2
SCALE0 = 1.0 / math.sqrt(128.0)
NEG = -3.0e38


class Sem:
    __slots__ = ("h", "v")

    def __init__(self, ctx, nc, name):
        self.h = ctx.enter_context(nc.semaphore(name))
        self.v = 0


class Em:

    ENGS = ("t", "v", "s", "g", "y")

    def __init__(self, nc, ctx):
        self.nc = nc
        self.ctx = ctx
        self.q = {k: [] for k in self.ENGS}
        self._sems = {}

    def sem(self, name) -> Sem:
        if name not in self._sems:
            self._sems[name] = Sem(self.ctx, self.nc, name)
        return self._sems[name]

    def set_autochain(self, eng, sem):
        if not hasattr(self, "_auto"):
            self._auto = {}
        self._auto[eng] = sem

    def op(self, eng, fn, waits=(), inc=None, inc_n=1):
        ac = getattr(self, "_auto", {}).get(eng)
        if ac is not None:
            waits = list(waits) + [(ac, ac.v)]
            if inc is None:
                inc = ac
        ret = None
        if inc is not None:
            inc.v += inc_n
            ret = inc.v
        wl = tuple((s.h, v) for (s, v) in waits if v > 0)

        def run(e, _fn=fn, _wl=wl, _inc=inc, _n=inc_n):
            for h, v in _wl:
                e.wait_ge(h, v)
            inst = _fn(e)
            if _inc is not None:
                inst.then_inc(_inc.h, _n)
        self.q[eng].append(run)
        return ret

    def wait(self, eng, sem, val):
        if val > 0:
            self.q[eng].append(lambda e, _h=sem.h, _v=val: e.wait_ge(_h, _v))

    def dma(self, eng, out_ap, in_ap, sem, waits=()):
        return self.op(eng, lambda e, _o=out_ap, _i=in_ap: e.dma_start(out=_o, in_=_i),
                       waits=waits, inc=sem, inc_n=16)

    def run_block(self):
        block = self.ctx.enter_context(self.nc.Block())

        def mk(lst):
            def body(eng):
                for fn in lst:
                    fn(eng)
            return body
        block.tensor(mk(self.q["t"]))
        block.vector(mk(self.q["v"]))
        block.scalar(mk(self.q["s"]))
        block.gpsimd(mk(self.q["g"]))
        block.sync(mk(self.q["y"]))


def sb(ctx, nc, name, shape, dt=BF16):
    return ctx.enter_context(nc.sbuf_tensor(name, shape, dt))


def ps(ctx, nc, name, shape, dt=F32):
    return ctx.enter_context(nc.psum_tensor(name, shape, dt))



def host_masks():
    qi = np.arange(128)[:, None]
    kj = np.arange(512)[None, :]
    out = np.zeros((4, 128, 512), np.float32)
    for u in range(4):
        out[u] = np.where(kj <= u * 128 + qi, 0.0, NEG)
    return np.ascontiguousarray(out)


def ident_np():
    return np.eye(128, dtype=BF)


def stripes_of_core(c):
    j = c % 4
    return (j, 7 - j)



def build_a1():
    nc = bass.Bass("TRN2", target_bir_lowering=False)
    xb = nc.dram_tensor("xb", [D, S], F32, kind="ExternalInput")
    wq_t = nc.dram_tensor("wq_t", [D, 512], F32, kind="ExternalInput")
    wk_t = nc.dram_tensor("wk_t", [D, 128], F32, kind="ExternalInput")
    wv_t = nc.dram_tensor("wv_t", [D, 128], F32, kind="ExternalInput")
    masks = nc.dram_tensor("masks", [4, 128, 512], F32, kind="ExternalInput")
    ident = nc.dram_tensor("ident", [128, 128], BF16, kind="ExternalInput")
    att = nc.dram_tensor("att", [4, 128, S], BF16, kind="ExternalOutput")

    KC = D // 128

    ctx = ExitStack()
    with ctx:
        em = Em(nc, ctx)
        wq = sb(ctx, nc, "wq", [128, KC * 512])
        wk = sb(ctx, nc, "wk", [128, KC * 128])
        wv = sb(ctx, nc, "wv", [128, KC * 128])
        xs = [sb(ctx, nc, f"xs{i}", [128, KC * 512]) for i in range(2)]
        qf = [sb(ctx, nc, f"qf{h}", [128, S]) for h in range(4)]
        kf = sb(ctx, nc, "kf", [128, S])
        vt = sb(ctx, nc, "vt", [128, 32 * 128])
        msk = sb(ctx, nc, "msk", [128, 4 * 512], F32)
        idn = sb(ctx, nc, "idn", [128, 128])
        p_sb = [sb(ctx, nc, f"p{h}", [128, 8 * 512]) for h in range(4)]
        pt_sb = sb(ctx, nc, "pt", [128, 512])
        att_sb = [sb(ctx, nc, f"atts{h}", [128, S]) for h in range(4)]
        stat = sb(ctx, nc, "stat", [128, 16], F32)
        sc_ps = ps(ctx, nc, "sc_ps", [128, 3072])
        pt_ps = ps(ctx, nc, "pt_ps", [128, 512], BF16)
        av_ps = ps(ctx, nc, "av_ps", [128, 512])

        ldw = em.sem("ldw")
        ldx = [em.sem("ldx0"), em.sem("ldx1")]
        st = em.sem("st")
        tp = em.sem("tp")
        ta = em.sem("ta")
        vv = em.sem("vv")
        ex = em.sem("ex")

        wqv = wq[:, :].rearrange("p (kc m) -> p kc m", kc=KC)
        wkv = wk[:, :].rearrange("p (kc m) -> p kc m", kc=KC)
        wvv = wv[:, :].rearrange("p (kc m) -> p kc m", kc=KC)

        em.dma("g", wqv, wq_t.ap().rearrange("(kc p) m -> p kc m", p=128), ldw)
        em.dma("g", wkv, wk_t.ap().rearrange("(kc p) m -> p kc m", p=128), ldw)
        em.dma("g", wvv, wv_t.ap().rearrange("(kc p) m -> p kc m", p=128), ldw)
        em.dma("g", msk[:, :].rearrange("p (u m) -> p u m", u=4),
               masks.ap().transpose([1, 0, 2]), ldw)
        v_allw = em.dma("g", idn[:, :], ident[:, :], ldw)

        xb_v = xb.ap().rearrange("(kc p) t -> p kc t", p=128)
        x_use = [0, 0]
        for tc in range(8):
            buf = tc % 2
            tsl = slice(tc * 512, (tc + 1) * 512)
            v_x = em.dma("g", xs[buf][:, :].rearrange("p (kc t) -> p kc t", kc=KC),
                         xb_v[:, :, tsl], ldx[buf],
                         waits=[(tp, x_use[buf])])
            xr = xs[buf][:, :].rearrange("p (kc t) -> p kc t", kc=KC)
            em.wait("t", vv, 9 * tc)
            em.wait("t", ldw, v_allw)
            em.wait("t", ldx[buf], v_x)
            for oc in range(5):
                dst = sc_ps[:, oc * 512:(oc + 1) * 512]
                for kc in range(KC):
                    w_ap = wqv[:, kc, oc * 128:(oc + 1) * 128] if oc < 4 else wkv[:, kc, :]
                    em.op("t", lambda e, _d=dst, _w=w_ap, _x=xr[:, kc, :],
                          _s=(kc == 0), _p=(kc == KC - 1):
                          e.matmul(_d, _w, _x, start=_s, stop=_p),
                          inc=tp if kc == KC - 1 else None)
            xr4 = xr.rearrange("p kc (a q) -> p kc a q", a=4)
            for stt in range(4):
                dst = av_ps[:, stt * 128:(stt + 1) * 128]
                for kc in range(KC):
                    em.op("t", lambda e, _d=dst, _x=xr4[:, kc, stt, :], _w=wvv[:, kc, :],
                          _s=(kc == 0), _p=(kc == KC - 1):
                          e.matmul(_d, _x, _w, start=_s, stop=_p),
                          inc=tp if kc == KC - 1 else None)
            x_use[buf] = tp.v
            base = 9 * tc
            for oc in range(4):
                em.op("v", lambda e, _d=qf[oc][:, tsl], _s=sc_ps[:, oc * 512:(oc + 1) * 512]:
                      e.tensor_copy(_d, _s), waits=[(tp, base + oc + 1)], inc=vv)
            em.op("v", lambda e, _d=kf[:, tsl], _s=sc_ps[:, 4 * 512:5 * 512]:
                  e.tensor_copy(_d, _s), waits=[(tp, base + 5)], inc=vv)
            for stt in range(4):
                dcol = (tc * 4 + stt) * 128
                em.op("v", lambda e, _d=vt[:, dcol:dcol + 128], _s=av_ps[:, stt * 128:(stt + 1) * 128]:
                      e.tensor_copy(_d, _s), waits=[(tp, base + 9)], inc=vv)
        proj_done = vv.v

        def stc(i):
            return stat[:, i:i + 1]

        em.wait("t", vv, proj_done)
        em.wait("v", ldw, v_allw)
        for t in range(32):
            s_stripe, u = t // 4, t % 4
            nkc = s_stripe + 1
            passes = [(0, min(nkc, 6))]
            if nkc > 6:
                passes.append((6, nkc))
            em.wait("s", ta, ta.v)
            em.wait("v", ta, ta.v)
            for h in range(4):
                q_ap = qf[h][:, t * 128:(t + 1) * 128]
                em.wait("v", ex, ex.v)
                for pi, (c0, c1) in enumerate(passes):
                    em.wait("t", ex, ex.v)
                    for c in range(c0, c1):
                        em.op("t", lambda e, _d=sc_ps[:, (c - c0) * 512:(c - c0 + 1) * 512],
                              _q=q_ap, _k=kf[:, c * 512:(c + 1) * 512]:
                              e.matmul(_d, _q, _k, start=True, stop=True),
                              inc=tp if c == c1 - 1 else None)
                    plen = (c1 - c0) * 512
                    wl = [(tp, tp.v)]
                    if pi == len(passes) - 1:
                        moff = (nkc - 1 - c0) * 512
                        em.op("v", lambda e, _d=sc_ps[:, moff:moff + 512], _m=msk[:, u * 512:(u + 1) * 512]:
                              e.tensor_tensor(_d, _d, _m, ALU.add), waits=wl, inc=vv)
                        wl = [(vv, vv.v)]
                    mcol = 0 if pi == 0 else 1
                    em.op("v", lambda e, _d=stc(mcol), _s=sc_ps[:, 0:plen]:
                          e.reduce_max(_d, _s, axis=AX.X), waits=wl, inc=vv)
                    em.wait("v", vv, vv.v)
                    if pi == 0:
                        em.op("v", lambda e: e.tensor_scalar_mul(stc(3), stc(0), -SCALE0), inc=vv)
                    else:
                        em.op("v", lambda e: e.tensor_tensor(stc(2), stc(0), stc(1), ALU.max), inc=vv)
                        em.wait("v", vv, vv.v)
                        em.op("v", lambda e: e.tensor_scalar_mul(stc(3), stc(2), -SCALE0), inc=vv)
                    vb = vv.v
                    scol = 4 if pi == 0 else 5
                    em.op("s", lambda e, _d=p_sb[h][:, c0 * 512:c0 * 512 + plen], _s=sc_ps[:, 0:plen],
                          _b=stc(3), _a=stc(scol):
                          e.activation(_d, _s, AF.Exp, bias=_b, scale=SCALE0, accum_out=_a),
                          waits=[(vv, vb)], inc=ex)
                if len(passes) == 1:
                    em.op("v", lambda e: e.reciprocal(stc(7), stc(4)), waits=[(ex, ex.v)], inc=vv)
                    em.wait("v", vv, vv.v)
                    tot = nkc * 512
                    em.op("v", lambda e, _p=p_sb[h][:, 0:tot]:
                          e.tensor_scalar_mul(_p, _p, stc(7)), inc=vv)
                else:
                    em.op("v", lambda e: e.tensor_tensor(stc(8), stc(0), stc(2), ALU.subtract),
                          waits=[(ex, ex.v)], inc=vv)
                    em.op("s", lambda e: e.activation(stc(9), stc(8), AF.Exp, bias=0.0, scale=SCALE0),
                          waits=[(vv, vv.v)], inc=ex)
                    em.op("v", lambda e: e.scalar_tensor_tensor(stc(6), stc(4), stc(9), stc(5),
                                                                ALU.mult, ALU.add),
                          waits=[(ex, ex.v)], inc=vv)
                    em.wait("v", vv, vv.v)
                    em.op("v", lambda e: e.reciprocal(stc(7), stc(6)), inc=vv)
                    em.wait("v", vv, vv.v)
                    em.op("v", lambda e: e.tensor_mul(stc(10), stc(9), stc(7)), inc=vv)
                    em.wait("v", vv, vv.v)
                    em.op("v", lambda e, _p=p_sb[h][:, 0:3072]:
                          e.tensor_scalar_mul(_p, _p, stc(10)), inc=vv)
                    tot2 = (nkc - 6) * 512
                    em.op("v", lambda e, _p=p_sb[h][:, 3072:3072 + tot2]:
                          e.tensor_scalar_mul(_p, _p, stc(7)), inc=vv)
            p_done = vv.v
            nkb = nkc * 4
            em.wait("t", vv, p_done)
            for kb in range(nkb):
                for h in range(4):
                    em.op("t", lambda e, _d=pt_ps[:, h * 128:(h + 1) * 128],
                          _s=p_sb[h][:, kb * 128:(kb + 1) * 128]:
                          e.transpose(_d, _s, idn[:, :]),
                          waits=[(vv, vv.v)] if h == 0 else (),
                          inc=ta if h == 3 else None)
                em.op("v", lambda e, _d=pt_sb[:, :], _s=pt_ps[:, :]:
                      e.tensor_copy(_d, _s), waits=[(ta, ta.v)], inc=vv)
                em.op("t", lambda e, _v=vt[:, kb * 128:(kb + 1) * 128], _p=pt_sb[:, :],
                      _s=(kb == 0), _e=(kb == nkb - 1):
                      e.matmul(av_ps[:, :], _v, _p, start=_s, stop=_e),
                      waits=[(vv, vv.v)], inc=ta if kb == nkb - 1 else None)
            for h in range(4):
                em.op("v", lambda e, _d=att_sb[h][:, t * 128:(t + 1) * 128],
                      _s=av_ps[:, h * 128:(h + 1) * 128]:
                      e.tensor_copy(_d, _s), waits=[(ta, ta.v)] if h == 0 else (), inc=vv)
        fin = vv.v
        for h in range(4):
            em.op("y", lambda e, _d=att[h, :, :], _s=att_sb[h][:, :]:
                  e.dma_start(out=_d, in_=_s), waits=[(vv, fin)], inc=st, inc_n=16)
        em.wait("y", st, st.v)
        em.run_block()
    return nc


def a1_in_maps(inputs):
    x = np.asarray(inputs["L0_attn_x"])
    wqkv = np.asarray(inputs["L0_attn_weight_qkv"])
    masks = host_masks()
    idn = ident_np()
    maps = []
    for c in range(8):
        b, g = c // 4, c % 4
        xb = np.ascontiguousarray(x[b].T)
        base = g * 768
        wq_t = np.ascontiguousarray(wqkv[base:base + 512].T)
        wk_t = np.ascontiguousarray(wqkv[base + 512:base + 640].T)
        wv_t = np.ascontiguousarray(wqkv[base + 640:base + 768].T)
        maps.append({"xb": xb, "wq_t": wq_t, "wk_t": wk_t, "wv_t": wv_t,
                     "masks": masks, "ident": idn})
    return maps



EPS = 1e-5


def build_a2():
    nc = bass.Bass("TRN2", target_bir_lowering=False)
    KC = D // 128
    T_ = 1024
    IE = 1408
    ICH = IE // 128
    ISH = 2816
    ISCH = ISH // 128

    att_in = nc.dram_tensor("att_in", [D, T_], BF16, kind="ExternalInput")
    x_res = nc.dram_tensor("x_res", [D, T_], F32, kind="ExternalInput")
    wproj_t = nc.dram_tensor("wproj_t", [D, D], F32, kind="ExternalInput")
    router_t = nc.dram_tensor("router_t", [D, 8], F32, kind="ExternalInput")
    gate_w = nc.dram_tensor("gate_w", [8, D, IE], F32, kind="ExternalInput")
    up_w = nc.dram_tensor("up_w", [8, D, IE], F32, kind="ExternalInput")
    down_w = nc.dram_tensor("down_w", [8, IE, D], F32, kind="ExternalInput")
    sg_w = nc.dram_tensor("sg_w", [D, ISH], F32, kind="ExternalInput")
    su_w = nc.dram_tensor("su_w", [D, ISH], F32, kind="ExternalInput")
    sd_w = nc.dram_tensor("sd_w", [ISH, D], F32, kind="ExternalInput")
    sg_b = nc.dram_tensor("sg_b", [ISH], F32, kind="ExternalInput")
    su_b = nc.dram_tensor("su_b", [ISH], F32, kind="ExternalInput")
    sd_b = nc.dram_tensor("sd_b", [D], F32, kind="ExternalInput")
    wq1 = nc.dram_tensor("wq1", [D, D], F32, kind="ExternalInput")
    wk1 = nc.dram_tensor("wk1", [D, D], F32, kind="ExternalInput")
    wv1 = nc.dram_tensor("wv1", [D, D], F32, kind="ExternalInput")
    ones1 = nc.dram_tensor("ones1", [1, 128], BF16, kind="ExternalInput")
    ones128 = nc.dram_tensor("ones128", [128, 1], BF16, kind="ExternalInput")
    sel = nc.dram_tensor("sel", [8, 8, 128], BF16, kind="ExternalInput")
    ident32 = nc.dram_tensor("ident32", [128, 128], F32, kind="ExternalInput")

    x2_out = nc.dram_tensor("x2_out", [D, T_], F32, kind="ExternalOutput")
    q1_out = nc.dram_tensor("q1_out", [D, T_], BF16, kind="ExternalOutput")
    k1_out = nc.dram_tensor("k1_out", [D, T_], BF16, kind="ExternalOutput")
    v1_out = nc.dram_tensor("v1_out", [D, T_], BF16, kind="ExternalOutput")

    ctx = ExitStack()
    with ctx:
        em = Em(nc, ctx)
        stream = sb(ctx, nc, "stream", [128, KC * T_], F32)
        xbf = sb(ctx, nc, "xbf", [128, KC * T_])
        att_sb = sb(ctx, nc, "att_sb", [128, 8 * T_])
        h_sb = sb(ctx, nc, "h_sb", [128, ICH * T_])
        cbc = sb(ctx, nc, "cbc", [128, T_])
        wb = [sb(ctx, nc, f"wb{i}", [128, 16 * 128]) for i in range(8)]
        s_ch = sb(ctx, nc, "s_ch", [128, T_])
        u_ch = sb(ctx, nc, "u_ch", [128, T_])
        sq_ch = [sb(ctx, nc, "sq0", [128, T_])] * 2
        sq2 = [sb(ctx, nc, "sq2_0", [128, T_])] * 2
        rwt = sb(ctx, nc, "rwt", [128, KC * 8])
        comb = sb(ctx, nc, "comb", [128, 8], F32)
        comb_fm = sb(ctx, nc, "comb_fm", [8, T_])
        rstat = sb(ctx, nc, "rstat", [128, 28], F32)
        lnrow = sb(ctx, nc, "lnrow", [1, 4 * T_], F32)
        lnrow_bf = sb(ctx, nc, "lnrow_bf", [1, 2 * T_])
        o1 = sb(ctx, nc, "o1", [1, 128])
        o128 = sb(ctx, nc, "o128", [128, 1])
        sel_sb = sb(ctx, nc, "sel_sb", [8, 8 * 128])
        id32 = sb(ctx, nc, "id32", [128, 128], F32)
        sgb = sb(ctx, nc, "sgb", [128, ISCH], F32)
        sub = sb(ctx, nc, "sub", [128, ISCH], F32)
        sdb = sb(ctx, nc, "sdb", [128, KC], F32)
        stg = [sb(ctx, nc, f"stg{i}", [128, T_]) for i in range(2)]
        pool = ps(ctx, nc, "pool", [128, 4096])
        A = pool[:, 0:1024]
        Bk = pool[:, 1024:2048]
        C = pool[:, 2048:3072]
        Dk = pool[:, 3072:4096]

        ldc = em.sem("ldc")
        ldg = em.sem("ldg")
        wsem = [em.sem(f"ws{i}") for i in range(8)]
        st = em.sem("st")
        tq = em.sem("tq")
        vv = em.sem("vv")
        ss = em.sem("ss")
        em.set_autochain("v", vv)
        em.set_autochain("s", ss)

        em.dma("y", stream[:, :].rearrange("p (kc t) -> p kc t", kc=KC),
               x_res.ap().rearrange("(kc p) t -> p kc t", p=128), ldc)
        for i in range(ISCH):
            em.dma("y", sgb[:, i:i + 1], sg_b.ap().rearrange("(i p) -> i p", p=128)[i:i + 1, :].transpose([1, 0]), ldc)
            em.dma("y", sub[:, i:i + 1], su_b.ap().rearrange("(i p) -> i p", p=128)[i:i + 1, :].transpose([1, 0]), ldc)
        for i in range(KC):
            em.dma("y", sdb[:, i:i + 1], sd_b.ap().rearrange("(i p) -> i p", p=128)[i:i + 1, :].transpose([1, 0]), ldc)
        em.dma("y", o1[:, :], ones1[:, :], ldc)
        em.dma("y", o128[:, :], ones128[:, :], ldc)
        em.dma("y", sel_sb[:, :].rearrange("p (e m) -> p e m", e=8),
               sel.ap().transpose([1, 0, 2]), ldc)
        v_ldc = em.dma("y", id32[:, :], ident32[:, :], ldc)
        v_rwt = em.dma("g", rwt[:, :].rearrange("p (kc m) -> p kc m", kc=KC),
                       router_t.ap().rearrange("(kc p) m -> p kc m", p=128), ldg)
        em.wait("t", ldc, v_ldc)
        em.wait("v", ldc, v_ldc)
        em.wait("s", ldc, v_ldc)

        wuse = [0] * 8

        def wload(i, dram_slice):
            kcs = dram_slice.shape[0] // 128
            return em.dma("g", wb[i][:, 0:kcs * 128].rearrange("p (kc m) -> p kc m", kc=kcs),
                          dram_slice.rearrange("(kc p) m -> p kc m", p=128),
                          wsem[i], waits=[(tq, wuse[i])])

        strv = stream[:, :].rearrange("p (kc t) -> p kc t", kc=KC)
        xbv = xbf[:, :].rearrange("p (kc t) -> p kc t", kc=KC)
        attv = att_sb[:, :].rearrange("p (kc t) -> p kc t", kc=8)

        def wbv(i, kcs=16):
            return wb[i][:, 0:kcs * 128].rearrange("p (kc m) -> p kc m", kc=kcs)

        lda = em.sem("lda")
        for kh in range(2):
            v_att = em.dma("g", attv[:, :, :],
                           att_in.ap().rearrange("(kc p) t -> p kc t", p=128)[:, kh * 8:(kh + 1) * 8, :],
                           lda, waits=[(vv, vv.v)])
            em.wait("t", lda, v_att)
            for oc in range(KC):
                wi = oc % 2
                v_w = wload(wi, wproj_t.ap()[kh * 1024:(kh + 1) * 1024, oc * 128:(oc + 1) * 128])
                dst = A if oc % 2 == 0 else Bk
                em.wait("t", wsem[wi], v_w)
                em.wait("t", vv, vv.v)
                for hf in range(2):
                    for kc in range(8):
                        em.op("t", lambda e, _d=dst[:, hf * 512:(hf + 1) * 512],
                              _w=wbv(wi, 8)[:, kc, :],
                              _x=attv[:, kc, hf * 512:(hf + 1) * 512],
                              _s=(kc == 0), _p=(kc == 7):
                              e.matmul(_d, _w, _x, start=_s, stop=_p),
                              inc=tq if (kc == 7 and hf == 1) else None)
                wuse[wi] = tq.v
                em.op("v", lambda e, _d=strv[:, oc, :], _s=dst[:, :]:
                      e.tensor_tensor(_d, _d, _s, ALU.add), waits=[(tq, tq.v)], inc=vv)

        def layernorm():
            em.wait("s", vv, vv.v)
            squse = [0, 0]
            for kc in range(KC):
                b2 = 0
                em.op("v", lambda e, _d=sq_ch[b2][:, :], _s=strv[:, kc, :]:
                      e.tensor_copy(_d, _s), waits=[(tq, squse[b2])], inc=vv)
                em.op("s", lambda e, _d=sq2[b2][:, :], _s=strv[:, kc, :]:
                      e.activation(_d, _s, AF.Square, bias=0.0, scale=1.0),
                      waits=[(tq, squse[b2])], inc=ss)
                for hf in range(2):
                    em.op("t", lambda e, _d=pool[0:1, 2048 + hf * 512:2048 + (hf + 1) * 512],
                          _o=o128[:, :], _x=sq_ch[b2][:, hf * 512:(hf + 1) * 512],
                          _s=(kc == 0), _p=(kc == KC - 1):
                          e.matmul(_d, _o, _x, start=_s, stop=_p),
                          waits=[(vv, vv.v)] if hf == 0 else (),
                          inc=tq if hf == 1 else None)
                for hf in range(2):
                    em.op("t", lambda e, _d=pool[0:1, 3072 + hf * 512:3072 + (hf + 1) * 512],
                          _o=o128[:, :], _x=sq2[b2][:, hf * 512:(hf + 1) * 512],
                          _s=(kc == 0), _p=(kc == KC - 1):
                          e.matmul(_d, _o, _x, start=_s, stop=_p),
                          waits=[(ss, ss.v)] if hf == 0 else (),
                          inc=tq if hf == 1 else None)
                squse[b2] = tq.v
            Ar, Br = lnrow[:, 0:T_], lnrow[:, T_:2 * T_]
            Cr, Dr = lnrow[:, 2 * T_:3 * T_], lnrow[:, 3 * T_:4 * T_]
            em.op("v", lambda e: e.tensor_copy(Ar, pool[0:1, 2048:3072]),
                  waits=[(tq, tq.v)], inc=vv)
            em.op("v", lambda e: e.tensor_copy(Br, pool[0:1, 3072:4096]), inc=vv)
            em.wait("v", vv, vv.v)
            em.op("v", lambda e: e.tensor_scalar_mul(Cr, Ar, 1.0 / D), inc=vv)
            em.wait("v", vv, vv.v)
            em.op("v", lambda e: e.tensor_scalar_mul(Ar, Br, 1.0 / D), inc=vv)
            em.wait("v", vv, vv.v)
            em.op("v", lambda e: e.tensor_mul(Br, Cr, Cr), inc=vv)
            em.wait("v", vv, vv.v)
            em.op("v", lambda e: e.tensor_sub(Ar, Ar, Br), inc=vv)
            em.wait("v", vv, vv.v)
            em.op("v", lambda e: e.tensor_scalar_add(Ar, Ar, EPS), inc=vv)
            em.op("s", lambda e: e.activation(Br, Ar, AF.Ln, bias=0.0, scale=1.0),
                  waits=[(vv, vv.v)], inc=ss)
            em.wait("s", ss, ss.v)
            em.op("s", lambda e: e.activation(Dr, Br, AF.Exp, bias=0.0, scale=-0.5),
                  inc=ss)
            em.op("v", lambda e: e.tensor_mul(Ar, Cr, Dr),
                  waits=[(ss, ss.v)], inc=vv)
            em.wait("v", vv, vv.v)
            em.op("v", lambda e: e.tensor_copy(lnrow_bf[:, 0:T_], Dr), inc=vv)
            em.op("v", lambda e: e.tensor_copy(lnrow_bf[:, T_:2 * T_], Ar), inc=vv)
            for hf in range(2):
                em.op("t", lambda e, _d=C[:, hf * 512:(hf + 1) * 512], _r=lnrow_bf[:, hf * 512:(hf + 1) * 512]:
                      e.matmul(_d, o1[:, :], _r, start=True, stop=True),
                      waits=[(vv, vv.v)] if hf == 0 else (), inc=tq if hf == 1 else None)
                em.op("t", lambda e, _d=Dk[:, hf * 512:(hf + 1) * 512], _r=lnrow_bf[:, T_ + hf * 512:T_ + (hf + 1) * 512]:
                      e.matmul(_d, o1[:, :], _r, start=True, stop=True),
                      inc=tq if hf == 1 else None)
            for kc in range(KC):
                em.op("v", lambda e, _x=strv[:, kc, :], _a=C[:, :]:
                      e.tensor_mul(_x, _x, _a), waits=[(tq, tq.v)] if kc == 0 else (), inc=vv)
                em.wait("v", vv, vv.v)
                em.op("v", lambda e, _x=strv[:, kc, :], _b=Dk[:, :]:
                      e.tensor_sub(_x, _x, _b), inc=vv)
                em.wait("v", vv, vv.v)
                em.op("v", lambda e, _d=xbv[:, kc, :], _s=strv[:, kc, :]:
                      e.tensor_copy(_d, _s), inc=vv)
            em.wait("t", vv, vv.v)

        layernorm()

        for tt in range(8):
            em.wait("t", ldg, v_rwt)
            for kc in range(KC):
                em.op("t", lambda e, _d=C[:, 0:8],
                      _x=xbv[:, kc, tt * 128:(tt + 1) * 128],
                      _w=rwt[:, :].rearrange("p (kc m) -> p kc m", kc=KC)[:, kc, :],
                      _s=(kc == 0), _p=(kc == KC - 1):
                      e.matmul(_d, _x, _w, start=_s, stop=_p),
                      inc=tq if kc == KC - 1 else None)
            em.op("v", lambda e: e.reduce_max(rstat[:, 0:1], C[:, 0:8], axis=AX.X),
                  waits=[(tq, tq.v)], inc=vv)
            em.wait("v", vv, vv.v)
            em.op("v", lambda e: e.tensor_scalar_mul(rstat[:, 1:2], rstat[:, 0:1], -1.0), inc=vv)
            em.op("s", lambda e: e.activation(comb[:, 0:8], C[:, 0:8], AF.Exp,
                                              bias=rstat[:, 1:2], scale=1.0),
                  waits=[(vv, vv.v), (tq, tq.v)], inc=ss)
            em.op("v", lambda e: e.reduce_max(rstat[:, 2:3], comb[:, 0:8], axis=AX.X),
                  waits=[(ss, ss.v)], inc=vv)
            em.wait("v", vv, vv.v)
            em.op("v", lambda e: e.tensor_single_scalar(rstat[:, 12:20], comb[:, 0:8], rstat[:, 2:3], ALU.is_equal), inc=vv)
            em.wait("v", vv, vv.v)
            em.op("v", lambda e: e.scalar_tensor_tensor(rstat[:, 20:28], rstat[:, 12:20], -1.0e30,
                                                        comb[:, 0:8], ALU.mult, ALU.add), inc=vv)
            em.wait("v", vv, vv.v)
            em.op("v", lambda e: e.reduce_max(rstat[:, 3:4], rstat[:, 20:28], axis=AX.X), inc=vv)
            em.wait("v", vv, vv.v)
            em.op("v", lambda e: e.tensor_single_scalar(rstat[:, 20:28], comb[:, 0:8], rstat[:, 3:4], ALU.is_equal), inc=vv)
            em.op("v", lambda e: e.tensor_add(rstat[:, 4:5], rstat[:, 2:3], rstat[:, 3:4]), inc=vv)
            em.wait("v", vv, vv.v)
            em.op("v", lambda e: e.reciprocal(rstat[:, 5:6], rstat[:, 4:5]), inc=vv)
            em.wait("v", vv, vv.v)
            em.op("v", lambda e: e.tensor_mul(rstat[:, 6:7], rstat[:, 2:3], rstat[:, 5:6]), inc=vv)
            em.op("v", lambda e: e.tensor_mul(rstat[:, 7:8], rstat[:, 3:4], rstat[:, 5:6]), inc=vv)
            em.wait("v", vv, vv.v)
            em.op("v", lambda e: e.tensor_single_scalar(rstat[:, 12:20], rstat[:, 12:20], rstat[:, 6:7], ALU.mult), inc=vv)
            em.wait("v", vv, vv.v)
            em.op("v", lambda e: e.scalar_tensor_tensor(comb[:, 0:8], rstat[:, 20:28], rstat[:, 7:8],
                                                        rstat[:, 12:20], ALU.mult, ALU.add), inc=vv)
            em.op("t", lambda e, _d=Dk[0:8, 0:128], _s=comb[:, 0:8]:
                  e.transpose(_d, _s, id32[:, :]),
                  waits=[(vv, vv.v)], inc=tq)
            em.op("v", lambda e, _d=comb_fm[0:8, tt * 128:(tt + 1) * 128], _s=Dk[0:8, 0:128]:
                  e.tensor_copy(_d, _s), waits=[(tq, tq.v)], inc=vv)
        comb_done = vv.v

        def gu_half(dst, wi, hf):
            for kc in range(KC):
                em.op("t", lambda e, _d=dst[:, hf * 512:(hf + 1) * 512],
                      _w=wbv(wi)[:, kc, :],
                      _x=xbv[:, kc, hf * 512:(hf + 1) * 512],
                      _s=(kc == 0), _p=(kc == KC - 1):
                      e.matmul(_d, _w, _x, start=_s, stop=_p),
                      inc=tq if kc == KC - 1 else None)

        for e_ in range(8):
            for hf in range(2):
                em.op("t", lambda e, _d=C[:, hf * 512:(hf + 1) * 512],
                      _s=sel_sb[:, :].rearrange("p (e m) -> p e m", e=8)[:, e_, :],
                      _r=comb_fm[0:8, hf * 512:(hf + 1) * 512]:
                      e.matmul(_d, _s, _r, start=True, stop=True),
                      waits=[(vv, comb_done if e_ == 0 else vv.v)] if hf == 0 else (),
                      inc=tq if hf == 1 else None)
            em.op("v", lambda e: e.tensor_copy(cbc[:, :], C[:, :]), waits=[(tq, tq.v)], inc=vv)
            cb_ready = vv.v
            for ii in range(ICH):
                wg_i, wu_i = 2 + ii % 2, 4 + ii % 2
                v_wg = wload(wg_i, gate_w.ap()[e_, :, ii * 128:(ii + 1) * 128])
                v_wu = wload(wu_i, up_w.ap()[e_, :, ii * 128:(ii + 1) * 128])
                em.wait("t", wsem[wg_i], v_wg)
                em.wait("t", wsem[wu_i], v_wu)
                em.wait("t", ss, ss.v)
                em.wait("t", vv, vv.v)
                gu_half(A, wg_i, 0); gu_half(A, wg_i, 1)
                wuse[wg_i] = tq.v
                g_done = tq.v
                gu_half(Bk, wu_i, 0); gu_half(Bk, wu_i, 1)
                wuse[wu_i] = tq.v
                em.op("s", lambda e: e.activation(s_ch[:, :], A[:, :], AF.Silu, bias=0.0, scale=1.0),
                      waits=[(tq, g_done), (vv, vv.v)], inc=ss)
                em.op("v", lambda e: e.tensor_tensor(u_ch[:, :], s_ch[:, :], Bk[:, :], ALU.mult),
                      waits=[(ss, ss.v), (tq, tq.v), (vv, cb_ready)], inc=vv)
                em.wait("v", vv, vv.v)
                em.op("v", lambda e, _d=h_sb[:, ii * T_:(ii + 1) * T_]:
                      e.tensor_tensor(_d, u_ch[:, :], cbc[:, :], ALU.mult), inc=vv)
            h_done = vv.v
            for oc in range(KC):
                wdi = 6 + (oc % 2)
                v_wd = wload(wdi, down_w.ap()[e_, :, oc * 128:(oc + 1) * 128])
                dst = C if oc % 2 == 0 else Dk
                em.wait("t", wsem[wdi], v_wd)
                em.wait("t", vv, h_done if oc == 0 else vv.v)
                for hf in range(2):
                    for ii in range(ICH):
                        em.op("t", lambda e, _d=dst[:, hf * 512:(hf + 1) * 512],
                              _w=wbv(wdi, ICH)[:, ii, :],
                              _h=h_sb[:, ii * T_ + hf * 512: ii * T_ + (hf + 1) * 512],
                              _s=(ii == 0), _p=(ii == ICH - 1):
                              e.matmul(_d, _w, _h, start=_s, stop=_p),
                              inc=tq if (ii == ICH - 1 and hf == 1) else None)
                wuse[wdi] = tq.v
                em.op("v", lambda e, _x=strv[:, oc, :], _s=dst[:, :]:
                      e.tensor_tensor(_x, _x, _s, ALU.add), waits=[(tq, tq.v)], inc=vv)

        for half in range(2):
            ib0 = half * 11
            for ii in range(11):
                ic = ib0 + ii
                wg_i, wu_i = 2 + ii % 2, 4 + ii % 2
                v_wg = wload(wg_i, sg_w.ap()[:, ic * 128:(ic + 1) * 128])
                v_wu = wload(wu_i, su_w.ap()[:, ic * 128:(ic + 1) * 128])
                em.wait("t", wsem[wg_i], v_wg)
                em.wait("t", wsem[wu_i], v_wu)
                em.wait("t", ss, ss.v)
                em.wait("t", vv, vv.v)
                gu_half(A, wg_i, 0); gu_half(A, wg_i, 1)
                wuse[wg_i] = tq.v
                g_done = tq.v
                gu_half(Bk, wu_i, 0); gu_half(Bk, wu_i, 1)
                wuse[wu_i] = tq.v
                em.op("s", lambda e, _b=sgb[:, ic:ic + 1]:
                      e.activation(s_ch[:, :], A[:, :], AF.Silu, bias=_b, scale=1.0),
                      waits=[(tq, g_done), (vv, vv.v)], inc=ss)
                em.op("v", lambda e, _b=sub[:, ic:ic + 1]:
                      e.tensor_scalar_add(u_ch[:, :], Bk[:, :], _b),
                      waits=[(tq, tq.v)], inc=vv)
                em.wait("v", vv, vv.v)
                em.op("v", lambda e, _d=h_sb[:, ii * T_:(ii + 1) * T_]:
                      e.tensor_tensor(_d, s_ch[:, :], u_ch[:, :], ALU.mult),
                      waits=[(ss, ss.v)], inc=vv)
            h_done = vv.v
            for oc in range(KC):
                wdi = 6 + (oc % 2)
                v_wd = wload(wdi, sd_w.ap()[ib0 * 128:(ib0 + 11) * 128, oc * 128:(oc + 1) * 128])
                dst = C if oc % 2 == 0 else Dk
                em.wait("t", wsem[wdi], v_wd)
                em.wait("t", vv, h_done if oc == 0 else vv.v)
                for hf in range(2):
                    for ii in range(11):
                        em.op("t", lambda e, _d=dst[:, hf * 512:(hf + 1) * 512],
                              _w=wbv(wdi, 11)[:, ii, :],
                              _h=h_sb[:, ii * T_ + hf * 512: ii * T_ + (hf + 1) * 512],
                              _s=(ii == 0), _p=(ii == 11 - 1):
                              e.matmul(_d, _w, _h, start=_s, stop=_p),
                              inc=tq if (ii == 11 - 1 and hf == 1) else None)
                wuse[wdi] = tq.v
                em.op("v", lambda e, _x=strv[:, oc, :], _s=dst[:, :]:
                      e.tensor_tensor(_x, _x, _s, ALU.add), waits=[(tq, tq.v)], inc=vv)
        for oc in range(KC):
            em.op("v", lambda e, _x=strv[:, oc, :], _b=sdb[:, oc:oc + 1]:
                  e.tensor_scalar_add(_x, _x, _b), inc=vv)
        em.wait("t", vv, vv.v)

        layernorm()
        x2_ready = vv.v
        for kc in range(KC):
            em.op("y", lambda e, _d=x2_out.ap().rearrange("(kc p) t -> p kc t", p=128)[:, kc, :],
                  _s=strv[:, kc, :]:
                  e.dma_start(out=_d, in_=_s), waits=[(vv, x2_ready)], inc=st, inc_n=16)

        stsem = [em.sem("stg0"), em.sem("stg1")]
        stguse = [0, 0]
        for wt, outt in ((wq1, q1_out), (wk1, k1_out), (wv1, v1_out)):
            for oc in range(KC):
                wi = oc % 2
                v_w = wload(wi, wt.ap()[:, oc * 128:(oc + 1) * 128])
                dst = A if oc % 2 == 0 else Bk
                sgi = oc % 2
                em.wait("t", wsem[wi], v_w)
                em.wait("t", vv, vv.v)
                for hf in range(2):
                    for kc in range(KC):
                        em.op("t", lambda e, _d=dst[:, hf * 512:(hf + 1) * 512],
                              _w=wbv(wi)[:, kc, :],
                              _x=xbv[:, kc, hf * 512:(hf + 1) * 512],
                              _s=(kc == 0), _p=(kc == KC - 1):
                              e.matmul(_d, _w, _x, start=_s, stop=_p),
                              inc=tq if (kc == KC - 1 and hf == 1) else None)
                wuse[wi] = tq.v
                em.op("v", lambda e, _d=stg[sgi][:, :], _s=dst[:, :]:
                      e.tensor_copy(_d, _s), waits=[(tq, tq.v), (stsem[sgi], stguse[sgi])], inc=vv)
                stguse[sgi] = em.dma("y", outt.ap().rearrange("(kc p) t -> p kc t", p=128)[:, oc, :],
                                     stg[sgi][:, :], stsem[sgi], waits=[(vv, vv.v)])
        em.wait("y", st, st.v)
        em.wait("y", stsem[0], stguse[0])
        em.wait("y", stsem[1], stguse[1])
        em.run_block()
    return nc


def a2_in_maps(inputs, att_full):
    x = np.asarray(inputs["L0_attn_x"])
    maps = []
    common = {
        "wproj_t": np.ascontiguousarray(np.asarray(inputs["L0_attn_weight_proj"]).T),
        "router_t": np.ascontiguousarray(np.asarray(inputs["L0_ffn_router_w"]).T),
        "gate_w": np.asarray(inputs["L0_ffn_gate_w"]),
        "up_w": np.asarray(inputs["L0_ffn_up_w"]),
        "down_w": np.asarray(inputs["L0_ffn_down_w"]),
        "sg_w": np.asarray(inputs["L0_ffn_s_gate_w"]),
        "su_w": np.asarray(inputs["L0_ffn_s_up_w"]),
        "sd_w": np.asarray(inputs["L0_ffn_s_down_w"]),
        "sg_b": np.asarray(inputs["L0_ffn_s_gate_b"]),
        "su_b": np.asarray(inputs["L0_ffn_s_up_b"]),
        "sd_b": np.asarray(inputs["L0_ffn_s_down_b"]),
        "wq1": np.asarray(inputs["L1_attn_Wq"]),
        "wk1": np.asarray(inputs["L1_attn_Wk"]),
        "wv1": np.asarray(inputs["L1_attn_Wv"]),
        "ones1": np.ones((1, 128), BF),
        "ones128": np.ones((128, 1), BF),
        "sel": np.eye(8, dtype=BF)[:, :, None] * np.ones((1, 1, 128), BF),
        "ident32": np.eye(128, dtype=np.float32),
    }
    for c in range(8):
        b = c // 4
        s0, s1 = stripes_of_core(c)
        cols = np.r_[s0 * 512:(s0 + 1) * 512, s1 * 512:(s1 + 1) * 512]
        m = dict(common)
        m["att_in"] = np.ascontiguousarray(att_full[b][:, cols])
        m["x_res"] = np.ascontiguousarray(x[b].T[:, cols].astype(np.float32))
        maps.append(m)
    return maps



def build_b1():
    nc = bass.Bass("TRN2", target_bir_lowering=False)
    H = 16
    SEG = 512

    qf_d = nc.dram_tensor("qf", [D, S], BF16, kind="ExternalInput")
    kf_d = nc.dram_tensor("kf", [D, S], BF16, kind="ExternalInput")
    vt_d = nc.dram_tensor("vt", [S, D], BF16, kind="ExternalInput")
    betasT = nc.dram_tensor("betasT", [128, 16], F32, kind="ExternalInput")
    tri = nc.dram_tensor("tri", [128, 128], F32, kind="ExternalInput")
    ident = nc.dram_tensor("ident", [128, 128], BF16, kind="ExternalInput")
    ones1b = nc.dram_tensor("ones1b", [1, 128], BF16, kind="ExternalInput")
    att_d = nc.dram_tensor("att", [D, S], BF16, kind="ExternalOutput")

    ctx = ExitStack()
    with ctx:
        em = Em(nc, ctx)
        qseg = sb(ctx, nc, "qseg", [128, H * 1024])
        kseg = sb(ctx, nc, "kseg", [128, H * 1024])
        vseg = sb(ctx, nc, "vseg", [128, 8 * D])
        mem_aug = sb(ctx, nc, "mem_aug", [128, H * 129], F32)
        mem_bf = sb(ctx, nc, "mem_bf", [128, H * 129])
        sigk_b = [sb(ctx, nc, f"sigk{i}", [128, 512]) for i in range(2)]
        sigq_b = [sb(ctx, nc, f"sigq{i}", [128, 512]) for i in range(2)]
        sigkT = sb(ctx, nc, "sigkT", [128, 512])
        sck_b = [sb(ctx, nc, f"sck{i}", [128, 512], F32) for i in range(2)]
        sce_b = [sb(ctx, nc, f"sce{i}", [128, 512]) for i in range(2)]
        delt = sb(ctx, nc, "delt", [128, 512])
        p_sb = sb(ctx, nc, "p_sb", [128, 512])
        pt2 = sb(ctx, nc, "pt2", [128, 512])
        adot = sb(ctx, nc, "adot", [128, 512])
        astg = [sb(ctx, nc, f"astg{i}", [128, 512]) for i in range(2)]
        rowr = sb(ctx, nc, "rowr", [1, 1024], F32)
        rowb = sb(ctx, nc, "rowb", [1, 512])
        beta_sb = sb(ctx, nc, "beta_sb", [128, 16], F32)
        omb_sb = sb(ctx, nc, "omb_sb", [128, 16], F32)
        tri_sb = sb(ctx, nc, "tri_sb", [128, 128], F32)
        idn = sb(ctx, nc, "idn", [128, 128])
        o1 = sb(ctx, nc, "o1", [1, 128])
        stat_t = sb(ctx, nc, "stat_t", [128, 16], F32)
        pool = ps(ctx, nc, "pool", [128, 4096])
        sc_ps = pool[:, 0:512]
        ptp = pool[:, 512:1024]
        av_ps = pool[:, 1024:1152]
        md_ps = pool[:, 1152:1280]
        attm_ps = pool[:, 1536:2048]
        bc_ps = pool[:, 2048:2560]
        zq_ps = pool[0:1, 2560:3072]
        ds_ps = [pool[:, 3072:3201], pool[:, 3584:3713]]
        skT_ps = pool[:, 3328:3584]

        ldc = em.sem("ldc")
        lds = [em.sem("lds0"), em.sem("lds1")]
        st = em.sem("st")
        stsem = [em.sem("stg0"), em.sem("stg1")]
        tq = em.sem("tq")
        vv = em.sem("vv")
        ss = em.sem("ss")
        em.set_autochain("v", vv)
        em.set_autochain("s", ss)

        ptp_bf = ptp.tensor.bitcast(BF16)
        skT_bf = skT_ps.tensor.bitcast(BF16)

        em.dma("y", beta_sb[:, :], betasT[:, :], ldc)
        em.dma("y", tri_sb[:, :], tri[:, :], ldc)
        em.dma("y", idn[:, :], ident[:, :], ldc)
        v_ldc = em.dma("y", o1[:, :], ones1b[:, :], ldc)
        em.wait("t", ldc, v_ldc)
        em.wait("v", ldc, v_ldc)
        em.wait("s", ldc, v_ldc)
        em.op("s", lambda e: e.activation(beta_sb[:, :], beta_sb[:, :], AF.Sigmoid, bias=0.0, scale=1.0))
        em.op("v", lambda e: e.scalar_tensor_tensor(omb_sb[:, :], beta_sb[:, :], -1.0, beta_sb[:, :],
                                                    ALU.mult, ALU.bypass), waits=[(ss, ss.v)])
        em.op("v", lambda e: e.tensor_scalar_add(omb_sb[:, :], omb_sb[:, :], 1.0))
        em.op("v", lambda e: e.memset(mem_aug[:, :], 0.0))
        for h in range(H):
            em.op("v", lambda e, _c=mem_aug[:, h * 129 + 128:h * 129 + 129]:
                  e.memset(_c, 1.0 / 128.0))

        qsv = qseg[:, :].rearrange("p (h t) -> p h t", h=H)
        ksv = kseg[:, :].rearrange("p (h t) -> p h t", h=H)
        vsv = vseg[:, :].rearrange("p (s f) -> p s f", s=8)

        sig_use = [0, 0]

        def sigma(dst, src_ap, hp):
            em.op("v", lambda e, _d=sck_b[hp][:, :]: e.tensor_scalar_min(_d, src_ap, 0.0),
                  waits=[(tq, sig_use[hp]), (ss, ss.v)])
            em.op("s", lambda e, _d=sce_b[hp][:, :], _s=sck_b[hp][:, :]:
                  e.activation(_d, _s, AF.Exp, bias=0.0, scale=1.0),
                  waits=[(vv, vv.v)])
            em.op("v", lambda e, _e=sce_b[hp][:, :]: e.scalar_tensor_tensor(dst, src_ap, 0.0, _e,
                                                        ALU.max, ALU.add), waits=[(ss, ss.v)])

        seg_use = [0, 0]
        seg_use_v = [0, 0]
        stg_use = [0, 0]
        for s_ in range(8):
            par = s_ % 2
            tsl = slice(s_ * 512, (s_ + 1) * 512)
            v_q = em.dma("g", qsv[:, :, par * 512:(par + 1) * 512],
                         qf_d.ap().rearrange("(h p) t -> p h t", p=128)[:, :, tsl],
                         lds[par], waits=[(tq, seg_use[par]), (vv, seg_use_v[par])])
            v_k = em.dma("g", ksv[:, :, par * 512:(par + 1) * 512],
                         kf_d.ap().rearrange("(h p) t -> p h t", p=128)[:, :, tsl],
                         lds[par], waits=[])
            v_v = em.dma("g", vsv[:, par * 4:(par + 1) * 4, :],
                         vt_d.ap()[tsl, :].rearrange("(sub p) f -> p sub f", p=128),
                         lds[par], waits=[])
            em.wait("t", lds[par], v_v)
            em.wait("v", lds[par], v_v)
            for h in range(H):
                hp = h % 2
                sigk = sigk_b[hp]
                sigq = sigq_b[hp]
                kb0 = h * 1024 + par * 512
                kk = kseg[:, kb0:kb0 + 512]
                qq = qseg[:, kb0:kb0 + 512]
                mb = mem_bf[:, h * 129:(h + 1) * 129]
                ma = mem_aug[:, h * 129:(h + 1) * 129]
                em.op("v", lambda e, _d=mb, _s=ma: e.tensor_copy(_d, _s),
                      waits=[(tq, sig_use[hp])])
                sigma(sigk[:, :], kk, hp)
                sigma(sigq[:, :], qq, hp)
                sig_done = vv.v
                em.op("t", lambda e, _z=mem_bf[:, h * 129 + 128:h * 129 + 129], _q=sigq[:, :]:
                      e.matmul(zq_ps, _z, _q, start=True, stop=True),
                      waits=[(vv, sig_done)], inc=tq)
                em.op("v", lambda e: e.tensor_copy(rowr[:, 0:512], zq_ps), waits=[(tq, tq.v)])
                em.op("v", lambda e: e.reciprocal(rowr[:, 512:1024], rowr[:, 0:512]),
                      waits=[(vv, vv.v)])
                em.op("v", lambda e: e.tensor_copy(rowb[:, :], rowr[:, 512:1024]))
                em.op("t", lambda e: e.matmul(bc_ps, o1[:, :], rowb[:, :], start=True, stop=True),
                      waits=[(vv, vv.v)], inc=tq)
                em.op("v", lambda e, _q=sigq[:, :]: e.tensor_tensor(_q, _q, bc_ps, ALU.mult),
                      waits=[(tq, tq.v)])
                sq_done = vv.v
                em.op("t", lambda e, _m=mem_bf[:, h * 129:h * 129 + 128], _q=sigq[:, :]:
                      e.matmul(attm_ps, _m, _q, start=True, stop=True),
                      waits=[(vv, sq_done)], inc=tq)
                attm_done = tq.v
                for sub in range(4):
                    dsp = ds_ps[sub % 2]
                    em.op("t", lambda e, _l=sigk[:, sub * 128:(sub + 1) * 128], _m=mb, _d=dsp:
                          e.matmul(_d, _l, _m, start=True, stop=True),
                          waits=[(vv, vv.v)], inc=tq)
                    em.op("v", lambda e, _d=dsp, _o=stat_t[:, hp * 8 + 0:hp * 8 + 1]:
                          e.reciprocal(_o, _d[:, 128:129]),
                          waits=[(tq, tq.v)])
                    em.op("v", lambda e, _o=stat_t[:, hp * 8 + 1:hp * 8 + 2], _i=stat_t[:, hp * 8 + 0:hp * 8 + 1]:
                          e.tensor_scalar_mul(_o, _i, -1.0),
                          waits=[(vv, vv.v)])
                    em.op("v", lambda e, _d=delt[:, sub * 128:(sub + 1) * 128], _p=dsp,
                          _sc=stat_t[:, hp * 8 + 1:hp * 8 + 2],
                          _v=vseg[:, (par * 4 + sub) * D + h * 128:(par * 4 + sub) * D + (h + 1) * 128]:
                          e.scalar_tensor_tensor(_d, _p[:, 0:128], _sc, _v,
                                                 ALU.mult, ALU.add),
                          waits=[(vv, vv.v), (tq, tq.v)])
                delta_done = vv.v
                sgi = (s_ * H + h) % 2
                for sub in range(4):
                    nkb = sub + 1
                    em.op("t", lambda e, _q=qseg[:, kb0 + sub * 128:kb0 + (sub + 1) * 128],
                          _k=kseg[:, kb0:kb0 + nkb * 128], _d=pool[:, 0:nkb * 128]:
                          e.matmul(_d, _q, _k, start=True, stop=True),
                          waits=[(ss, ss.v), (vv, vv.v)], inc=tq)
                    em.op("v", lambda e, _d=pool[:, (nkb - 1) * 128:nkb * 128]:
                          e.tensor_tensor(_d, _d, tri_sb[:, :], ALU.add), waits=[(tq, tq.v)])
                    em.op("v", lambda e, _s=pool[:, 0:nkb * 128], _o=stat_t[:, hp * 8 + 2:hp * 8 + 3]:
                          e.reduce_max(_o, _s, axis=AX.X), waits=[(vv, vv.v)])
                    em.op("v", lambda e, _o=stat_t[:, hp * 8 + 3:hp * 8 + 4], _i=stat_t[:, hp * 8 + 2:hp * 8 + 3]:
                          e.tensor_scalar_mul(_o, _i, -SCALE0),
                          waits=[(vv, vv.v)])
                    em.op("s", lambda e, _d=p_sb[:, 0:nkb * 128], _s=pool[:, 0:nkb * 128],
                          _b=stat_t[:, hp * 8 + 3:hp * 8 + 4], _a=stat_t[:, hp * 8 + 4:hp * 8 + 5]:
                          e.activation(_d, _s, AF.Exp, bias=_b, scale=SCALE0,
                                       accum_out=_a),
                          waits=[(vv, vv.v), (tq, tq.v)])
                    em.op("v", lambda e, _o=stat_t[:, hp * 8 + 5:hp * 8 + 6], _i=stat_t[:, hp * 8 + 4:hp * 8 + 5]:
                          e.reciprocal(_o, _i),
                          waits=[(ss, ss.v)])
                    em.op("s", lambda e, _p=p_sb[:, 0:nkb * 128], _m=stat_t[:, hp * 8 + 5:hp * 8 + 6]:
                          e.mul(_p, _p, _m),
                          waits=[(vv, vv.v)])
                    pn_s = ss.v
                    pn = vv.v
                    for kb in range(nkb):
                        em.op("t", lambda e, _d=ptp_bf[:, 1024 + kb * 128:1024 + (kb + 1) * 128],
                              _s=p_sb[:, kb * 128:(kb + 1) * 128]:
                              e.transpose(_d, _s, idn[:, :]),
                              waits=[(ss, pn_s), (vv, pn)] if kb == 0 else (),
                              inc=tq if kb == nkb - 1 else None)
                    em.op("v", lambda e, _d=pt2[:, 0:nkb * 128], _s=ptp_bf[:, 1024:1024 + nkb * 128]:
                          e.tensor_copy(_d, _s), waits=[(tq, tq.v)])
                    ptd = vv.v
                    for kb in range(nkb):
                        em.op("t", lambda e, _v=vseg[:, (par * 4 + kb) * D + h * 128:(par * 4 + kb) * D + (h + 1) * 128],
                              _p=pt2[:, kb * 128:(kb + 1) * 128],
                              _s=(kb == 0), _e=(kb == nkb - 1):
                              e.matmul(av_ps, _v, _p, start=_s, stop=_e),
                              waits=[(vv, ptd)] if kb == 0 else (),
                              inc=tq if kb == nkb - 1 else None)
                    em.op("s", lambda e, _d=adot[:, sub * 128:(sub + 1) * 128],
                          _o=omb_sb[:, h:h + 1]:
                          e.mul(_d, av_ps, _o),
                          waits=[(tq, tq.v)])
                    em.op("v", lambda e, _d=astg[sgi][:, sub * 128:(sub + 1) * 128],
                          _a=pool[:, 1536 + sub * 128:1536 + (sub + 1) * 128],
                          _o=adot[:, sub * 128:(sub + 1) * 128],
                          _bt=beta_sb[:, h:h + 1]:
                          e.scalar_tensor_tensor(_d, _a, _bt, _o,
                                                 ALU.mult, ALU.add),
                          waits=[(tq, attm_done), (stsem[sgi], stg_use[sgi]), (ss, ss.v)])
                stg_use[sgi] = em.dma("y", att_d.ap()[h * 128:(h + 1) * 128, tsl],
                                      astg[sgi][:, :], stsem[sgi], waits=[(vv, vv.v)])
                for sub in range(4):
                    em.op("t", lambda e, _d=skT_bf[:, 6656 + sub * 128:6656 + (sub + 1) * 128],
                          _s=sigk[:, sub * 128:(sub + 1) * 128]:
                          e.transpose(_d, _s, idn[:, :]),
                          waits=[(vv, delta_done)] if sub == 0 else (),
                          inc=tq if sub == 3 else None)
                em.op("v", lambda e: e.tensor_copy(sigkT[:, :], skT_bf[:, 6656:6656 + 512]),
                      waits=[(tq, tq.v)])
                em.wait("t", vv, vv.v)
                for sub in range(4):
                    em.op("t", lambda e, _l=sigkT[:, sub * 128:(sub + 1) * 128],
                          _r=delt[:, sub * 128:(sub + 1) * 128],
                          _s=(sub == 0), _e=(sub == 3):
                          e.matmul(md_ps, _l, _r, start=_s, stop=_e),
                          inc=tq if sub == 3 else None)
                em.op("v", lambda e, _m=mem_aug[:, h * 129:h * 129 + 128]:
                      e.tensor_tensor(_m, _m, md_ps, ALU.add), waits=[(tq, tq.v)])
                em.op("v", lambda e, _o=stat_t[:, hp * 8 + 6:hp * 8 + 7], _k=sigk[:, :]:
                      e.reduce_sum(_o, _k, axis=AX.X))
                em.op("v", lambda e, _z=mem_aug[:, h * 129 + 128:h * 129 + 129],
                      _zs=stat_t[:, hp * 8 + 6:hp * 8 + 7]:
                      e.tensor_tensor(_z, _z, _zs, ALU.add), waits=[(vv, vv.v)])
                sig_use[hp] = tq.v
            seg_use[par] = tq.v
            seg_use_v[par] = vv.v
        em.wait("y", stsem[0], stg_use[0])
        em.wait("y", stsem[1], stg_use[1])
        em.run_block()
    return nc



def build_b2():
    nc = bass.Bass("TRN2", target_bir_lowering=False)
    KC = D // 128
    T_ = 1024
    I1 = 7168
    IC = I1 // 128
    R = 64
    SCL = 16.0 / 64.0

    att_in = nc.dram_tensor("att_in", [D, T_], BF16, kind="ExternalInput")
    x2_in = nc.dram_tensor("x2_in", [D, T_], F32, kind="ExternalInput")
    wo = nc.dram_tensor("wo", [D, D], F32, kind="ExternalInput")
    conf_t = nc.dram_tensor("conf_t", [D, 8], F32, kind="ExternalInput")
    conf_b = nc.dram_tensor("conf_b", [1, 8], F32, kind="ExternalInput")
    wealth = nc.dram_tensor("wealth", [1, 8], F32, kind="ExternalInput")
    bgate = nc.dram_tensor("bgate", [D, I1], F32, kind="ExternalInput")
    bup = nc.dram_tensor("bup", [D, I1], F32, kind="ExternalInput")
    bdown = nc.dram_tensor("bdown", [I1, D], F32, kind="ExternalInput")
    gA = nc.dram_tensor("gA", [8, D, R], F32, kind="ExternalInput")
    gB = nc.dram_tensor("gB", [8, R, I1], F32, kind="ExternalInput")
    uA = nc.dram_tensor("uA", [8, D, R], F32, kind="ExternalInput")
    uB = nc.dram_tensor("uB", [8, R, I1], F32, kind="ExternalInput")
    dA = nc.dram_tensor("dA", [8, I1, R], F32, kind="ExternalInput")
    dB = nc.dram_tensor("dB", [8, R, D], F32, kind="ExternalInput")
    ones1 = nc.dram_tensor("ones1", [1, 128], BF16, kind="ExternalInput")
    ones128 = nc.dram_tensor("ones128", [128, 1], BF16, kind="ExternalInput")
    sel = nc.dram_tensor("sel", [8, 8, 128], BF16, kind="ExternalInput")
    ident32 = nc.dram_tensor("ident32", [128, 128], F32, kind="ExternalInput")
    identb = nc.dram_tensor("identb", [128, 128], BF16, kind="ExternalInput")

    out_d = nc.dram_tensor("out", [D, T_], F32, kind="ExternalOutput")
    x3_dram = nc.dram_tensor("x3_sp", [D, T_], F32, kind="Internal")
    hb_dram = nc.dram_tensor("hb_sp", [IC, 128, T_], BF16, kind="Internal")

    ctx = ExitStack()
    with ctx:
        em = Em(nc, ctx)
        AR = sb(ctx, nc, "AR", [128, 94 * 1024])
        ARf = AR.bitcast(F32)
        xbf = AR[:, 0:16 * 1024]
        stream = ARf[:, 8192:8192 + 16 * 1024]
        z_g = AR[0:64, 16384:16384 + 8192]
        z_u = AR[0:64, 24576:24576 + 8192]
        base_g = AR[:, 32768:32768 + 1024]
        base_u = AR[:, 33792:33792 + 1024]
        gb_b = [AR[0:64, 34816 + i * 1024:34816 + (i + 1) * 1024] for i in range(2)]
        ub_b = [AR[0:64, 36864 + i * 1024:36864 + (i + 1) * 1024] for i in range(2)]
        dA_b = [AR[:, 38912 + i * 512:38912 + (i + 1) * 512] for i in range(2)]
        suw = AR[:, 39936:39936 + 1024]
        hbar = ARf[:, 20480:20480 + 1024]
        s_ch = AR[:, 43008:43008 + 1024]
        g_ch = AR[:, 44032:44032 + 1024]
        u_ch = AR[:, 45056:45056 + 1024]
        att_sb = AR[:, 49152:49152 + 8192]
        y_sb = AR[0:64, 49152:49152 + 8192]
        wd = AR[:, 57344:57344 + 14336]
        wb = [AR[:, 71680 + i * 2048:71680 + (i + 1) * 2048] for i in range(4)]
        stg = [AR[:, 79872 + i * 1024:79872 + (i + 1) * 1024] for i in range(2)]
        lnrow = ARf[:, 40960:40960 + 4096][0:1, :]
        lnrow_bf = AR[:, 90112:90112 + 2048][0:1, :]
        cbc8 = AR[:, 81920:81920 + 8192]
        comb_fm = AR[0:8, 92160:92160 + 1024]
        hb_st = [AR[:, 93184 + i * 1024:93184 + (i + 1) * 1024] for i in range(2)]
        rwt = sb(ctx, nc, "rwt", [128, KC * 8])
        comb = sb(ctx, nc, "comb", [128, 8], F32)
        rstat = sb(ctx, nc, "rstat", [128, 12], F32)
        rows8 = sb(ctx, nc, "rows8", [1, 16])
        rows8f = sb(ctx, nc, "rows8f", [1, 16], F32)
        o1 = sb(ctx, nc, "o1", [1, 128])
        o128 = sb(ctx, nc, "o128", [128, 1])
        sel_sb = sb(ctx, nc, "sel_sb", [8, 8 * 128])
        id32 = sb(ctx, nc, "id32", [128, 128], F32)
        idnb = sb(ctx, nc, "idnb", [128, 128])
        sq_ch = [sb(ctx, nc, f"sq{i}", [128, T_]) for i in range(2)]
        sq2 = [sb(ctx, nc, f"sq2_{i}", [128, T_]) for i in range(2)]
        x3c = [sb(ctx, nc, f"x3c{i}", [128, T_], F32) for i in range(2)]
        pool = ps(ctx, nc, "pool", [128, 4096])
        A = pool[:, 0:1024]
        Bk = pool[:, 1024:2048]
        C = pool[:, 2048:3072]
        Dk = pool[:, 3072:4096]

        ldc = em.sem("ldc")
        ldg = em.sem("ldg")
        lda = em.sem("lda")
        ldh = [em.sem("ldh0"), em.sem("ldh1")]
        wsem = [em.sem(f"ws{i}") for i in range(4)]
        gsem = [em.sem(f"gs{i}") for i in range(2)]
        usem = [em.sem(f"us{i}") for i in range(2)]
        asem = [em.sem(f"as{i}") for i in range(2)]
        dsem = em.sem("ds")
        x3sem = [em.sem(f"x3s{i}") for i in range(2)]
        spl = em.sem("spl")
        st = em.sem("st")
        stsem = [em.sem("stg0"), em.sem("stg1")]
        tq = em.sem("tq")
        vv = em.sem("vv")
        ss = em.sem("ss")
        em.set_autochain("v", vv)
        em.set_autochain("s", ss)

        em.dma("y", stream.rearrange("p (kc t) -> p kc t", kc=KC),
               x2_in.ap().rearrange("(kc p) t -> p kc t", p=128), ldc)
        em.dma("y", rows8f[:, 0:8], conf_b[:, :], ldc)
        em.dma("y", rows8f[:, 8:16], wealth[:, :], ldc)
        em.dma("y", o1[:, :], ones1[:, :], ldc)
        em.dma("y", o128[:, :], ones128[:, :], ldc)
        em.dma("y", sel_sb[:, :].rearrange("p (e m) -> p e m", e=8),
               sel.ap().transpose([1, 0, 2]), ldc)
        em.dma("y", id32[:, :], ident32[:, :], ldc)
        v_ldc = em.dma("y", idnb[:, :], identb[:, :], ldc)
        v_rwt = em.dma("g", rwt[:, :].rearrange("p (kc m) -> p kc m", kc=KC),
                       conf_t.ap().rearrange("(kc p) m -> p kc m", p=128), ldg)
        em.wait("t", ldc, v_ldc)
        em.wait("v", ldc, v_ldc)
        em.wait("s", ldc, v_ldc)

        wuse = [0] * 4

        def wload(i, dram_slice, sem=None):
            sem = wsem[i] if sem is None else sem
            kcs = dram_slice.shape[0] // 128
            m_ = dram_slice.shape[1]
            return em.dma("g", wb[i][:, 0:kcs * m_].rearrange("p (kc m) -> p kc m", kc=kcs),
                          dram_slice.rearrange("(kc p) m -> p kc m", p=128),
                          sem, waits=[(tq, wuse[i])])

        strv = stream.rearrange("p (kc t) -> p kc t", kc=KC)
        xbv = xbf.rearrange("p (kc t) -> p kc t", kc=KC)
        attv = att_sb.rearrange("p (kc t) -> p kc t", kc=8)

        def wbv(i, kcs=16):
            return wb[i].rearrange("p (kc m) -> p kc m", kc=16)[:, 0:kcs, :]

        for kh in range(2):
            v_att = em.dma("g", attv[:, :, :],
                           att_in.ap().rearrange("(kc p) t -> p kc t", p=128)[:, kh * 8:(kh + 1) * 8, :],
                           lda, waits=[(vv, vv.v)])
            em.wait("t", lda, v_att)
            for oc in range(KC):
                wi = oc % 2
                v_w = wload(wi, wo.ap()[kh * 1024:(kh + 1) * 1024, oc * 128:(oc + 1) * 128])
                dst = A if oc % 2 == 0 else Bk
                em.wait("t", wsem[wi], v_w)
                em.wait("t", vv, vv.v)
                for hf in range(2):
                    for kc in range(8):
                        em.op("t", lambda e, _d=pool[:, (oc % 2) * 1024 + hf * 512:(oc % 2) * 1024 + (hf + 1) * 512],
                              _w=wbv(wi, 8)[:, kc, :],
                              _x=attv[:, kc, hf * 512:(hf + 1) * 512],
                              _s=(kc == 0), _p=(kc == 7):
                              e.matmul(_d, _w, _x, start=_s, stop=_p),
                              inc=tq if (kc == 7 and hf == 1) else None)
                wuse[wi] = tq.v
                em.op("v", lambda e, _d=strv[:, oc, :], _s=dst:
                      e.tensor_tensor(_d, _d, _s, ALU.add), waits=[(tq, tq.v)], inc=vv)

        def layernorm():
            em.wait("s", vv, vv.v)
            squse = [0, 0]
            for kc in range(KC):
                b2_ = kc % 2
                em.op("v", lambda e, _d=sq_ch[b2_][:, :], _s=strv[:, kc, :]:
                      e.tensor_copy(_d, _s), waits=[(tq, squse[b2_])])
                em.op("s", lambda e, _d=sq2[b2_][:, :], _s=strv[:, kc, :]:
                      e.activation(_d, _s, AF.Square, bias=0.0, scale=1.0),
                      waits=[(tq, squse[b2_])])
                for hf in range(2):
                    em.op("t", lambda e, _d=pool[0:1, 2048 + hf * 512:2048 + (hf + 1) * 512],
                          _o=o128[:, :], _x=sq_ch[b2_][:, hf * 512:(hf + 1) * 512],
                          _s=(kc == 0), _p=(kc == KC - 1):
                          e.matmul(_d, _o, _x, start=_s, stop=_p),
                          waits=[(vv, vv.v)] if hf == 0 else (),
                          inc=tq if hf == 1 else None)
                for hf in range(2):
                    em.op("t", lambda e, _d=pool[0:1, 3072 + hf * 512:3072 + (hf + 1) * 512],
                          _o=o128[:, :], _x=sq2[b2_][:, hf * 512:(hf + 1) * 512],
                          _s=(kc == 0), _p=(kc == KC - 1):
                          e.matmul(_d, _o, _x, start=_s, stop=_p),
                          waits=[(ss, ss.v)] if hf == 0 else (),
                          inc=tq if hf == 1 else None)
                squse[b2_] = tq.v
            Ar, Br = lnrow[:, 0:T_], lnrow[:, T_:2 * T_]
            Cr, Dr = lnrow[:, 2 * T_:3 * T_], lnrow[:, 3 * T_:4 * T_]
            em.op("v", lambda e: e.tensor_copy(Ar, pool[0:1, 2048:3072]), waits=[(tq, tq.v)])
            em.op("v", lambda e: e.tensor_copy(Br, pool[0:1, 3072:4096]))
            em.op("v", lambda e: e.tensor_scalar_mul(Cr, Ar, 1.0 / D))
            em.op("v", lambda e: e.tensor_scalar_mul(Ar, Br, 1.0 / D))
            em.op("v", lambda e: e.tensor_mul(Br, Cr, Cr))
            em.op("v", lambda e: e.tensor_sub(Ar, Ar, Br))
            em.op("v", lambda e: e.tensor_scalar_add(Ar, Ar, EPS))
            em.op("s", lambda e: e.activation(Br, Ar, AF.Ln, bias=0.0, scale=1.0),
                  waits=[(vv, vv.v)])
            em.op("s", lambda e: e.activation(Dr, Br, AF.Exp, bias=0.0, scale=-0.5))
            em.op("v", lambda e: e.tensor_mul(Ar, Cr, Dr), waits=[(ss, ss.v)])
            em.op("v", lambda e: e.tensor_copy(lnrow_bf[:, 0:T_], Dr))
            em.op("v", lambda e: e.tensor_copy(lnrow_bf[:, T_:2 * T_], Ar))
            for hf in range(2):
                em.op("t", lambda e, _d=C[:, hf * 512:(hf + 1) * 512], _r=lnrow_bf[:, hf * 512:(hf + 1) * 512]:
                      e.matmul(_d, o1[:, :], _r, start=True, stop=True),
                      waits=[(vv, vv.v)] if hf == 0 else (), inc=tq if hf == 1 else None)
                em.op("t", lambda e, _d=Dk[:, hf * 512:(hf + 1) * 512], _r=lnrow_bf[:, T_ + hf * 512:T_ + (hf + 1) * 512]:
                      e.matmul(_d, o1[:, :], _r, start=True, stop=True),
                      inc=tq if hf == 1 else None)
            for kc in range(KC):
                em.op("v", lambda e, _x=strv[:, kc, :]:
                      e.tensor_mul(_x, _x, C), waits=[(tq, tq.v)] if kc == 0 else ())
                em.op("v", lambda e, _x=strv[:, kc, :]:
                      e.tensor_sub(_x, _x, Dk))
                em.op("v", lambda e, _d=xbv[:, kc, :], _s=strv[:, kc, :]:
                      e.tensor_copy(_d, _s))
            em.wait("t", vv, vv.v)

        layernorm()

        v_spl = 0
        for kc in range(KC):
            v_spl = em.dma("y", x3_dram.ap().rearrange("(kc p) t -> p kc t", p=128)[:, kc, :],
                           strv[:, kc, :], spl, waits=[(vv, vv.v)] if kc == 0 else ())

        em.op("v", lambda e: e.tensor_copy(rows8[:, 0:8], rows8f[:, 0:8]))
        em.op("v", lambda e: e.tensor_copy(rows8[:, 8:16], rows8f[:, 8:16]))
        for tt in range(8):
            em.wait("t", ldg, v_rwt)
            for kc in range(KC):
                em.op("t", lambda e, _d=C[:, 0:8],
                      _x=xbv[:, kc, tt * 128:(tt + 1) * 128],
                      _w=rwt[:, :].rearrange("p (kc m) -> p kc m", kc=KC)[:, kc, :],
                      _s=(kc == 0), _p=(kc == KC - 1):
                      e.matmul(_d, _x, _w, start=_s, stop=_p),
                      inc=tq if kc == KC - 1 else None)
            em.op("t", lambda e: e.matmul(Dk[:, 0:8], o1[:, :], rows8[:, 0:8], start=True, stop=True),
                  waits=[(vv, vv.v)], inc=tq)
            em.op("t", lambda e: e.matmul(Dk[:, 512:520], o1[:, :], rows8[:, 8:16], start=True, stop=True),
                  inc=tq)
            em.op("v", lambda e: e.tensor_copy(comb[:, 0:8], C[:, 0:8]), waits=[(tq, tq.v)])
            em.op("v", lambda e: e.tensor_tensor(comb[:, 0:8], comb[:, 0:8], Dk[:, 0:8], ALU.add))
            em.op("s", lambda e: e.activation(comb[:, 0:8], comb[:, 0:8], AF.Sigmoid, bias=0.0, scale=1.0),
                  waits=[(vv, vv.v)])
            em.op("v", lambda e: e.tensor_tensor(comb[:, 0:8], comb[:, 0:8], Dk[:, 512:520], ALU.mult),
                  waits=[(ss, ss.v)])
            em.op("v", lambda e: e.reduce_max(rstat[:, 2:3], comb[:, 0:8], axis=AX.X))
            eqs = x3c[0]
            em.op("v", lambda e: e.tensor_single_scalar(eqs[:, 0:8], comb[:, 0:8], rstat[:, 2:3], ALU.is_equal))
            em.op("v", lambda e: e.scalar_tensor_tensor(eqs[:, 8:16], eqs[:, 0:8], -1.0e30,
                                                        comb[:, 0:8], ALU.mult, ALU.add))
            em.op("v", lambda e: e.reduce_max(rstat[:, 3:4], eqs[:, 8:16], axis=AX.X))
            em.op("v", lambda e: e.tensor_single_scalar(eqs[:, 8:16], comb[:, 0:8], rstat[:, 3:4], ALU.is_equal))
            em.op("v", lambda e: e.tensor_sub(rstat[:, 4:5], rstat[:, 3:4], rstat[:, 2:3]))
            em.op("s", lambda e: e.activation(rstat[:, 5:6], rstat[:, 4:5], AF.Exp, bias=0.0, scale=1.0),
                  waits=[(vv, vv.v)])
            em.op("v", lambda e: e.tensor_scalar_add(rstat[:, 6:7], rstat[:, 5:6], 1.0),
                  waits=[(ss, ss.v)])
            em.op("v", lambda e: e.reciprocal(rstat[:, 7:8], rstat[:, 6:7]))
            em.op("v", lambda e: e.tensor_mul(rstat[:, 8:9], rstat[:, 5:6], rstat[:, 7:8]))
            em.op("v", lambda e: e.tensor_single_scalar(eqs[:, 0:8], eqs[:, 0:8], rstat[:, 7:8], ALU.mult))
            em.op("v", lambda e: e.scalar_tensor_tensor(comb[:, 0:8], eqs[:, 8:16], rstat[:, 8:9],
                                                        eqs[:, 0:8], ALU.mult, ALU.add))
            em.op("t", lambda e, _d=Dk[0:8, 0:128], _s=comb[:, 0:8]:
                  e.transpose(_d, _s, id32[:, :]), waits=[(vv, vv.v)], inc=tq)
            em.op("v", lambda e, _d=comb_fm[:, tt * 128:(tt + 1) * 128], _s=Dk[0:8, 0:128]:
                  e.tensor_copy(_d, _s), waits=[(tq, tq.v)])
        comb_done = vv.v
        for e_ in range(8):
            for hf in range(2):
                em.op("t", lambda e, _d=C[:, hf * 512:(hf + 1) * 512],
                      _s=sel_sb[:, :].rearrange("p (e m) -> p e m", e=8)[:, e_, :],
                      _r=comb_fm[:, hf * 512:(hf + 1) * 512]:
                      e.matmul(_d, _s, _r, start=True, stop=True),
                      waits=[(vv, vv.v)] if hf == 0 else (), inc=tq if hf == 1 else None)
            em.op("v", lambda e, _d=cbc8[:, e_ * 1024:(e_ + 1) * 1024]:
                  e.tensor_copy(_d, C), waits=[(tq, tq.v)])

        em.wait("t", spl, v_spl)
        em.wait("v", spl, v_spl)
        em.wait("g", spl, v_spl)
        for e_ in range(8):
            for which, zdst, wAd in (("g", z_g, gA), ("u", z_u, uA)):
                wi = e_ % 2
                v_w = wload(wi, wAd.ap()[e_, :, :])
                em.wait("t", wsem[wi], v_w)
                em.wait("t", vv, vv.v)
                for hf in range(2):
                    for kc in range(KC):
                        em.op("t", lambda e, _d=C[0:64, hf * 512:(hf + 1) * 512],
                              _w=wb[wi][:, 0:16 * 64].rearrange("p (kc m) -> p kc m", kc=16)[:, kc, :],
                              _x=xbv[:, kc, hf * 512:(hf + 1) * 512],
                              _s=(kc == 0), _p=(kc == KC - 1):
                              e.matmul(_d, _w, _x, start=_s, stop=_p),
                              inc=tq if (kc == KC - 1 and hf == 1) else None)
                wuse[wi] = tq.v
                em.op("v", lambda e, _d=zdst[:, e_ * 1024:(e_ + 1) * 1024], _s=C[0:64, :]:
                      e.tensor_copy(_d, _s), waits=[(tq, tq.v)])
        em.op("v", lambda e: e.tensor_scalar_mul(z_g[:, :], z_g[:, :], SCL))
        em.op("v", lambda e: e.tensor_scalar_mul(z_u[:, :], z_u[:, :], SCL))
        z_done = vv.v

        em.op("v", lambda e: e.memset(y_sb[:, :], 0.0), waits=[(lda, 32), (tq, tq.v)])
        hb_dma = None
        for i in range(IC):
            par = i % 2
            v_bg = wload(par, bgate.ap()[:, i * 128:(i + 1) * 128])
            v_bu = wload(2 + par, bup.ap()[:, i * 128:(i + 1) * 128], sem=wsem[2 + par])
            v_gb = em.dma("g", gb_b[par].rearrange("q (e m) -> q e m", e=8),
                          gB.ap()[:, :, i * 128:(i + 1) * 128].transpose([1, 0, 2]),
                          gsem[par], waits=[(tq, wuse[par])])
            v_ub = em.dma("g", ub_b[par].rearrange("q (e m) -> q e m", e=8),
                          uB.ap()[:, :, i * 128:(i + 1) * 128].transpose([1, 0, 2]),
                          usem[par], waits=[(tq, wuse[par])])
            v_da = em.dma("g", dA_b[par].rearrange("p (e m) -> p e m", e=8),
                          dA.ap()[:, i * 128:(i + 1) * 128, :].transpose([1, 0, 2]),
                          asem[par], waits=[(tq, wuse[par])])
            em.wait("t", wsem[par], v_bg)
            em.wait("t", wsem[2 + par], v_bu)
            em.wait("t", gsem[par], v_gb)
            em.wait("t", usem[par], v_ub)
            em.wait("t", asem[par], v_da)
            em.wait("t", vv, vv.v)
            for hf in range(2):
                for kc in range(KC):
                    em.op("t", lambda e, _d=A[:, hf * 512:(hf + 1) * 512],
                          _w=wbv(par)[:, kc, :], _x=xbv[:, kc, hf * 512:(hf + 1) * 512],
                          _s=(kc == 0), _p=(kc == KC - 1):
                          e.matmul(_d, _w, _x, start=_s, stop=_p),
                          inc=tq if kc == KC - 1 else None)
            wuse[par] = tq.v
            bg_done = tq.v
            for hf in range(2):
                for kc in range(KC):
                    em.op("t", lambda e, _d=Bk[:, hf * 512:(hf + 1) * 512],
                          _w=wbv(2 + par)[:, kc, :], _x=xbv[:, kc, hf * 512:(hf + 1) * 512],
                          _s=(kc == 0), _p=(kc == KC - 1):
                          e.matmul(_d, _w, _x, start=_s, stop=_p),
                          inc=tq if kc == KC - 1 else None)
            wuse[2 + par] = tq.v
            em.op("v", lambda e: e.tensor_copy(base_g, A), waits=[(tq, bg_done)])
            em.op("v", lambda e: e.tensor_copy(base_u, Bk), waits=[(tq, tq.v)])
            base_done = vv.v
            silu_prev = 0
            su_prev = base_done
            for e_ in range(8):
                em.wait("t", ss, silu_prev)
                em.wait("t", vv, su_prev)
                for hf in range(2):
                    em.op("t", lambda e, _d=C[:, hf * 512:(hf + 1) * 512],
                          _w=gb_b[par].rearrange("q (e m) -> q e m", e=8)[:, e_, :],
                          _z=z_g[:, e_ * 1024 + hf * 512:e_ * 1024 + (hf + 1) * 512]:
                          e.matmul(_d, _w, _z, start=True, stop=False))
                    em.op("t", lambda e, _d=C[:, hf * 512:(hf + 1) * 512],
                          _r=base_g[:, hf * 512:(hf + 1) * 512]:
                          e.matmul(_d, idnb[:, :], _r, start=False, stop=True),
                          inc=tq if hf == 1 else None)
                g_done_ = tq.v
                for hf in range(2):
                    em.op("t", lambda e, _d=Dk[:, hf * 512:(hf + 1) * 512],
                          _w=ub_b[par].rearrange("q (e m) -> q e m", e=8)[:, e_, :],
                          _z=z_u[:, e_ * 1024 + hf * 512:e_ * 1024 + (hf + 1) * 512]:
                          e.matmul(_d, _w, _z, start=True, stop=False))
                    em.op("t", lambda e, _d=Dk[:, hf * 512:(hf + 1) * 512],
                          _r=base_u[:, hf * 512:(hf + 1) * 512]:
                          e.matmul(_d, idnb[:, :], _r, start=False, stop=True),
                          inc=tq if hf == 1 else None)
                u_done_ = tq.v
                em.op("s", lambda e: e.activation(s_ch[:, :], C, AF.Silu, bias=0.0, scale=1.0),
                      waits=[(tq, g_done_)])
                silu_prev = ss.v
                em.op("v", lambda e: e.tensor_tensor(u_ch[:, :], s_ch[:, :], Dk, ALU.mult),
                      waits=[(ss, ss.v), (tq, u_done_)])
                su_prev = vv.v
                em.op("v", lambda e, _c=cbc8[:, e_ * 1024:(e_ + 1) * 1024]:
                      e.tensor_tensor(suw[:, :], u_ch[:, :], _c, ALU.mult))
                suw_done = vv.v
                for hf in range(2):
                    em.op("t", lambda e, _d=A[0:64, hf * 512:(hf + 1) * 512],
                          _w=dA_b[par].rearrange("p (e m) -> p e m", e=8)[:, e_, :],
                          _r=suw[:, hf * 512:(hf + 1) * 512]:
                          e.matmul(_d, _w, _r, start=True, stop=True),
                          waits=[(vv, suw_done)] if hf == 0 else (),
                          inc=tq if hf == 1 else None)
                y_done_ = tq.v
                if e_ == 0:
                    em.op("v", lambda e: e.tensor_copy(hbar, suw[:, :]),
                          waits=[hb_dma] if hb_dma else ())
                else:
                    em.op("v", lambda e: e.tensor_tensor(hbar, hbar, suw[:, :], ALU.add))
                em.op("v", lambda e, _d=y_sb[:, e_ * 1024:(e_ + 1) * 1024]:
                      e.tensor_tensor(_d, _d, A[0:64, :], ALU.add), waits=[(tq, y_done_)])
            hb_dma = (ldh[par], em.dma("g", hb_dram.ap()[i, :, :], hbar, ldh[par],
                                       waits=[(vv, vv.v)]))
        em.op("v", lambda e: e.tensor_scalar_mul(y_sb[:, :], y_sb[:, :], SCL),
              waits=[(tq, tq.v)])
        y_scaled = vv.v
        em.wait("g", ldh[0], ldh[0].v)
        em.wait("g", ldh[1], ldh[1].v)

        hb_use = [0, 0]
        wd_use = 0
        db_use = 0
        x3use = [vv.v, vv.v]
        ldh_fin = [ldh[0].v, ldh[1].v]
        em.wait("v", ldh[0], ldh_fin[0])
        em.wait("v", ldh[1], ldh_fin[1])
        for ocp in range(8):
            v_wd = em.dma("g", wd.rearrange("p (ic m) -> p ic m", ic=IC),
                          bdown.ap()[:, ocp * 256:(ocp + 1) * 256].rearrange("(ic p) m -> p ic m", p=128),
                          dsem, waits=[(tq, wd_use)])
            v_db = em.dma("g", wb[3].rearrange("p (e m) -> p e m", e=8)[0:64, :, 0:256],
                          dB.ap()[:, :, ocp * 256:(ocp + 1) * 256].transpose([1, 0, 2]),
                          wsem[3], waits=[(tq, db_use)])
            em.wait("t", dsem, v_wd)
            em.wait("t", wsem[3], v_db)
            em.wait("t", vv, vv.v)
            for i in range(IC):
                par = i % 2
                v_hb = em.dma("g", hb_st[par], hb_dram.ap()[i, :, :], ldh[par],
                              waits=[(tq, hb_use[par])])
                em.wait("t", ldh[par], v_hb)
                for sub in range(4):
                    oc2, hf = sub // 2, sub % 2
                    em.op("t", lambda e, _d=pool[:, 2048 + sub * 512:2048 + (sub + 1) * 512],
                          _w=wd.rearrange("p (ic m) -> p ic m", ic=IC)[:, i, oc2 * 128:(oc2 + 1) * 128],
                          _h=hb_st[par][:, hf * 512:(hf + 1) * 512],
                          _s=(i == 0):
                          e.matmul(_d, _w, _h, start=_s, stop=False),
                          inc=tq if sub == 3 else None)
                hb_use[par] = tq.v
            for sub in range(4):
                oc2, hf = sub // 2, sub % 2
                for e_ in range(8):
                    em.op("t", lambda e, _d=pool[:, 2048 + sub * 512:2048 + (sub + 1) * 512],
                          _w=wb[3].rearrange("p (e m) -> p e m", e=8)[0:64, e_, oc2 * 128:(oc2 + 1) * 128],
                          _y=y_sb[:, e_ * 1024 + hf * 512:e_ * 1024 + (hf + 1) * 512],
                          _p=(e_ == 7):
                          e.matmul(_d, _w, _y, start=False, stop=_p),
                          inc=tq if e_ == 7 else None)
            wd_use = tq.v
            db_use = tq.v
            mm_done = tq.v
            for oc2 in range(2):
                oc = ocp * 2 + oc2
                x3b = x3c[oc % 2]
                v_x3 = em.dma("y", x3b[:, :],
                              x3_dram.ap().rearrange("(kc p) t -> p kc t", p=128)[:, oc, :],
                              x3sem[oc % 2], waits=[(vv, x3use[oc % 2]), (spl, v_spl)])
                em.wait("v", x3sem[oc % 2], v_x3)
                for hf in range(2):
                    sub = oc2 * 2 + hf
                    em.op("v", lambda e, _d=strv[:, oc, hf * 512:(hf + 1) * 512],
                          _p=pool[:, 2048 + sub * 512:2048 + (sub + 1) * 512],
                          _x=x3b[:, hf * 512:(hf + 1) * 512]:
                          e.tensor_tensor(_d, _p, _x, ALU.add), waits=[(tq, mm_done)])
                x3use[oc % 2] = vv.v

        layernorm()
        fin = vv.v
        for kc in range(KC):
            em.op("y", lambda e, _d=out_d.ap().rearrange("(kc p) t -> p kc t", p=128)[:, kc, :],
                  _s=strv[:, kc, :]:
                  e.dma_start(out=_d, in_=_s), waits=[(vv, fin)], inc=st, inc_n=16)
        em.wait("y", st, st.v)
        em.run_block()
    return nc
from concourse.bass_utils import run_bass_kernel_spmd

_NCACHE = {}


def _get_nc(name, builder):
    if name not in _NCACHE:
        _NCACHE[name] = builder()
    return _NCACHE[name]


def _own_cols(c):
    a, b_ = stripes_of_core(c)
    return np.r_[a * 512:(a + 1) * 512, b_ * 512:(b_ + 1) * 512]


def kernel(**inputs):
    cores = list(range(8))

    nc1 = _get_nc("a1", build_a1)
    r1 = run_bass_kernel_spmd(nc1, a1_in_maps(inputs), core_ids=cores)
    att_full = np.stack([
        np.concatenate([r1.results[b * 4 + g]["att"].reshape(512, S)
                        for g in range(4)], axis=0)
        for b in range(B)])

    nc2 = _get_nc("a2", build_a2)
    r2 = run_bass_kernel_spmd(nc2, a2_in_maps(inputs, att_full), core_ids=cores)

    qf = np.zeros((B, D, S), BF)
    kf = np.zeros((B, D, S), BF)
    vt = np.zeros((B, S, D), BF)
    x2 = [None] * 8
    for c in range(8):
        b = c // 4
        cols = _own_cols(c)
        qf[b][:, cols] = r2.results[c]["q1_out"]
        kf[b][:, cols] = r2.results[c]["k1_out"]
        vt[b][cols, :] = r2.results[c]["v1_out"].T
        x2[c] = r2.results[c]["x2_out"]

    nc3 = _get_nc("b1", build_b1)
    tri = host_masks()[0][:, 0:128].copy()
    betasT = np.ascontiguousarray(np.asarray(inputs["L1_attn_betas"])[0, :, 0, :].T)
    maps3 = []
    for c in range(8):
        b = c // 4
        maps3.append({"qf": qf[b], "kf": kf[b], "vt": vt[b], "betasT": betasT,
                      "tri": tri, "ident": ident_np(),
                      "ones1b": np.ones((1, 128), BF)})
    r3 = run_bass_kernel_spmd(nc3, maps3, core_ids=cores)
    T = np.arange(S)
    seg_i, t_i = T // 512, T % 512
    a_i, b_i = t_i // 32, t_i % 32
    f = np.arange(D)
    j_i, v_i = f // 128, f % 128
    _rows = a_i[None, :] * 128 + v_i[:, None]
    _cols = seg_i[None, :] * 512 + b_i[None, :] * 16 + j_i[:, None]
    att1 = [np.ascontiguousarray(r3.results[0]["att"][_rows, _cols]),
            np.ascontiguousarray(r3.results[4]["att"][_rows, _cols])]

    nc4 = _get_nc("b2", build_b2)
    common4 = {
        "wo": np.asarray(inputs["L1_attn_Wo"]),
        "conf_t": np.ascontiguousarray(np.asarray(inputs["L1_ffn_conf_W"]).T),
        "conf_b": np.asarray(inputs["L1_ffn_conf_b"]).reshape(1, 8),
        "wealth": np.asarray(inputs["L1_ffn_expert_wealth"]).reshape(1, 8),
        "bgate": np.asarray(inputs["L1_ffn_base_gate"]),
        "bup": np.asarray(inputs["L1_ffn_base_up"]),
        "bdown": np.asarray(inputs["L1_ffn_base_down"]),
        "gA": np.asarray(inputs["L1_ffn_gate_A"]), "gB": np.asarray(inputs["L1_ffn_gate_B"]),
        "uA": np.asarray(inputs["L1_ffn_up_A"]), "uB": np.asarray(inputs["L1_ffn_up_B"]),
        "dA": np.asarray(inputs["L1_ffn_down_A"]), "dB": np.asarray(inputs["L1_ffn_down_B"]),
        "ones1": np.ones((1, 128), BF), "ones128": np.ones((128, 1), BF),
        "sel": np.ascontiguousarray(np.eye(8, dtype=BF)[:, :, None] * np.ones((1, 1, 128), BF)),
        "ident32": np.eye(128, dtype=np.float32),
        "identb": np.eye(128, dtype=BF),
    }
    maps4 = []
    for c in range(8):
        b = c // 4
        cols = _own_cols(c)
        m = dict(common4)
        m["att_in"] = np.ascontiguousarray(att1[b][:, cols])
        m["x2_in"] = x2[c]
        maps4.append(m)
    r4 = run_bass_kernel_spmd(nc4, maps4, core_ids=cores)

    out = np.zeros((B, S, D), np.float32)
    for c in range(8):
        b = c // 4
        cols = _own_cols(c)
        out[b][cols, :] = r4.results[c]["out"].T
    return out


# version 06
# speedup vs baseline: 1.3132x; 1.2117x over previous
import math
import numpy as np
import ml_dtypes
from contextlib import ExitStack

import concourse.bass as bass
import concourse.mybir as mybir

F32 = mybir.dt.float32
BF16 = mybir.dt.bfloat16
AF = mybir.ActivationFunctionType
ALU = mybir.AluOpType
AX = mybir.AxisListType

BF = ml_dtypes.bfloat16

D = 2048
S = 4096
B = 2
SCALE0 = 1.0 / math.sqrt(128.0)
NEG = -3.0e38


class Sem:
    __slots__ = ("h", "v")

    def __init__(self, ctx, nc, name):
        self.h = ctx.enter_context(nc.semaphore(name))
        self.v = 0


class Em:

    ENGS = ("t", "v", "s", "g", "y")

    def __init__(self, nc, ctx):
        self.nc = nc
        self.ctx = ctx
        self.q = {k: [] for k in self.ENGS}
        self._sems = {}

    def sem(self, name) -> Sem:
        if name not in self._sems:
            self._sems[name] = Sem(self.ctx, self.nc, name)
        return self._sems[name]

    def set_autochain(self, eng, sem):
        if not hasattr(self, "_auto"):
            self._auto = {}
        self._auto[eng] = sem

    def op(self, eng, fn, waits=(), inc=None, inc_n=1):
        ac = getattr(self, "_auto", {}).get(eng)
        if ac is not None:
            waits = list(waits) + [(ac, ac.v)]
            if inc is None:
                inc = ac
        ret = None
        if inc is not None:
            inc.v += inc_n
            ret = inc.v
        wl = tuple((s.h, v) for (s, v) in waits if v > 0)

        def run(e, _fn=fn, _wl=wl, _inc=inc, _n=inc_n):
            for h, v in _wl:
                e.wait_ge(h, v)
            inst = _fn(e)
            if _inc is not None:
                inst.then_inc(_inc.h, _n)
        self.q[eng].append(run)
        return ret

    def wait(self, eng, sem, val):
        if val > 0:
            self.q[eng].append(lambda e, _h=sem.h, _v=val: e.wait_ge(_h, _v))

    def dma(self, eng, out_ap, in_ap, sem, waits=()):
        return self.op(eng, lambda e, _o=out_ap, _i=in_ap: e.dma_start(out=_o, in_=_i),
                       waits=waits, inc=sem, inc_n=16)

    def run_block(self):
        block = self.ctx.enter_context(self.nc.Block())

        def mk(lst):
            def body(eng):
                for fn in lst:
                    fn(eng)
            return body
        block.tensor(mk(self.q["t"]))
        block.vector(mk(self.q["v"]))
        block.scalar(mk(self.q["s"]))
        block.gpsimd(mk(self.q["g"]))
        block.sync(mk(self.q["y"]))


def sb(ctx, nc, name, shape, dt=BF16):
    return ctx.enter_context(nc.sbuf_tensor(name, shape, dt))


def ps(ctx, nc, name, shape, dt=F32):
    return ctx.enter_context(nc.psum_tensor(name, shape, dt))



def host_masks():
    qi = np.arange(128)[:, None]
    kj = np.arange(512)[None, :]
    out = np.zeros((4, 128, 512), np.float32)
    for u in range(4):
        out[u] = np.where(kj <= u * 128 + qi, 0.0, NEG)
    return np.ascontiguousarray(out)


def ident_np():
    return np.eye(128, dtype=BF)


def stripes_of_core(c):
    j = c % 4
    return (j, 7 - j)



def build_a1():
    nc = bass.Bass("TRN2", target_bir_lowering=False)
    xb = nc.dram_tensor("xb", [D, S], F32, kind="ExternalInput")
    wq_t = nc.dram_tensor("wq_t", [D, 512], F32, kind="ExternalInput")
    wk_t = nc.dram_tensor("wk_t", [D, 128], F32, kind="ExternalInput")
    wv_t = nc.dram_tensor("wv_t", [D, 128], F32, kind="ExternalInput")
    masks = nc.dram_tensor("masks", [4, 128, 512], F32, kind="ExternalInput")
    ident = nc.dram_tensor("ident", [128, 128], BF16, kind="ExternalInput")
    att = nc.dram_tensor("att", [4, 128, S], BF16, kind="ExternalOutput")

    KC = D // 128

    ctx = ExitStack()
    with ctx:
        em = Em(nc, ctx)
        wq = sb(ctx, nc, "wq", [128, KC * 512])
        wk = sb(ctx, nc, "wk", [128, KC * 128])
        wv = sb(ctx, nc, "wv", [128, KC * 128])
        xs = [sb(ctx, nc, f"xs{i}", [128, KC * 512]) for i in range(2)]
        qf = [sb(ctx, nc, f"qf{h}", [128, S]) for h in range(4)]
        kf = sb(ctx, nc, "kf", [128, S])
        vt = sb(ctx, nc, "vt", [128, 32 * 128])
        msk = sb(ctx, nc, "msk", [128, 4 * 512], F32)
        idn = sb(ctx, nc, "idn", [128, 128])
        p_sb = [sb(ctx, nc, f"p{h}", [128, 8 * 512]) for h in range(4)]
        pt_sb = sb(ctx, nc, "pt", [128, 512])
        att_sb = [sb(ctx, nc, f"atts{h}", [128, S]) for h in range(4)]
        stat = sb(ctx, nc, "stat", [128, 16], F32)
        sc_ps = ps(ctx, nc, "sc_ps", [128, 3072])
        pt_ps = ps(ctx, nc, "pt_ps", [128, 512], BF16)
        av_ps = ps(ctx, nc, "av_ps", [128, 512])

        ldw = em.sem("ldw")
        ldx = [em.sem("ldx0"), em.sem("ldx1")]
        st = em.sem("st")
        tp = em.sem("tp")
        ta = em.sem("ta")
        vv = em.sem("vv")
        ex = em.sem("ex")

        wqv = wq[:, :].rearrange("p (kc m) -> p kc m", kc=KC)
        wkv = wk[:, :].rearrange("p (kc m) -> p kc m", kc=KC)
        wvv = wv[:, :].rearrange("p (kc m) -> p kc m", kc=KC)

        em.dma("g", wqv, wq_t.ap().rearrange("(kc p) m -> p kc m", p=128), ldw)
        em.dma("g", wkv, wk_t.ap().rearrange("(kc p) m -> p kc m", p=128), ldw)
        em.dma("g", wvv, wv_t.ap().rearrange("(kc p) m -> p kc m", p=128), ldw)
        em.dma("g", msk[:, :].rearrange("p (u m) -> p u m", u=4),
               masks.ap().transpose([1, 0, 2]), ldw)
        v_allw = em.dma("g", idn[:, :], ident[:, :], ldw)

        xb_v = xb.ap().rearrange("(kc p) t -> p kc t", p=128)
        x_use = [0, 0]
        for tc in range(8):
            buf = tc % 2
            tsl = slice(tc * 512, (tc + 1) * 512)
            v_x = em.dma("g", xs[buf][:, :].rearrange("p (kc t) -> p kc t", kc=KC),
                         xb_v[:, :, tsl], ldx[buf],
                         waits=[(tp, x_use[buf])])
            xr = xs[buf][:, :].rearrange("p (kc t) -> p kc t", kc=KC)
            em.wait("t", vv, 9 * tc)
            em.wait("t", ldw, v_allw)
            em.wait("t", ldx[buf], v_x)
            for oc in range(5):
                dst = sc_ps[:, oc * 512:(oc + 1) * 512]
                for kc in range(KC):
                    w_ap = wqv[:, kc, oc * 128:(oc + 1) * 128] if oc < 4 else wkv[:, kc, :]
                    em.op("t", lambda e, _d=dst, _w=w_ap, _x=xr[:, kc, :],
                          _s=(kc == 0), _p=(kc == KC - 1):
                          e.matmul(_d, _w, _x, start=_s, stop=_p),
                          inc=tp if kc == KC - 1 else None)
            xr4 = xr.rearrange("p kc (a q) -> p kc a q", a=4)
            for stt in range(4):
                dst = av_ps[:, stt * 128:(stt + 1) * 128]
                for kc in range(KC):
                    em.op("t", lambda e, _d=dst, _x=xr4[:, kc, stt, :], _w=wvv[:, kc, :],
                          _s=(kc == 0), _p=(kc == KC - 1):
                          e.matmul(_d, _x, _w, start=_s, stop=_p),
                          inc=tp if kc == KC - 1 else None)
            x_use[buf] = tp.v
            base = 9 * tc
            for oc in range(4):
                em.op("v", lambda e, _d=qf[oc][:, tsl], _s=sc_ps[:, oc * 512:(oc + 1) * 512]:
                      e.tensor_copy(_d, _s), waits=[(tp, base + oc + 1)], inc=vv)
            em.op("v", lambda e, _d=kf[:, tsl], _s=sc_ps[:, 4 * 512:5 * 512]:
                  e.tensor_copy(_d, _s), waits=[(tp, base + 5)], inc=vv)
            for stt in range(4):
                dcol = (tc * 4 + stt) * 128
                em.op("v", lambda e, _d=vt[:, dcol:dcol + 128], _s=av_ps[:, stt * 128:(stt + 1) * 128]:
                      e.tensor_copy(_d, _s), waits=[(tp, base + 9)], inc=vv)
        proj_done = vv.v

        def stc(i):
            return stat[:, i:i + 1]

        em.wait("t", vv, proj_done)
        em.wait("v", ldw, v_allw)
        for t in range(32):
            s_stripe, u = t // 4, t % 4
            nkc = s_stripe + 1
            passes = [(0, min(nkc, 6))]
            if nkc > 6:
                passes.append((6, nkc))
            em.wait("s", ta, ta.v)
            em.wait("v", ta, ta.v)
            for h in range(4):
                q_ap = qf[h][:, t * 128:(t + 1) * 128]
                em.wait("v", ex, ex.v)
                for pi, (c0, c1) in enumerate(passes):
                    em.wait("t", ex, ex.v)
                    for c in range(c0, c1):
                        em.op("t", lambda e, _d=sc_ps[:, (c - c0) * 512:(c - c0 + 1) * 512],
                              _q=q_ap, _k=kf[:, c * 512:(c + 1) * 512]:
                              e.matmul(_d, _q, _k, start=True, stop=True),
                              inc=tp if c == c1 - 1 else None)
                    plen = (c1 - c0) * 512
                    wl = [(tp, tp.v)]
                    if pi == len(passes) - 1:
                        moff = (nkc - 1 - c0) * 512
                        em.op("v", lambda e, _d=sc_ps[:, moff:moff + 512], _m=msk[:, u * 512:(u + 1) * 512]:
                              e.tensor_tensor(_d, _d, _m, ALU.add), waits=wl, inc=vv)
                        wl = [(vv, vv.v)]
                    mcol = 0 if pi == 0 else 1
                    em.op("v", lambda e, _d=stc(mcol), _s=sc_ps[:, 0:plen]:
                          e.reduce_max(_d, _s, axis=AX.X), waits=wl, inc=vv)
                    em.wait("v", vv, vv.v)
                    if pi == 0:
                        em.op("v", lambda e: e.tensor_scalar_mul(stc(3), stc(0), -SCALE0), inc=vv)
                    else:
                        em.op("v", lambda e: e.tensor_tensor(stc(2), stc(0), stc(1), ALU.max), inc=vv)
                        em.wait("v", vv, vv.v)
                        em.op("v", lambda e: e.tensor_scalar_mul(stc(3), stc(2), -SCALE0), inc=vv)
                    vb = vv.v
                    scol = 4 if pi == 0 else 5
                    em.op("s", lambda e, _d=p_sb[h][:, c0 * 512:c0 * 512 + plen], _s=sc_ps[:, 0:plen],
                          _b=stc(3), _a=stc(scol):
                          e.activation(_d, _s, AF.Exp, bias=_b, scale=SCALE0, accum_out=_a),
                          waits=[(vv, vb)], inc=ex)
                if len(passes) == 1:
                    em.op("v", lambda e: e.reciprocal(stc(7), stc(4)), waits=[(ex, ex.v)], inc=vv)
                    em.wait("v", vv, vv.v)
                    tot = nkc * 512
                    em.op("v", lambda e, _p=p_sb[h][:, 0:tot]:
                          e.tensor_scalar_mul(_p, _p, stc(7)), inc=vv)
                else:
                    em.op("v", lambda e: e.tensor_tensor(stc(8), stc(0), stc(2), ALU.subtract),
                          waits=[(ex, ex.v)], inc=vv)
                    em.op("s", lambda e: e.activation(stc(9), stc(8), AF.Exp, bias=0.0, scale=SCALE0),
                          waits=[(vv, vv.v)], inc=ex)
                    em.op("v", lambda e: e.scalar_tensor_tensor(stc(6), stc(4), stc(9), stc(5),
                                                                ALU.mult, ALU.add),
                          waits=[(ex, ex.v)], inc=vv)
                    em.wait("v", vv, vv.v)
                    em.op("v", lambda e: e.reciprocal(stc(7), stc(6)), inc=vv)
                    em.wait("v", vv, vv.v)
                    em.op("v", lambda e: e.tensor_mul(stc(10), stc(9), stc(7)), inc=vv)
                    em.wait("v", vv, vv.v)
                    em.op("v", lambda e, _p=p_sb[h][:, 0:3072]:
                          e.tensor_scalar_mul(_p, _p, stc(10)), inc=vv)
                    tot2 = (nkc - 6) * 512
                    em.op("v", lambda e, _p=p_sb[h][:, 3072:3072 + tot2]:
                          e.tensor_scalar_mul(_p, _p, stc(7)), inc=vv)
            p_done = vv.v
            nkb = nkc * 4
            em.wait("t", vv, p_done)
            for kb in range(nkb):
                for h in range(4):
                    em.op("t", lambda e, _d=pt_ps[:, h * 128:(h + 1) * 128],
                          _s=p_sb[h][:, kb * 128:(kb + 1) * 128]:
                          e.transpose(_d, _s, idn[:, :]),
                          waits=[(vv, vv.v)] if h == 0 else (),
                          inc=ta if h == 3 else None)
                em.op("v", lambda e, _d=pt_sb[:, :], _s=pt_ps[:, :]:
                      e.tensor_copy(_d, _s), waits=[(ta, ta.v)], inc=vv)
                em.op("t", lambda e, _v=vt[:, kb * 128:(kb + 1) * 128], _p=pt_sb[:, :],
                      _s=(kb == 0), _e=(kb == nkb - 1):
                      e.matmul(av_ps[:, :], _v, _p, start=_s, stop=_e),
                      waits=[(vv, vv.v)], inc=ta if kb == nkb - 1 else None)
            for h in range(4):
                em.op("v", lambda e, _d=att_sb[h][:, t * 128:(t + 1) * 128],
                      _s=av_ps[:, h * 128:(h + 1) * 128]:
                      e.tensor_copy(_d, _s), waits=[(ta, ta.v)] if h == 0 else (), inc=vv)
        fin = vv.v
        for h in range(4):
            em.op("y", lambda e, _d=att[h, :, :], _s=att_sb[h][:, :]:
                  e.dma_start(out=_d, in_=_s), waits=[(vv, fin)], inc=st, inc_n=16)
        em.wait("y", st, st.v)
        em.run_block()
    return nc


def a1_in_maps(inputs):
    x = np.asarray(inputs["L0_attn_x"])
    wqkv = np.asarray(inputs["L0_attn_weight_qkv"])
    masks = host_masks()
    idn = ident_np()
    maps = []
    for c in range(8):
        b, g = c // 4, c % 4
        xb = np.ascontiguousarray(x[b].T)
        base = g * 768
        wq_t = np.ascontiguousarray(wqkv[base:base + 512].T)
        wk_t = np.ascontiguousarray(wqkv[base + 512:base + 640].T)
        wv_t = np.ascontiguousarray(wqkv[base + 640:base + 768].T)
        maps.append({"xb": xb, "wq_t": wq_t, "wk_t": wk_t, "wv_t": wv_t,
                     "masks": masks, "ident": idn})
    return maps



EPS = 1e-5


def build_a2():
    nc = bass.Bass("TRN2", target_bir_lowering=False)
    KC = D // 128
    T_ = 1024
    IE = 1408
    ICH = IE // 128
    ISH = 2816
    ISCH = ISH // 128

    att_in = nc.dram_tensor("att_in", [D, T_], BF16, kind="ExternalInput")
    x_res = nc.dram_tensor("x_res", [D, T_], F32, kind="ExternalInput")
    wproj_t = nc.dram_tensor("wproj_t", [D, D], F32, kind="ExternalInput")
    router_t = nc.dram_tensor("router_t", [D, 8], F32, kind="ExternalInput")
    gate_w = nc.dram_tensor("gate_w", [8, D, IE], F32, kind="ExternalInput")
    up_w = nc.dram_tensor("up_w", [8, D, IE], F32, kind="ExternalInput")
    down_w = nc.dram_tensor("down_w", [8, IE, D], F32, kind="ExternalInput")
    sg_w = nc.dram_tensor("sg_w", [D, ISH], F32, kind="ExternalInput")
    su_w = nc.dram_tensor("su_w", [D, ISH], F32, kind="ExternalInput")
    sd_w = nc.dram_tensor("sd_w", [ISH, D], F32, kind="ExternalInput")
    sg_b = nc.dram_tensor("sg_b", [ISH], F32, kind="ExternalInput")
    su_b = nc.dram_tensor("su_b", [ISH], F32, kind="ExternalInput")
    sd_b = nc.dram_tensor("sd_b", [D], F32, kind="ExternalInput")
    wq1 = nc.dram_tensor("wq1", [D, D], F32, kind="ExternalInput")
    wk1 = nc.dram_tensor("wk1", [D, D], F32, kind="ExternalInput")
    wv1 = nc.dram_tensor("wv1", [D, D], F32, kind="ExternalInput")
    ones1 = nc.dram_tensor("ones1", [1, 128], BF16, kind="ExternalInput")
    ones128 = nc.dram_tensor("ones128", [128, 1], BF16, kind="ExternalInput")
    sel = nc.dram_tensor("sel", [8, 8, 128], BF16, kind="ExternalInput")
    ident32 = nc.dram_tensor("ident32", [128, 128], F32, kind="ExternalInput")

    x2_out = nc.dram_tensor("x2_out", [D, T_], F32, kind="ExternalOutput")
    q1_out = nc.dram_tensor("q1_out", [D, T_], BF16, kind="ExternalOutput")
    k1_out = nc.dram_tensor("k1_out", [D, T_], BF16, kind="ExternalOutput")
    v1_out = nc.dram_tensor("v1_out", [D, T_], BF16, kind="ExternalOutput")

    ctx = ExitStack()
    with ctx:
        em = Em(nc, ctx)
        stream = sb(ctx, nc, "stream", [128, KC * T_], F32)
        xbf = sb(ctx, nc, "xbf", [128, KC * T_])
        att_sb = sb(ctx, nc, "att_sb", [128, 8 * T_])
        h_sb = sb(ctx, nc, "h_sb", [128, ICH * T_])
        cbc = sb(ctx, nc, "cbc", [128, T_])
        wb = [sb(ctx, nc, f"wb{i}", [128, 16 * 128]) for i in range(8)]
        s_ch = sb(ctx, nc, "s_ch", [128, T_])
        u_ch = sb(ctx, nc, "u_ch", [128, T_])
        sq_ch = [sb(ctx, nc, "sq0", [128, T_])] * 2
        sq2 = [sb(ctx, nc, "sq2_0", [128, T_])] * 2
        rwt = sb(ctx, nc, "rwt", [128, KC * 8])
        comb = sb(ctx, nc, "comb", [128, 8], F32)
        comb_fm = sb(ctx, nc, "comb_fm", [8, T_])
        rstat = sb(ctx, nc, "rstat", [128, 28], F32)
        lnrow = sb(ctx, nc, "lnrow", [1, 4 * T_], F32)
        lnrow_bf = sb(ctx, nc, "lnrow_bf", [1, 2 * T_])
        o1 = sb(ctx, nc, "o1", [1, 128])
        o128 = sb(ctx, nc, "o128", [128, 1])
        sel_sb = sb(ctx, nc, "sel_sb", [8, 8 * 128])
        id32 = sb(ctx, nc, "id32", [128, 128], F32)
        sgb = sb(ctx, nc, "sgb", [128, ISCH], F32)
        sub = sb(ctx, nc, "sub", [128, ISCH], F32)
        sdb = sb(ctx, nc, "sdb", [128, KC], F32)
        stg = [sb(ctx, nc, f"stg{i}", [128, T_]) for i in range(2)]
        pool = ps(ctx, nc, "pool", [128, 4096])
        A = pool[:, 0:1024]
        Bk = pool[:, 1024:2048]
        C = pool[:, 2048:3072]
        Dk = pool[:, 3072:4096]

        ldc = em.sem("ldc")
        ldg = em.sem("ldg")
        wsem = [em.sem(f"ws{i}") for i in range(8)]
        st = em.sem("st")
        tq = em.sem("tq")
        vv = em.sem("vv")
        ss = em.sem("ss")
        em.set_autochain("v", vv)
        em.set_autochain("s", ss)

        em.dma("y", stream[:, :].rearrange("p (kc t) -> p kc t", kc=KC),
               x_res.ap().rearrange("(kc p) t -> p kc t", p=128), ldc)
        for i in range(ISCH):
            em.dma("y", sgb[:, i:i + 1], sg_b.ap().rearrange("(i p) -> i p", p=128)[i:i + 1, :].transpose([1, 0]), ldc)
            em.dma("y", sub[:, i:i + 1], su_b.ap().rearrange("(i p) -> i p", p=128)[i:i + 1, :].transpose([1, 0]), ldc)
        for i in range(KC):
            em.dma("y", sdb[:, i:i + 1], sd_b.ap().rearrange("(i p) -> i p", p=128)[i:i + 1, :].transpose([1, 0]), ldc)
        em.dma("y", o1[:, :], ones1[:, :], ldc)
        em.dma("y", o128[:, :], ones128[:, :], ldc)
        em.dma("y", sel_sb[:, :].rearrange("p (e m) -> p e m", e=8),
               sel.ap().transpose([1, 0, 2]), ldc)
        v_ldc = em.dma("y", id32[:, :], ident32[:, :], ldc)
        v_rwt = em.dma("g", rwt[:, :].rearrange("p (kc m) -> p kc m", kc=KC),
                       router_t.ap().rearrange("(kc p) m -> p kc m", p=128), ldg)
        em.wait("t", ldc, v_ldc)
        em.wait("v", ldc, v_ldc)
        em.wait("s", ldc, v_ldc)

        wuse = [0] * 8

        def wload(i, dram_slice):
            kcs = dram_slice.shape[0] // 128
            return em.dma("g", wb[i][:, 0:kcs * 128].rearrange("p (kc m) -> p kc m", kc=kcs),
                          dram_slice.rearrange("(kc p) m -> p kc m", p=128),
                          wsem[i], waits=[(tq, wuse[i])])

        strv = stream[:, :].rearrange("p (kc t) -> p kc t", kc=KC)
        xbv = xbf[:, :].rearrange("p (kc t) -> p kc t", kc=KC)
        attv = att_sb[:, :].rearrange("p (kc t) -> p kc t", kc=8)

        def wbv(i, kcs=16):
            return wb[i][:, 0:kcs * 128].rearrange("p (kc m) -> p kc m", kc=kcs)

        lda = em.sem("lda")
        for kh in range(2):
            v_att = em.dma("g", attv[:, :, :],
                           att_in.ap().rearrange("(kc p) t -> p kc t", p=128)[:, kh * 8:(kh + 1) * 8, :],
                           lda, waits=[(vv, vv.v)])
            em.wait("t", lda, v_att)
            for oc in range(KC):
                wi = oc % 2
                v_w = wload(wi, wproj_t.ap()[kh * 1024:(kh + 1) * 1024, oc * 128:(oc + 1) * 128])
                dst = A if oc % 2 == 0 else Bk
                em.wait("t", wsem[wi], v_w)
                em.wait("t", vv, vv.v)
                for hf in range(2):
                    for kc in range(8):
                        em.op("t", lambda e, _d=dst[:, hf * 512:(hf + 1) * 512],
                              _w=wbv(wi, 8)[:, kc, :],
                              _x=attv[:, kc, hf * 512:(hf + 1) * 512],
                              _s=(kc == 0), _p=(kc == 7):
                              e.matmul(_d, _w, _x, start=_s, stop=_p),
                              inc=tq if (kc == 7 and hf == 1) else None)
                wuse[wi] = tq.v
                em.op("v", lambda e, _d=strv[:, oc, :], _s=dst[:, :]:
                      e.tensor_tensor(_d, _d, _s, ALU.add), waits=[(tq, tq.v)], inc=vv)

        def layernorm():
            em.wait("s", vv, vv.v)
            squse = [0, 0]
            for kc in range(KC):
                b2 = 0
                em.op("v", lambda e, _d=sq_ch[b2][:, :], _s=strv[:, kc, :]:
                      e.tensor_copy(_d, _s), waits=[(tq, squse[b2])], inc=vv)
                em.op("s", lambda e, _d=sq2[b2][:, :], _s=strv[:, kc, :]:
                      e.activation(_d, _s, AF.Square, bias=0.0, scale=1.0),
                      waits=[(tq, squse[b2])], inc=ss)
                for hf in range(2):
                    em.op("t", lambda e, _d=pool[0:1, 2048 + hf * 512:2048 + (hf + 1) * 512],
                          _o=o128[:, :], _x=sq_ch[b2][:, hf * 512:(hf + 1) * 512],
                          _s=(kc == 0), _p=(kc == KC - 1):
                          e.matmul(_d, _o, _x, start=_s, stop=_p),
                          waits=[(vv, vv.v)] if hf == 0 else (),
                          inc=tq if hf == 1 else None)
                for hf in range(2):
                    em.op("t", lambda e, _d=pool[0:1, 3072 + hf * 512:3072 + (hf + 1) * 512],
                          _o=o128[:, :], _x=sq2[b2][:, hf * 512:(hf + 1) * 512],
                          _s=(kc == 0), _p=(kc == KC - 1):
                          e.matmul(_d, _o, _x, start=_s, stop=_p),
                          waits=[(ss, ss.v)] if hf == 0 else (),
                          inc=tq if hf == 1 else None)
                squse[b2] = tq.v
            Ar, Br = lnrow[:, 0:T_], lnrow[:, T_:2 * T_]
            Cr, Dr = lnrow[:, 2 * T_:3 * T_], lnrow[:, 3 * T_:4 * T_]
            em.op("v", lambda e: e.tensor_copy(Ar, pool[0:1, 2048:3072]),
                  waits=[(tq, tq.v)], inc=vv)
            em.op("v", lambda e: e.tensor_copy(Br, pool[0:1, 3072:4096]), inc=vv)
            em.wait("v", vv, vv.v)
            em.op("v", lambda e: e.tensor_scalar_mul(Cr, Ar, 1.0 / D), inc=vv)
            em.wait("v", vv, vv.v)
            em.op("v", lambda e: e.tensor_scalar_mul(Ar, Br, 1.0 / D), inc=vv)
            em.wait("v", vv, vv.v)
            em.op("v", lambda e: e.tensor_mul(Br, Cr, Cr), inc=vv)
            em.wait("v", vv, vv.v)
            em.op("v", lambda e: e.tensor_sub(Ar, Ar, Br), inc=vv)
            em.wait("v", vv, vv.v)
            em.op("v", lambda e: e.tensor_scalar_add(Ar, Ar, EPS), inc=vv)
            em.op("s", lambda e: e.activation(Br, Ar, AF.Ln, bias=0.0, scale=1.0),
                  waits=[(vv, vv.v)], inc=ss)
            em.wait("s", ss, ss.v)
            em.op("s", lambda e: e.activation(Dr, Br, AF.Exp, bias=0.0, scale=-0.5),
                  inc=ss)
            em.op("v", lambda e: e.tensor_mul(Ar, Cr, Dr),
                  waits=[(ss, ss.v)], inc=vv)
            em.wait("v", vv, vv.v)
            em.op("v", lambda e: e.tensor_copy(lnrow_bf[:, 0:T_], Dr), inc=vv)
            em.op("v", lambda e: e.tensor_copy(lnrow_bf[:, T_:2 * T_], Ar), inc=vv)
            for hf in range(2):
                em.op("t", lambda e, _d=C[:, hf * 512:(hf + 1) * 512], _r=lnrow_bf[:, hf * 512:(hf + 1) * 512]:
                      e.matmul(_d, o1[:, :], _r, start=True, stop=True),
                      waits=[(vv, vv.v)] if hf == 0 else (), inc=tq if hf == 1 else None)
                em.op("t", lambda e, _d=Dk[:, hf * 512:(hf + 1) * 512], _r=lnrow_bf[:, T_ + hf * 512:T_ + (hf + 1) * 512]:
                      e.matmul(_d, o1[:, :], _r, start=True, stop=True),
                      inc=tq if hf == 1 else None)
            for kc in range(KC):
                em.op("v", lambda e, _x=strv[:, kc, :], _a=C[:, :]:
                      e.tensor_mul(_x, _x, _a), waits=[(tq, tq.v)] if kc == 0 else (), inc=vv)
                em.wait("v", vv, vv.v)
                em.op("v", lambda e, _x=strv[:, kc, :], _b=Dk[:, :]:
                      e.tensor_sub(_x, _x, _b), inc=vv)
                em.wait("v", vv, vv.v)
                em.op("v", lambda e, _d=xbv[:, kc, :], _s=strv[:, kc, :]:
                      e.tensor_copy(_d, _s), inc=vv)
            em.wait("t", vv, vv.v)

        layernorm()

        for tt in range(8):
            em.wait("t", ldg, v_rwt)
            for kc in range(KC):
                em.op("t", lambda e, _d=C[:, 0:8],
                      _x=xbv[:, kc, tt * 128:(tt + 1) * 128],
                      _w=rwt[:, :].rearrange("p (kc m) -> p kc m", kc=KC)[:, kc, :],
                      _s=(kc == 0), _p=(kc == KC - 1):
                      e.matmul(_d, _x, _w, start=_s, stop=_p),
                      inc=tq if kc == KC - 1 else None)
            em.op("v", lambda e: e.reduce_max(rstat[:, 0:1], C[:, 0:8], axis=AX.X),
                  waits=[(tq, tq.v)], inc=vv)
            em.wait("v", vv, vv.v)
            em.op("v", lambda e: e.tensor_scalar_mul(rstat[:, 1:2], rstat[:, 0:1], -1.0), inc=vv)
            em.op("s", lambda e: e.activation(comb[:, 0:8], C[:, 0:8], AF.Exp,
                                              bias=rstat[:, 1:2], scale=1.0),
                  waits=[(vv, vv.v), (tq, tq.v)], inc=ss)
            em.op("v", lambda e: e.reduce_max(rstat[:, 2:3], comb[:, 0:8], axis=AX.X),
                  waits=[(ss, ss.v)], inc=vv)
            em.wait("v", vv, vv.v)
            em.op("v", lambda e: e.tensor_single_scalar(rstat[:, 12:20], comb[:, 0:8], rstat[:, 2:3], ALU.is_equal), inc=vv)
            em.wait("v", vv, vv.v)
            em.op("v", lambda e: e.scalar_tensor_tensor(rstat[:, 20:28], rstat[:, 12:20], -1.0e30,
                                                        comb[:, 0:8], ALU.mult, ALU.add), inc=vv)
            em.wait("v", vv, vv.v)
            em.op("v", lambda e: e.reduce_max(rstat[:, 3:4], rstat[:, 20:28], axis=AX.X), inc=vv)
            em.wait("v", vv, vv.v)
            em.op("v", lambda e: e.tensor_single_scalar(rstat[:, 20:28], comb[:, 0:8], rstat[:, 3:4], ALU.is_equal), inc=vv)
            em.op("v", lambda e: e.tensor_add(rstat[:, 4:5], rstat[:, 2:3], rstat[:, 3:4]), inc=vv)
            em.wait("v", vv, vv.v)
            em.op("v", lambda e: e.reciprocal(rstat[:, 5:6], rstat[:, 4:5]), inc=vv)
            em.wait("v", vv, vv.v)
            em.op("v", lambda e: e.tensor_mul(rstat[:, 6:7], rstat[:, 2:3], rstat[:, 5:6]), inc=vv)
            em.op("v", lambda e: e.tensor_mul(rstat[:, 7:8], rstat[:, 3:4], rstat[:, 5:6]), inc=vv)
            em.wait("v", vv, vv.v)
            em.op("v", lambda e: e.tensor_single_scalar(rstat[:, 12:20], rstat[:, 12:20], rstat[:, 6:7], ALU.mult), inc=vv)
            em.wait("v", vv, vv.v)
            em.op("v", lambda e: e.scalar_tensor_tensor(comb[:, 0:8], rstat[:, 20:28], rstat[:, 7:8],
                                                        rstat[:, 12:20], ALU.mult, ALU.add), inc=vv)
            em.op("t", lambda e, _d=Dk[0:8, 0:128], _s=comb[:, 0:8]:
                  e.transpose(_d, _s, id32[:, :]),
                  waits=[(vv, vv.v)], inc=tq)
            em.op("v", lambda e, _d=comb_fm[0:8, tt * 128:(tt + 1) * 128], _s=Dk[0:8, 0:128]:
                  e.tensor_copy(_d, _s), waits=[(tq, tq.v)], inc=vv)
        comb_done = vv.v

        def gu_half(dst, wi, hf):
            for kc in range(KC):
                em.op("t", lambda e, _d=dst[:, hf * 512:(hf + 1) * 512],
                      _w=wbv(wi)[:, kc, :],
                      _x=xbv[:, kc, hf * 512:(hf + 1) * 512],
                      _s=(kc == 0), _p=(kc == KC - 1):
                      e.matmul(_d, _w, _x, start=_s, stop=_p),
                      inc=tq if kc == KC - 1 else None)

        for e_ in range(8):
            for hf in range(2):
                em.op("t", lambda e, _d=C[:, hf * 512:(hf + 1) * 512],
                      _s=sel_sb[:, :].rearrange("p (e m) -> p e m", e=8)[:, e_, :],
                      _r=comb_fm[0:8, hf * 512:(hf + 1) * 512]:
                      e.matmul(_d, _s, _r, start=True, stop=True),
                      waits=[(vv, comb_done if e_ == 0 else vv.v)] if hf == 0 else (),
                      inc=tq if hf == 1 else None)
            em.op("v", lambda e: e.tensor_copy(cbc[:, :], C[:, :]), waits=[(tq, tq.v)], inc=vv)
            cb_ready = vv.v
            for ii in range(ICH):
                wg_i, wu_i = 2 + ii % 2, 4 + ii % 2
                v_wg = wload(wg_i, gate_w.ap()[e_, :, ii * 128:(ii + 1) * 128])
                v_wu = wload(wu_i, up_w.ap()[e_, :, ii * 128:(ii + 1) * 128])
                em.wait("t", wsem[wg_i], v_wg)
                em.wait("t", wsem[wu_i], v_wu)
                em.wait("t", ss, ss.v)
                em.wait("t", vv, vv.v)
                gu_half(A, wg_i, 0); gu_half(A, wg_i, 1)
                wuse[wg_i] = tq.v
                g_done = tq.v
                gu_half(Bk, wu_i, 0); gu_half(Bk, wu_i, 1)
                wuse[wu_i] = tq.v
                em.op("s", lambda e: e.activation(s_ch[:, :], A[:, :], AF.Silu, bias=0.0, scale=1.0),
                      waits=[(tq, g_done), (vv, vv.v)], inc=ss)
                em.op("v", lambda e: e.tensor_tensor(u_ch[:, :], s_ch[:, :], Bk[:, :], ALU.mult),
                      waits=[(ss, ss.v), (tq, tq.v), (vv, cb_ready)], inc=vv)
                em.wait("v", vv, vv.v)
                em.op("v", lambda e, _d=h_sb[:, ii * T_:(ii + 1) * T_]:
                      e.tensor_tensor(_d, u_ch[:, :], cbc[:, :], ALU.mult), inc=vv)
            h_done = vv.v
            for oc in range(KC):
                wdi = 6 + (oc % 2)
                v_wd = wload(wdi, down_w.ap()[e_, :, oc * 128:(oc + 1) * 128])
                dst = C if oc % 2 == 0 else Dk
                em.wait("t", wsem[wdi], v_wd)
                em.wait("t", vv, h_done if oc == 0 else vv.v)
                for hf in range(2):
                    for ii in range(ICH):
                        em.op("t", lambda e, _d=dst[:, hf * 512:(hf + 1) * 512],
                              _w=wbv(wdi, ICH)[:, ii, :],
                              _h=h_sb[:, ii * T_ + hf * 512: ii * T_ + (hf + 1) * 512],
                              _s=(ii == 0), _p=(ii == ICH - 1):
                              e.matmul(_d, _w, _h, start=_s, stop=_p),
                              inc=tq if (ii == ICH - 1 and hf == 1) else None)
                wuse[wdi] = tq.v
                em.op("v", lambda e, _x=strv[:, oc, :], _s=dst[:, :]:
                      e.tensor_tensor(_x, _x, _s, ALU.add), waits=[(tq, tq.v)], inc=vv)

        for half in range(2):
            ib0 = half * 11
            for ii in range(11):
                ic = ib0 + ii
                wg_i, wu_i = 2 + ii % 2, 4 + ii % 2
                v_wg = wload(wg_i, sg_w.ap()[:, ic * 128:(ic + 1) * 128])
                v_wu = wload(wu_i, su_w.ap()[:, ic * 128:(ic + 1) * 128])
                em.wait("t", wsem[wg_i], v_wg)
                em.wait("t", wsem[wu_i], v_wu)
                em.wait("t", ss, ss.v)
                em.wait("t", vv, vv.v)
                gu_half(A, wg_i, 0); gu_half(A, wg_i, 1)
                wuse[wg_i] = tq.v
                g_done = tq.v
                gu_half(Bk, wu_i, 0); gu_half(Bk, wu_i, 1)
                wuse[wu_i] = tq.v
                em.op("s", lambda e, _b=sgb[:, ic:ic + 1]:
                      e.activation(s_ch[:, :], A[:, :], AF.Silu, bias=_b, scale=1.0),
                      waits=[(tq, g_done), (vv, vv.v)], inc=ss)
                em.op("v", lambda e, _b=sub[:, ic:ic + 1]:
                      e.tensor_scalar_add(u_ch[:, :], Bk[:, :], _b),
                      waits=[(tq, tq.v)], inc=vv)
                em.wait("v", vv, vv.v)
                em.op("v", lambda e, _d=h_sb[:, ii * T_:(ii + 1) * T_]:
                      e.tensor_tensor(_d, s_ch[:, :], u_ch[:, :], ALU.mult),
                      waits=[(ss, ss.v)], inc=vv)
            h_done = vv.v
            for oc in range(KC):
                wdi = 6 + (oc % 2)
                v_wd = wload(wdi, sd_w.ap()[ib0 * 128:(ib0 + 11) * 128, oc * 128:(oc + 1) * 128])
                dst = C if oc % 2 == 0 else Dk
                em.wait("t", wsem[wdi], v_wd)
                em.wait("t", vv, h_done if oc == 0 else vv.v)
                for hf in range(2):
                    for ii in range(11):
                        em.op("t", lambda e, _d=dst[:, hf * 512:(hf + 1) * 512],
                              _w=wbv(wdi, 11)[:, ii, :],
                              _h=h_sb[:, ii * T_ + hf * 512: ii * T_ + (hf + 1) * 512],
                              _s=(ii == 0), _p=(ii == 11 - 1):
                              e.matmul(_d, _w, _h, start=_s, stop=_p),
                              inc=tq if (ii == 11 - 1 and hf == 1) else None)
                wuse[wdi] = tq.v
                em.op("v", lambda e, _x=strv[:, oc, :], _s=dst[:, :]:
                      e.tensor_tensor(_x, _x, _s, ALU.add), waits=[(tq, tq.v)], inc=vv)
        for oc in range(KC):
            em.op("v", lambda e, _x=strv[:, oc, :], _b=sdb[:, oc:oc + 1]:
                  e.tensor_scalar_add(_x, _x, _b), inc=vv)
        em.wait("t", vv, vv.v)

        layernorm()
        x2_ready = vv.v
        for kc in range(KC):
            em.op("y", lambda e, _d=x2_out.ap().rearrange("(kc p) t -> p kc t", p=128)[:, kc, :],
                  _s=strv[:, kc, :]:
                  e.dma_start(out=_d, in_=_s), waits=[(vv, x2_ready)], inc=st, inc_n=16)

        stsem = [em.sem("stg0"), em.sem("stg1")]
        stguse = [0, 0]
        for wt, outt in ((wq1, q1_out), (wk1, k1_out), (wv1, v1_out)):
            for oc in range(KC):
                wi = oc % 2
                v_w = wload(wi, wt.ap()[:, oc * 128:(oc + 1) * 128])
                dst = A if oc % 2 == 0 else Bk
                sgi = oc % 2
                em.wait("t", wsem[wi], v_w)
                em.wait("t", vv, vv.v)
                for hf in range(2):
                    for kc in range(KC):
                        em.op("t", lambda e, _d=dst[:, hf * 512:(hf + 1) * 512],
                              _w=wbv(wi)[:, kc, :],
                              _x=xbv[:, kc, hf * 512:(hf + 1) * 512],
                              _s=(kc == 0), _p=(kc == KC - 1):
                              e.matmul(_d, _w, _x, start=_s, stop=_p),
                              inc=tq if (kc == KC - 1 and hf == 1) else None)
                wuse[wi] = tq.v
                em.op("v", lambda e, _d=stg[sgi][:, :], _s=dst[:, :]:
                      e.tensor_copy(_d, _s), waits=[(tq, tq.v), (stsem[sgi], stguse[sgi])], inc=vv)
                stguse[sgi] = em.dma("y", outt.ap().rearrange("(kc p) t -> p kc t", p=128)[:, oc, :],
                                     stg[sgi][:, :], stsem[sgi], waits=[(vv, vv.v)])
        em.wait("y", st, st.v)
        em.wait("y", stsem[0], stguse[0])
        em.wait("y", stsem[1], stguse[1])
        em.run_block()
    return nc


def a2_in_maps(inputs, att_full):
    x = np.asarray(inputs["L0_attn_x"])
    maps = []
    common = {
        "wproj_t": np.ascontiguousarray(np.asarray(inputs["L0_attn_weight_proj"]).T),
        "router_t": np.ascontiguousarray(np.asarray(inputs["L0_ffn_router_w"]).T),
        "gate_w": np.asarray(inputs["L0_ffn_gate_w"]),
        "up_w": np.asarray(inputs["L0_ffn_up_w"]),
        "down_w": np.asarray(inputs["L0_ffn_down_w"]),
        "sg_w": np.asarray(inputs["L0_ffn_s_gate_w"]),
        "su_w": np.asarray(inputs["L0_ffn_s_up_w"]),
        "sd_w": np.asarray(inputs["L0_ffn_s_down_w"]),
        "sg_b": np.asarray(inputs["L0_ffn_s_gate_b"]),
        "su_b": np.asarray(inputs["L0_ffn_s_up_b"]),
        "sd_b": np.asarray(inputs["L0_ffn_s_down_b"]),
        "wq1": np.asarray(inputs["L1_attn_Wq"]),
        "wk1": np.asarray(inputs["L1_attn_Wk"]),
        "wv1": np.asarray(inputs["L1_attn_Wv"]),
        "ones1": np.ones((1, 128), BF),
        "ones128": np.ones((128, 1), BF),
        "sel": np.eye(8, dtype=BF)[:, :, None] * np.ones((1, 1, 128), BF),
        "ident32": np.eye(128, dtype=np.float32),
    }
    for c in range(8):
        b = c // 4
        s0, s1 = stripes_of_core(c)
        cols = np.r_[s0 * 512:(s0 + 1) * 512, s1 * 512:(s1 + 1) * 512]
        m = dict(common)
        m["att_in"] = np.ascontiguousarray(att_full[b][:, cols])
        m["x_res"] = np.ascontiguousarray(x[b].T[:, cols].astype(np.float32))
        maps.append(m)
    return maps



def build_b1():
    nc = bass.Bass("TRN2", target_bir_lowering=False)
    H = 4
    HD = H * 128
    SEG = 512

    qf_d = nc.dram_tensor("qf", [HD, S], BF16, kind="ExternalInput")
    kf_d = nc.dram_tensor("kf", [HD, S], BF16, kind="ExternalInput")
    vt_d = nc.dram_tensor("vt", [S, HD], BF16, kind="ExternalInput")
    betasT = nc.dram_tensor("betasT", [128, H], F32, kind="ExternalInput")
    tri = nc.dram_tensor("tri", [128, 128], F32, kind="ExternalInput")
    ident = nc.dram_tensor("ident", [128, 128], BF16, kind="ExternalInput")
    ones1b = nc.dram_tensor("ones1b", [1, 128], BF16, kind="ExternalInput")
    att_d = nc.dram_tensor("att", [HD, S], BF16, kind="ExternalOutput")

    ctx = ExitStack()
    with ctx:
        em = Em(nc, ctx)
        qseg = sb(ctx, nc, "qseg", [128, H * 1024])
        kseg = sb(ctx, nc, "kseg", [128, H * 1024])
        vseg = sb(ctx, nc, "vseg", [128, 8 * HD])
        mem_aug = sb(ctx, nc, "mem_aug", [128, H * 129], F32)
        mem_bf = sb(ctx, nc, "mem_bf", [128, H * 129])
        sigk_b = [sb(ctx, nc, f"sigk{i}", [128, 512]) for i in range(2)]
        sigq_b = [sb(ctx, nc, f"sigq{i}", [128, 512]) for i in range(2)]
        sigkT = sb(ctx, nc, "sigkT", [128, 512])
        sck_b = [sb(ctx, nc, f"sck{i}", [128, 512], F32) for i in range(2)]
        sce_b = [sb(ctx, nc, f"sce{i}", [128, 512]) for i in range(2)]
        delt = sb(ctx, nc, "delt", [128, 512])
        p_sb = sb(ctx, nc, "p_sb", [128, 512])
        pt2 = sb(ctx, nc, "pt2", [128, 512])
        adot = sb(ctx, nc, "adot", [128, 512])
        astg = [sb(ctx, nc, f"astg{i}", [128, 512]) for i in range(2)]
        rowr = sb(ctx, nc, "rowr", [1, 1024], F32)
        rowb = sb(ctx, nc, "rowb", [1, 512])
        beta_sb = sb(ctx, nc, "beta_sb", [128, H], F32)
        omb_sb = sb(ctx, nc, "omb_sb", [128, H], F32)
        tri_sb = sb(ctx, nc, "tri_sb", [128, 128], F32)
        idn = sb(ctx, nc, "idn", [128, 128])
        o1 = sb(ctx, nc, "o1", [1, 128])
        stat_t = sb(ctx, nc, "stat_t", [128, 16], F32)
        pool = ps(ctx, nc, "pool", [128, 4096])
        sc_ps = pool[:, 0:512]
        ptp = pool[:, 512:1024]
        av_ps = pool[:, 1024:1152]
        md_ps = pool[:, 1152:1280]
        attm_ps = pool[:, 1536:2048]
        bc_ps = pool[:, 2048:2560]
        zq_ps = pool[0:1, 2560:3072]
        ds_ps = [pool[:, 3072:3201], pool[:, 3584:3713]]
        skT_ps = pool[:, 3328:3584]

        ldc = em.sem("ldc")
        lds = [em.sem("lds0"), em.sem("lds1")]
        st = em.sem("st")
        stsem = [em.sem("stg0"), em.sem("stg1")]
        tq = em.sem("tq")
        vv = em.sem("vv")
        ss = em.sem("ss")
        em.set_autochain("v", vv)
        em.set_autochain("s", ss)

        ptp_bf = ptp.tensor.bitcast(BF16)
        skT_bf = skT_ps.tensor.bitcast(BF16)

        em.dma("y", beta_sb[:, :], betasT[:, :], ldc)
        em.dma("y", tri_sb[:, :], tri[:, :], ldc)
        em.dma("y", idn[:, :], ident[:, :], ldc)
        v_ldc = em.dma("y", o1[:, :], ones1b[:, :], ldc)
        em.wait("t", ldc, v_ldc)
        em.wait("v", ldc, v_ldc)
        em.wait("s", ldc, v_ldc)
        em.op("s", lambda e: e.activation(beta_sb[:, :], beta_sb[:, :], AF.Sigmoid, bias=0.0, scale=1.0))
        em.op("v", lambda e: e.scalar_tensor_tensor(omb_sb[:, :], beta_sb[:, :], -1.0, beta_sb[:, :],
                                                    ALU.mult, ALU.bypass), waits=[(ss, ss.v)])
        em.op("v", lambda e: e.tensor_scalar_add(omb_sb[:, :], omb_sb[:, :], 1.0))
        em.op("v", lambda e: e.memset(mem_aug[:, :], 0.0))
        for h in range(H):
            em.op("v", lambda e, _c=mem_aug[:, h * 129 + 128:h * 129 + 129]:
                  e.memset(_c, 1.0 / 128.0))

        qsv = qseg[:, :].rearrange("p (h t) -> p h t", h=H)
        ksv = kseg[:, :].rearrange("p (h t) -> p h t", h=H)
        vsv = vseg[:, :].rearrange("p (s f) -> p s f", s=8)

        sig_use = [0, 0]

        def sigma(dst, src_ap, hp):
            em.op("v", lambda e, _d=sck_b[hp][:, :]: e.tensor_scalar_min(_d, src_ap, 0.0),
                  waits=[(tq, sig_use[hp]), (ss, ss.v)])
            em.op("s", lambda e, _d=sce_b[hp][:, :], _s=sck_b[hp][:, :]:
                  e.activation(_d, _s, AF.Exp, bias=0.0, scale=1.0),
                  waits=[(vv, vv.v)])
            em.op("v", lambda e, _e=sce_b[hp][:, :]: e.scalar_tensor_tensor(dst, src_ap, 0.0, _e,
                                                        ALU.max, ALU.add), waits=[(ss, ss.v)])

        seg_use = [0, 0]
        seg_use_v = [0, 0]
        stg_use = [0, 0]
        for s_ in range(8):
            par = s_ % 2
            tsl = slice(s_ * 512, (s_ + 1) * 512)
            v_q = em.dma("g", qsv[:, :, par * 512:(par + 1) * 512],
                         qf_d.ap().rearrange("(h p) t -> p h t", p=128)[:, :, tsl],
                         lds[par], waits=[(tq, seg_use[par]), (vv, seg_use_v[par])])
            v_k = em.dma("g", ksv[:, :, par * 512:(par + 1) * 512],
                         kf_d.ap().rearrange("(h p) t -> p h t", p=128)[:, :, tsl],
                         lds[par], waits=[])
            v_v = em.dma("g", vsv[:, par * 4:(par + 1) * 4, :],
                         vt_d.ap()[tsl, :].rearrange("(sub p) f -> p sub f", p=128),
                         lds[par], waits=[])
            em.wait("t", lds[par], v_v)
            em.wait("v", lds[par], v_v)
            for h in range(H):
                hp = h % 2
                sigk = sigk_b[hp]
                sigq = sigq_b[hp]
                kb0 = h * 1024 + par * 512
                kk = kseg[:, kb0:kb0 + 512]
                qq = qseg[:, kb0:kb0 + 512]
                mb = mem_bf[:, h * 129:(h + 1) * 129]
                ma = mem_aug[:, h * 129:(h + 1) * 129]
                em.op("v", lambda e, _d=mb, _s=ma: e.tensor_copy(_d, _s),
                      waits=[(tq, sig_use[hp])])
                sigma(sigk[:, :], kk, hp)
                sigma(sigq[:, :], qq, hp)
                sig_done = vv.v
                em.op("t", lambda e, _z=mem_bf[:, h * 129 + 128:h * 129 + 129], _q=sigq[:, :]:
                      e.matmul(zq_ps, _z, _q, start=True, stop=True),
                      waits=[(vv, sig_done)], inc=tq)
                em.op("v", lambda e: e.tensor_copy(rowr[:, 0:512], zq_ps), waits=[(tq, tq.v)])
                em.op("v", lambda e: e.reciprocal(rowr[:, 512:1024], rowr[:, 0:512]),
                      waits=[(vv, vv.v)])
                em.op("v", lambda e: e.tensor_copy(rowb[:, :], rowr[:, 512:1024]))
                em.op("t", lambda e: e.matmul(bc_ps, o1[:, :], rowb[:, :], start=True, stop=True),
                      waits=[(vv, vv.v)], inc=tq)
                em.op("v", lambda e, _q=sigq[:, :]: e.tensor_tensor(_q, _q, bc_ps, ALU.mult),
                      waits=[(tq, tq.v)])
                sq_done = vv.v
                em.op("t", lambda e, _m=mem_bf[:, h * 129:h * 129 + 128], _q=sigq[:, :]:
                      e.matmul(attm_ps, _m, _q, start=True, stop=True),
                      waits=[(vv, sq_done)], inc=tq)
                attm_done = tq.v
                for sub in range(4):
                    dsp = ds_ps[sub % 2]
                    em.op("t", lambda e, _l=sigk[:, sub * 128:(sub + 1) * 128], _m=mb, _d=dsp:
                          e.matmul(_d, _l, _m, start=True, stop=True),
                          waits=[(vv, vv.v)], inc=tq)
                    em.op("v", lambda e, _d=dsp, _o=stat_t[:, hp * 8 + 0:hp * 8 + 1]:
                          e.reciprocal(_o, _d[:, 128:129]),
                          waits=[(tq, tq.v)])
                    em.op("v", lambda e, _o=stat_t[:, hp * 8 + 1:hp * 8 + 2], _i=stat_t[:, hp * 8 + 0:hp * 8 + 1]:
                          e.tensor_scalar_mul(_o, _i, -1.0),
                          waits=[(vv, vv.v)])
                    em.op("v", lambda e, _d=delt[:, sub * 128:(sub + 1) * 128], _p=dsp,
                          _sc=stat_t[:, hp * 8 + 1:hp * 8 + 2],
                          _v=vseg[:, (par * 4 + sub) * HD + h * 128:(par * 4 + sub) * HD + (h + 1) * 128]:
                          e.scalar_tensor_tensor(_d, _p[:, 0:128], _sc, _v,
                                                 ALU.mult, ALU.add),
                          waits=[(vv, vv.v), (tq, tq.v)])
                delta_done = vv.v
                sgi = (s_ * H + h) % 2
                for sub in range(4):
                    nkb = sub + 1
                    em.op("t", lambda e, _q=qseg[:, kb0 + sub * 128:kb0 + (sub + 1) * 128],
                          _k=kseg[:, kb0:kb0 + nkb * 128], _d=pool[:, 0:nkb * 128]:
                          e.matmul(_d, _q, _k, start=True, stop=True),
                          waits=[(ss, ss.v), (vv, vv.v)], inc=tq)
                    em.op("v", lambda e, _d=pool[:, (nkb - 1) * 128:nkb * 128]:
                          e.tensor_tensor(_d, _d, tri_sb[:, :], ALU.add), waits=[(tq, tq.v)])
                    em.op("v", lambda e, _s=pool[:, 0:nkb * 128], _o=stat_t[:, hp * 8 + 2:hp * 8 + 3]:
                          e.reduce_max(_o, _s, axis=AX.X), waits=[(vv, vv.v)])
                    em.op("v", lambda e, _o=stat_t[:, hp * 8 + 3:hp * 8 + 4], _i=stat_t[:, hp * 8 + 2:hp * 8 + 3]:
                          e.tensor_scalar_mul(_o, _i, -SCALE0),
                          waits=[(vv, vv.v)])
                    em.op("s", lambda e, _d=p_sb[:, 0:nkb * 128], _s=pool[:, 0:nkb * 128],
                          _b=stat_t[:, hp * 8 + 3:hp * 8 + 4], _a=stat_t[:, hp * 8 + 4:hp * 8 + 5]:
                          e.activation(_d, _s, AF.Exp, bias=_b, scale=SCALE0,
                                       accum_out=_a),
                          waits=[(vv, vv.v), (tq, tq.v)])
                    em.op("v", lambda e, _o=stat_t[:, hp * 8 + 5:hp * 8 + 6], _i=stat_t[:, hp * 8 + 4:hp * 8 + 5]:
                          e.reciprocal(_o, _i),
                          waits=[(ss, ss.v)])
                    em.op("s", lambda e, _p=p_sb[:, 0:nkb * 128], _m=stat_t[:, hp * 8 + 5:hp * 8 + 6]:
                          e.mul(_p, _p, _m),
                          waits=[(vv, vv.v)])
                    pn_s = ss.v
                    pn = vv.v
                    for kb in range(nkb):
                        em.op("t", lambda e, _d=ptp_bf[:, 1024 + kb * 128:1024 + (kb + 1) * 128],
                              _s=p_sb[:, kb * 128:(kb + 1) * 128]:
                              e.transpose(_d, _s, idn[:, :]),
                              waits=[(ss, pn_s), (vv, pn)] if kb == 0 else (),
                              inc=tq if kb == nkb - 1 else None)
                    em.op("v", lambda e, _d=pt2[:, 0:nkb * 128], _s=ptp_bf[:, 1024:1024 + nkb * 128]:
                          e.tensor_copy(_d, _s), waits=[(tq, tq.v)])
                    ptd = vv.v
                    for kb in range(nkb):
                        em.op("t", lambda e, _v=vseg[:, (par * 4 + kb) * HD + h * 128:(par * 4 + kb) * HD + (h + 1) * 128],
                              _p=pt2[:, kb * 128:(kb + 1) * 128],
                              _s=(kb == 0), _e=(kb == nkb - 1):
                              e.matmul(av_ps, _v, _p, start=_s, stop=_e),
                              waits=[(vv, ptd)] if kb == 0 else (),
                              inc=tq if kb == nkb - 1 else None)
                    em.op("s", lambda e, _d=adot[:, sub * 128:(sub + 1) * 128],
                          _o=omb_sb[:, h:h + 1]:
                          e.mul(_d, av_ps, _o),
                          waits=[(tq, tq.v)])
                    em.op("v", lambda e, _d=astg[sgi][:, sub * 128:(sub + 1) * 128],
                          _a=pool[:, 1536 + sub * 128:1536 + (sub + 1) * 128],
                          _o=adot[:, sub * 128:(sub + 1) * 128],
                          _bt=beta_sb[:, h:h + 1]:
                          e.scalar_tensor_tensor(_d, _a, _bt, _o,
                                                 ALU.mult, ALU.add),
                          waits=[(tq, attm_done), (stsem[sgi], stg_use[sgi]), (ss, ss.v)])
                stg_use[sgi] = em.dma("y", att_d.ap()[h * 128:(h + 1) * 128, tsl],
                                      astg[sgi][:, :], stsem[sgi], waits=[(vv, vv.v)])
                for sub in range(4):
                    em.op("t", lambda e, _d=skT_bf[:, 6656 + sub * 128:6656 + (sub + 1) * 128],
                          _s=sigk[:, sub * 128:(sub + 1) * 128]:
                          e.transpose(_d, _s, idn[:, :]),
                          waits=[(vv, delta_done)] if sub == 0 else (),
                          inc=tq if sub == 3 else None)
                em.op("v", lambda e: e.tensor_copy(sigkT[:, :], skT_bf[:, 6656:6656 + 512]),
                      waits=[(tq, tq.v)])
                em.wait("t", vv, vv.v)
                for sub in range(4):
                    em.op("t", lambda e, _l=sigkT[:, sub * 128:(sub + 1) * 128],
                          _r=delt[:, sub * 128:(sub + 1) * 128],
                          _s=(sub == 0), _e=(sub == 3):
                          e.matmul(md_ps, _l, _r, start=_s, stop=_e),
                          inc=tq if sub == 3 else None)
                em.op("v", lambda e, _m=mem_aug[:, h * 129:h * 129 + 128]:
                      e.tensor_tensor(_m, _m, md_ps, ALU.add), waits=[(tq, tq.v)])
                em.op("v", lambda e, _o=stat_t[:, hp * 8 + 6:hp * 8 + 7], _k=sigk[:, :]:
                      e.reduce_sum(_o, _k, axis=AX.X))
                em.op("v", lambda e, _z=mem_aug[:, h * 129 + 128:h * 129 + 129],
                      _zs=stat_t[:, hp * 8 + 6:hp * 8 + 7]:
                      e.tensor_tensor(_z, _z, _zs, ALU.add), waits=[(vv, vv.v)])
                sig_use[hp] = tq.v
            seg_use[par] = tq.v
            seg_use_v[par] = vv.v
        em.wait("y", stsem[0], stg_use[0])
        em.wait("y", stsem[1], stg_use[1])
        em.run_block()
    return nc



def build_b2():
    nc = bass.Bass("TRN2", target_bir_lowering=False)
    KC = D // 128
    T_ = 1024
    I1 = 7168
    IC = I1 // 128
    R = 64
    SCL = 16.0 / 64.0

    att_in = nc.dram_tensor("att_in", [D, T_], BF16, kind="ExternalInput")
    x2_in = nc.dram_tensor("x2_in", [D, T_], F32, kind="ExternalInput")
    wo = nc.dram_tensor("wo", [D, D], F32, kind="ExternalInput")
    conf_t = nc.dram_tensor("conf_t", [D, 8], F32, kind="ExternalInput")
    conf_b = nc.dram_tensor("conf_b", [1, 8], F32, kind="ExternalInput")
    wealth = nc.dram_tensor("wealth", [1, 8], F32, kind="ExternalInput")
    bgate = nc.dram_tensor("bgate", [D, I1], F32, kind="ExternalInput")
    bup = nc.dram_tensor("bup", [D, I1], F32, kind="ExternalInput")
    bdown = nc.dram_tensor("bdown", [I1, D], F32, kind="ExternalInput")
    gA = nc.dram_tensor("gA", [8, D, R], F32, kind="ExternalInput")
    gB = nc.dram_tensor("gB", [8, R, I1], F32, kind="ExternalInput")
    uA = nc.dram_tensor("uA", [8, D, R], F32, kind="ExternalInput")
    uB = nc.dram_tensor("uB", [8, R, I1], F32, kind="ExternalInput")
    dA = nc.dram_tensor("dA", [8, I1, R], F32, kind="ExternalInput")
    dB = nc.dram_tensor("dB", [8, R, D], F32, kind="ExternalInput")
    ones1 = nc.dram_tensor("ones1", [1, 128], BF16, kind="ExternalInput")
    ones128 = nc.dram_tensor("ones128", [128, 1], BF16, kind="ExternalInput")
    sel = nc.dram_tensor("sel", [8, 8, 128], BF16, kind="ExternalInput")
    ident32 = nc.dram_tensor("ident32", [128, 128], F32, kind="ExternalInput")
    identb = nc.dram_tensor("identb", [128, 128], BF16, kind="ExternalInput")

    out_d = nc.dram_tensor("out", [D, T_], F32, kind="ExternalOutput")
    x3_dram = nc.dram_tensor("x3_sp", [D, T_], F32, kind="Internal")
    hb_dram = nc.dram_tensor("hb_sp", [IC, 128, T_], F32, kind="Internal")

    ctx = ExitStack()
    with ctx:
        em = Em(nc, ctx)
        AR = sb(ctx, nc, "AR", [128, 94 * 1024])
        ARf = AR.bitcast(F32)
        xbf = AR[:, 0:16 * 1024]
        stream = ARf[:, 8192:8192 + 16 * 1024]
        z_g = AR[0:64, 16384:16384 + 8192]
        z_u = AR[0:64, 24576:24576 + 8192]
        base_g = AR[:, 32768:32768 + 1024]
        base_u = AR[:, 33792:33792 + 1024]
        gb_b = [AR[0:64, 34816 + i * 1024:34816 + (i + 1) * 1024] for i in range(2)]
        ub_b = [AR[0:64, 36864 + i * 1024:36864 + (i + 1) * 1024] for i in range(2)]
        dA_b = [AR[:, 38912 + i * 512:38912 + (i + 1) * 512] for i in range(2)]
        suw_b = [AR[:, 39936:39936 + 1024], AR[:, 44032:44032 + 1024]]
        hbar = ARf[:, 20480:20480 + 1024]
        s_ch = AR[:, 43008:43008 + 1024]
        g_ch = AR[:, 46080:46080 + 1024]
        u_ch = AR[:, 45056:45056 + 1024]
        att_sb = AR[:, 49152:49152 + 8192]
        y_sb = AR[0:64, 49152:49152 + 8192]
        wd = AR[:, 57344:57344 + 14336]
        wb = [AR[:, 71680 + i * 2048:71680 + (i + 1) * 2048] for i in range(4)]
        stg = [AR[:, 79872 + i * 1024:79872 + (i + 1) * 1024] for i in range(2)]
        lnrow = ARf[:, 40960:40960 + 4096][0:1, :]
        lnrow_bf = AR[:, 90112:90112 + 2048][0:1, :]
        cbc8 = AR[:, 81920:81920 + 8192]
        comb_fm = AR[0:8, 92160:92160 + 1024]
        hb_st = [AR[:, 93184 + i * 1024:93184 + (i + 1) * 1024] for i in range(2)]
        rwt = sb(ctx, nc, "rwt", [128, KC * 8])
        comb = sb(ctx, nc, "comb", [128, 8], F32)
        rstat = sb(ctx, nc, "rstat", [128, 12], F32)
        rows8 = sb(ctx, nc, "rows8", [1, 16])
        rows8f = sb(ctx, nc, "rows8f", [1, 16], F32)
        o1 = sb(ctx, nc, "o1", [1, 128])
        o128 = sb(ctx, nc, "o128", [128, 1])
        sel_sb = sb(ctx, nc, "sel_sb", [8, 8 * 128])
        id32 = sb(ctx, nc, "id32", [128, 128], F32)
        idnb = sb(ctx, nc, "idnb", [128, 128])
        sq_ch = [sb(ctx, nc, f"sq{i}", [128, T_]) for i in range(2)]
        sq2 = [sb(ctx, nc, f"sq2_{i}", [128, T_]) for i in range(2)]
        x3c = [sb(ctx, nc, f"x3c{i}", [128, T_], F32) for i in range(2)]
        pool = ps(ctx, nc, "pool", [128, 4096])
        A = pool[:, 0:1024]
        Bk = pool[:, 1024:2048]
        C = pool[:, 2048:3072]
        Dk = pool[:, 3072:4096]

        ldc = em.sem("ldc")
        ldg = em.sem("ldg")
        lda = em.sem("lda")
        ldh = [em.sem("ldh0"), em.sem("ldh1")]
        wsem = [em.sem(f"ws{i}") for i in range(4)]
        gsem = [em.sem(f"gs{i}") for i in range(2)]
        usem = [em.sem(f"us{i}") for i in range(2)]
        asem = [em.sem(f"as{i}") for i in range(2)]
        dsem = em.sem("ds")
        x3sem = [em.sem(f"x3s{i}") for i in range(2)]
        spl = em.sem("spl")
        st = em.sem("st")
        stsem = [em.sem("stg0"), em.sem("stg1")]
        tq = em.sem("tq")
        vv = em.sem("vv")
        ss = em.sem("ss")
        em.set_autochain("v", vv)
        em.set_autochain("s", ss)

        em.dma("y", stream.rearrange("p (kc t) -> p kc t", kc=KC),
               x2_in.ap().rearrange("(kc p) t -> p kc t", p=128), ldc)
        em.dma("y", rows8f[:, 0:8], conf_b[:, :], ldc)
        em.dma("y", rows8f[:, 8:16], wealth[:, :], ldc)
        em.dma("y", o1[:, :], ones1[:, :], ldc)
        em.dma("y", o128[:, :], ones128[:, :], ldc)
        em.dma("y", sel_sb[:, :].rearrange("p (e m) -> p e m", e=8),
               sel.ap().transpose([1, 0, 2]), ldc)
        em.dma("y", id32[:, :], ident32[:, :], ldc)
        v_ldc = em.dma("y", idnb[:, :], identb[:, :], ldc)
        v_rwt = em.dma("g", rwt[:, :].rearrange("p (kc m) -> p kc m", kc=KC),
                       conf_t.ap().rearrange("(kc p) m -> p kc m", p=128), ldg)
        em.wait("t", ldc, v_ldc)
        em.wait("v", ldc, v_ldc)
        em.wait("s", ldc, v_ldc)

        wuse = [0] * 4

        def wload(i, dram_slice, sem=None):
            sem = wsem[i] if sem is None else sem
            kcs = dram_slice.shape[0] // 128
            m_ = dram_slice.shape[1]
            return em.dma("g", wb[i][:, 0:kcs * m_].rearrange("p (kc m) -> p kc m", kc=kcs),
                          dram_slice.rearrange("(kc p) m -> p kc m", p=128),
                          sem, waits=[(tq, wuse[i])])

        strv = stream.rearrange("p (kc t) -> p kc t", kc=KC)
        xbv = xbf.rearrange("p (kc t) -> p kc t", kc=KC)
        attv = att_sb.rearrange("p (kc t) -> p kc t", kc=8)

        def wbv(i, kcs=16):
            return wb[i].rearrange("p (kc m) -> p kc m", kc=16)[:, 0:kcs, :]

        for kh in range(2):
            v_att = em.dma("g", attv[:, :, :],
                           att_in.ap().rearrange("(kc p) t -> p kc t", p=128)[:, kh * 8:(kh + 1) * 8, :],
                           lda, waits=[(vv, vv.v)])
            em.wait("t", lda, v_att)
            for oc in range(KC):
                wi = oc % 2
                v_w = wload(wi, wo.ap()[kh * 1024:(kh + 1) * 1024, oc * 128:(oc + 1) * 128])
                dst = A if oc % 2 == 0 else Bk
                em.wait("t", wsem[wi], v_w)
                em.wait("t", vv, vv.v)
                for hf in range(2):
                    for kc in range(8):
                        em.op("t", lambda e, _d=pool[:, (oc % 2) * 1024 + hf * 512:(oc % 2) * 1024 + (hf + 1) * 512],
                              _w=wbv(wi, 8)[:, kc, :],
                              _x=attv[:, kc, hf * 512:(hf + 1) * 512],
                              _s=(kc == 0), _p=(kc == 7):
                              e.matmul(_d, _w, _x, start=_s, stop=_p),
                              inc=tq if (kc == 7 and hf == 1) else None)
                wuse[wi] = tq.v
                em.op("v", lambda e, _d=strv[:, oc, :], _s=dst:
                      e.tensor_tensor(_d, _d, _s, ALU.add), waits=[(tq, tq.v)], inc=vv)

        def layernorm():
            em.wait("s", vv, vv.v)
            squse = [0, 0]
            for kc in range(KC):
                b2_ = kc % 2
                em.op("v", lambda e, _d=sq_ch[b2_][:, :], _s=strv[:, kc, :]:
                      e.tensor_copy(_d, _s), waits=[(tq, squse[b2_])])
                em.op("s", lambda e, _d=sq2[b2_][:, :], _s=strv[:, kc, :]:
                      e.activation(_d, _s, AF.Square, bias=0.0, scale=1.0),
                      waits=[(tq, squse[b2_])])
                for hf in range(2):
                    em.op("t", lambda e, _d=pool[0:1, 2048 + hf * 512:2048 + (hf + 1) * 512],
                          _o=o128[:, :], _x=sq_ch[b2_][:, hf * 512:(hf + 1) * 512],
                          _s=(kc == 0), _p=(kc == KC - 1):
                          e.matmul(_d, _o, _x, start=_s, stop=_p),
                          waits=[(vv, vv.v)] if hf == 0 else (),
                          inc=tq if hf == 1 else None)
                for hf in range(2):
                    em.op("t", lambda e, _d=pool[0:1, 3072 + hf * 512:3072 + (hf + 1) * 512],
                          _o=o128[:, :], _x=sq2[b2_][:, hf * 512:(hf + 1) * 512],
                          _s=(kc == 0), _p=(kc == KC - 1):
                          e.matmul(_d, _o, _x, start=_s, stop=_p),
                          waits=[(ss, ss.v)] if hf == 0 else (),
                          inc=tq if hf == 1 else None)
                squse[b2_] = tq.v
            Ar, Br = lnrow[:, 0:T_], lnrow[:, T_:2 * T_]
            Cr, Dr = lnrow[:, 2 * T_:3 * T_], lnrow[:, 3 * T_:4 * T_]
            em.op("v", lambda e: e.tensor_copy(Ar, pool[0:1, 2048:3072]), waits=[(tq, tq.v)])
            em.op("v", lambda e: e.tensor_copy(Br, pool[0:1, 3072:4096]))
            em.op("v", lambda e: e.tensor_scalar_mul(Cr, Ar, 1.0 / D))
            em.op("v", lambda e: e.tensor_scalar_mul(Ar, Br, 1.0 / D))
            em.op("v", lambda e: e.tensor_mul(Br, Cr, Cr))
            em.op("v", lambda e: e.tensor_sub(Ar, Ar, Br))
            em.op("v", lambda e: e.tensor_scalar_add(Ar, Ar, EPS))
            em.op("s", lambda e: e.activation(Br, Ar, AF.Ln, bias=0.0, scale=1.0),
                  waits=[(vv, vv.v)])
            em.op("s", lambda e: e.activation(Dr, Br, AF.Exp, bias=0.0, scale=-0.5))
            em.op("v", lambda e: e.tensor_mul(Ar, Cr, Dr), waits=[(ss, ss.v)])
            em.op("v", lambda e: e.tensor_copy(lnrow_bf[:, 0:T_], Dr))
            em.op("v", lambda e: e.tensor_copy(lnrow_bf[:, T_:2 * T_], Ar))
            for hf in range(2):
                em.op("t", lambda e, _d=C[:, hf * 512:(hf + 1) * 512], _r=lnrow_bf[:, hf * 512:(hf + 1) * 512]:
                      e.matmul(_d, o1[:, :], _r, start=True, stop=True),
                      waits=[(vv, vv.v)] if hf == 0 else (), inc=tq if hf == 1 else None)
                em.op("t", lambda e, _d=Dk[:, hf * 512:(hf + 1) * 512], _r=lnrow_bf[:, T_ + hf * 512:T_ + (hf + 1) * 512]:
                      e.matmul(_d, o1[:, :], _r, start=True, stop=True),
                      inc=tq if hf == 1 else None)
            for kc in range(KC):
                em.op("v", lambda e, _x=strv[:, kc, :]:
                      e.tensor_mul(_x, _x, C), waits=[(tq, tq.v)] if kc == 0 else ())
                em.op("v", lambda e, _x=strv[:, kc, :]:
                      e.tensor_sub(_x, _x, Dk))
                em.op("v", lambda e, _d=xbv[:, kc, :], _s=strv[:, kc, :]:
                      e.tensor_copy(_d, _s))
            em.wait("t", vv, vv.v)

        layernorm()

        v_spl = 0
        for kc in range(KC):
            v_spl = em.dma("y", x3_dram.ap().rearrange("(kc p) t -> p kc t", p=128)[:, kc, :],
                           strv[:, kc, :], spl, waits=[(vv, vv.v)] if kc == 0 else ())

        em.op("v", lambda e: e.tensor_copy(rows8[:, 0:8], rows8f[:, 0:8]))
        em.op("v", lambda e: e.tensor_copy(rows8[:, 8:16], rows8f[:, 8:16]))
        for tt in range(8):
            em.wait("t", ldg, v_rwt)
            for kc in range(KC):
                em.op("t", lambda e, _d=C[:, 0:8],
                      _x=xbv[:, kc, tt * 128:(tt + 1) * 128],
                      _w=rwt[:, :].rearrange("p (kc m) -> p kc m", kc=KC)[:, kc, :],
                      _s=(kc == 0), _p=(kc == KC - 1):
                      e.matmul(_d, _x, _w, start=_s, stop=_p),
                      inc=tq if kc == KC - 1 else None)
            em.op("t", lambda e: e.matmul(Dk[:, 0:8], o1[:, :], rows8[:, 0:8], start=True, stop=True),
                  waits=[(vv, vv.v)], inc=tq)
            em.op("t", lambda e: e.matmul(Dk[:, 512:520], o1[:, :], rows8[:, 8:16], start=True, stop=True),
                  inc=tq)
            em.op("v", lambda e: e.tensor_copy(comb[:, 0:8], C[:, 0:8]), waits=[(tq, tq.v)])
            em.op("v", lambda e: e.tensor_tensor(comb[:, 0:8], comb[:, 0:8], Dk[:, 0:8], ALU.add))
            em.op("s", lambda e: e.activation(comb[:, 0:8], comb[:, 0:8], AF.Sigmoid, bias=0.0, scale=1.0),
                  waits=[(vv, vv.v)])
            em.op("v", lambda e: e.tensor_tensor(comb[:, 0:8], comb[:, 0:8], Dk[:, 512:520], ALU.mult),
                  waits=[(ss, ss.v)])
            em.op("v", lambda e: e.reduce_max(rstat[:, 2:3], comb[:, 0:8], axis=AX.X))
            eqs = x3c[0]
            em.op("v", lambda e: e.tensor_single_scalar(eqs[:, 0:8], comb[:, 0:8], rstat[:, 2:3], ALU.is_equal))
            em.op("v", lambda e: e.scalar_tensor_tensor(eqs[:, 8:16], eqs[:, 0:8], -1.0e30,
                                                        comb[:, 0:8], ALU.mult, ALU.add))
            em.op("v", lambda e: e.reduce_max(rstat[:, 3:4], eqs[:, 8:16], axis=AX.X))
            em.op("v", lambda e: e.tensor_single_scalar(eqs[:, 8:16], comb[:, 0:8], rstat[:, 3:4], ALU.is_equal))
            em.op("v", lambda e: e.tensor_sub(rstat[:, 4:5], rstat[:, 3:4], rstat[:, 2:3]))
            em.op("s", lambda e: e.activation(rstat[:, 5:6], rstat[:, 4:5], AF.Exp, bias=0.0, scale=1.0),
                  waits=[(vv, vv.v)])
            em.op("v", lambda e: e.tensor_scalar_add(rstat[:, 6:7], rstat[:, 5:6], 1.0),
                  waits=[(ss, ss.v)])
            em.op("v", lambda e: e.reciprocal(rstat[:, 7:8], rstat[:, 6:7]))
            em.op("v", lambda e: e.tensor_mul(rstat[:, 8:9], rstat[:, 5:6], rstat[:, 7:8]))
            em.op("v", lambda e: e.tensor_single_scalar(eqs[:, 0:8], eqs[:, 0:8], rstat[:, 7:8], ALU.mult))
            em.op("v", lambda e: e.scalar_tensor_tensor(comb[:, 0:8], eqs[:, 8:16], rstat[:, 8:9],
                                                        eqs[:, 0:8], ALU.mult, ALU.add))
            em.op("t", lambda e, _d=Dk[0:8, 0:128], _s=comb[:, 0:8]:
                  e.transpose(_d, _s, id32[:, :]), waits=[(vv, vv.v)], inc=tq)
            em.op("v", lambda e, _d=comb_fm[:, tt * 128:(tt + 1) * 128], _s=Dk[0:8, 0:128]:
                  e.tensor_copy(_d, _s), waits=[(tq, tq.v)])
        comb_done = vv.v
        for e_ in range(8):
            for hf in range(2):
                em.op("t", lambda e, _d=C[:, hf * 512:(hf + 1) * 512],
                      _s=sel_sb[:, :].rearrange("p (e m) -> p e m", e=8)[:, e_, :],
                      _r=comb_fm[:, hf * 512:(hf + 1) * 512]:
                      e.matmul(_d, _s, _r, start=True, stop=True),
                      waits=[(vv, vv.v)] if hf == 0 else (), inc=tq if hf == 1 else None)
            em.op("v", lambda e, _d=cbc8[:, e_ * 1024:(e_ + 1) * 1024]:
                  e.tensor_copy(_d, C), waits=[(tq, tq.v)])

        em.wait("t", spl, v_spl)
        em.wait("v", spl, v_spl)
        em.wait("g", spl, v_spl)
        for e_ in range(8):
            for which, zdst, wAd in (("g", z_g, gA), ("u", z_u, uA)):
                wi = e_ % 2
                v_w = wload(wi, wAd.ap()[e_, :, :])
                em.wait("t", wsem[wi], v_w)
                em.wait("t", vv, vv.v)
                for hf in range(2):
                    for kc in range(KC):
                        em.op("t", lambda e, _d=C[0:64, hf * 512:(hf + 1) * 512],
                              _w=wb[wi][:, 0:16 * 64].rearrange("p (kc m) -> p kc m", kc=16)[:, kc, :],
                              _x=xbv[:, kc, hf * 512:(hf + 1) * 512],
                              _s=(kc == 0), _p=(kc == KC - 1):
                              e.matmul(_d, _w, _x, start=_s, stop=_p),
                              inc=tq if (kc == KC - 1 and hf == 1) else None)
                wuse[wi] = tq.v
                em.op("v", lambda e, _d=zdst[:, e_ * 1024:(e_ + 1) * 1024], _s=C[0:64, :]:
                      e.tensor_copy(_d, _s), waits=[(tq, tq.v)])
        em.op("v", lambda e: e.tensor_scalar_mul(z_g[:, :], z_g[:, :], SCL))
        em.op("v", lambda e: e.tensor_scalar_mul(z_u[:, :], z_u[:, :], SCL))
        z_done = vv.v

        em.op("v", lambda e: e.memset(y_sb[:, :], 0.0), waits=[(lda, 32), (tq, tq.v)])
        hb_dma = None
        lb_use = [0, 0]
        for i in range(IC):
            par = i % 2
            v_bg = wload(par, bgate.ap()[:, i * 128:(i + 1) * 128])
            v_bu = wload(2 + par, bup.ap()[:, i * 128:(i + 1) * 128], sem=wsem[2 + par])
            v_gb = em.dma("g", gb_b[par].rearrange("q (e m) -> q e m", e=8),
                          gB.ap()[:, :, i * 128:(i + 1) * 128].transpose([1, 0, 2]),
                          gsem[par], waits=[(tq, lb_use[par])])
            v_ub = em.dma("g", ub_b[par].rearrange("q (e m) -> q e m", e=8),
                          uB.ap()[:, :, i * 128:(i + 1) * 128].transpose([1, 0, 2]),
                          usem[par], waits=[(tq, lb_use[par])])
            v_da = em.dma("g", dA_b[par].rearrange("p (e m) -> p e m", e=8),
                          dA.ap()[:, i * 128:(i + 1) * 128, :].transpose([1, 0, 2]),
                          asem[par], waits=[(tq, lb_use[par])])
            em.wait("t", wsem[par], v_bg)
            em.wait("t", wsem[2 + par], v_bu)
            em.wait("t", gsem[par], v_gb)
            em.wait("t", usem[par], v_ub)
            em.wait("t", asem[par], v_da)
            em.wait("t", vv, vv.v)
            for hf in range(2):
                for kc in range(KC):
                    em.op("t", lambda e, _d=A[:, hf * 512:(hf + 1) * 512],
                          _w=wbv(par)[:, kc, :], _x=xbv[:, kc, hf * 512:(hf + 1) * 512],
                          _s=(kc == 0), _p=(kc == KC - 1):
                          e.matmul(_d, _w, _x, start=_s, stop=_p),
                          inc=tq if kc == KC - 1 else None)
            wuse[par] = tq.v
            bg_done = tq.v
            for hf in range(2):
                for kc in range(KC):
                    em.op("t", lambda e, _d=Bk[:, hf * 512:(hf + 1) * 512],
                          _w=wbv(2 + par)[:, kc, :], _x=xbv[:, kc, hf * 512:(hf + 1) * 512],
                          _s=(kc == 0), _p=(kc == KC - 1):
                          e.matmul(_d, _w, _x, start=_s, stop=_p),
                          inc=tq if kc == KC - 1 else None)
            wuse[2 + par] = tq.v
            em.op("v", lambda e: e.tensor_copy(base_g, A), waits=[(tq, bg_done)])
            em.op("v", lambda e: e.tensor_copy(base_u, Bk), waits=[(tq, tq.v)])
            base_done = vv.v
            silu_prev = 0
            su_prev = base_done
            s_free = 0
            pend_y = None
            for e_ in range(8):
                suw = suw_b[e_ % 2]
                em.wait("t", ss, silu_prev)
                em.wait("t", vv, su_prev)
                for hf in range(2):
                    em.op("t", lambda e, _d=C[:, hf * 512:(hf + 1) * 512],
                          _w=gb_b[par].rearrange("q (e m) -> q e m", e=8)[:, e_, :],
                          _z=z_g[:, e_ * 1024 + hf * 512:e_ * 1024 + (hf + 1) * 512]:
                          e.matmul(_d, _w, _z, start=True, stop=False))
                    em.op("t", lambda e, _d=C[:, hf * 512:(hf + 1) * 512],
                          _r=base_g[:, hf * 512:(hf + 1) * 512]:
                          e.matmul(_d, idnb[:, :], _r, start=False, stop=True),
                          inc=tq if hf == 1 else None)
                g_done_ = tq.v
                for hf in range(2):
                    em.op("t", lambda e, _d=Dk[:, hf * 512:(hf + 1) * 512],
                          _w=ub_b[par].rearrange("q (e m) -> q e m", e=8)[:, e_, :],
                          _z=z_u[:, e_ * 1024 + hf * 512:e_ * 1024 + (hf + 1) * 512]:
                          e.matmul(_d, _w, _z, start=True, stop=False))
                    em.op("t", lambda e, _d=Dk[:, hf * 512:(hf + 1) * 512],
                          _r=base_u[:, hf * 512:(hf + 1) * 512]:
                          e.matmul(_d, idnb[:, :], _r, start=False, stop=True),
                          inc=tq if hf == 1 else None)
                u_done_ = tq.v
                if pend_y is not None:
                    pe, pdone = pend_y
                    for hf in range(2):
                        em.op("t", lambda e, _d=A[0:64, hf * 512:(hf + 1) * 512],
                              _w=dA_b[par].rearrange("p (e m) -> p e m", e=8)[:, pe, :],
                              _r=suw_b[pe % 2][:, hf * 512:(hf + 1) * 512]:
                              e.matmul(_d, _w, _r, start=True, stop=True),
                              waits=[(vv, pdone)] if hf == 0 else (),
                              inc=tq if hf == 1 else None)
                    em.op("v", lambda e, _d=y_sb[:, pe * 1024:(pe + 1) * 1024]:
                          e.tensor_tensor(_d, _d, A[0:64, :], ALU.add), waits=[(tq, tq.v)])
                em.op("s", lambda e: e.copy(g_ch[:, :], C), waits=[(tq, g_done_)])
                silu_prev = ss.v
                em.op("s", lambda e: e.activation(s_ch[:, :], g_ch[:, :], AF.Silu, bias=0.0, scale=1.0),
                      waits=[(vv, s_free)])
                em.op("v", lambda e: e.tensor_copy(u_ch[:, :], Dk), waits=[(tq, u_done_)])
                su_prev = vv.v
                em.op("v", lambda e: e.tensor_tensor(u_ch[:, :], s_ch[:, :], u_ch[:, :], ALU.mult),
                      waits=[(ss, ss.v)])
                s_free = vv.v
                em.op("v", lambda e, _c=cbc8[:, e_ * 1024:(e_ + 1) * 1024], _sw=suw:
                      e.tensor_tensor(_sw[:, :], u_ch[:, :], _c, ALU.mult))
                suw_done = vv.v
                if e_ == 0:
                    em.op("v", lambda e, _sw=suw: e.tensor_copy(hbar, _sw[:, :]),
                          waits=[hb_dma] if hb_dma else ())
                else:
                    em.op("v", lambda e, _sw=suw: e.tensor_tensor(hbar, hbar, _sw[:, :], ALU.add))
                pend_y = (e_, suw_done)
            pe, pdone = pend_y
            for hf in range(2):
                em.op("t", lambda e, _d=A[0:64, hf * 512:(hf + 1) * 512],
                      _w=dA_b[par].rearrange("p (e m) -> p e m", e=8)[:, pe, :],
                      _r=suw_b[pe % 2][:, hf * 512:(hf + 1) * 512]:
                      e.matmul(_d, _w, _r, start=True, stop=True),
                      waits=[(vv, pdone)] if hf == 0 else (),
                      inc=tq if hf == 1 else None)
            em.op("v", lambda e, _d=y_sb[:, pe * 1024:(pe + 1) * 1024]:
                  e.tensor_tensor(_d, _d, A[0:64, :], ALU.add), waits=[(tq, tq.v)])
            lb_use[par] = tq.v
            hb_dma = (ldh[par], em.dma("y", hb_dram.ap()[i, :, :], hbar, ldh[par],
                                       waits=[(vv, vv.v)]))
        em.op("v", lambda e: e.tensor_scalar_mul(y_sb[:, :], y_sb[:, :], SCL),
              waits=[(tq, tq.v)])
        y_scaled = vv.v
        em.wait("g", ldh[0], ldh[0].v)
        em.wait("g", ldh[1], ldh[1].v)

        hb2 = [AR[:, 81920 + i * 2048:81920 + (i + 1) * 2048] for i in range(2)]
        ldh6 = [em.sem("ldh6_0"), em.sem("ldh6_1")]
        dB6 = AR[0:64, 86016:86016 + 4096]
        x3use = [vv.v, vv.v]
        hb_use = [0, 0]
        wd_use = [0, 0]
        db_use = 0
        ldh_fin = [ldh[0].v, ldh[1].v]
        em.wait("v", ldh[0], ldh_fin[0])
        em.wait("v", ldh[1], ldh_fin[1])
        em.wait("g", ldh[0], ldh_fin[0])
        em.wait("g", ldh[1], ldh_fin[1])
        for ocq in range(4):
            v_db = em.dma("g", dB6.rearrange("q (e m) -> q e m", e=8),
                          dB.ap()[:, :, ocq * 512:(ocq + 1) * 512].transpose([1, 0, 2]),
                          wsem[3], waits=[(tq, db_use)])
            em.wait("t", vv, vv.v)
            for step in range(28):
                i = step * 2
                par = step % 2
                v_hb = em.dma("g", hb2[par].rearrange("p (a t) -> p a t", a=2),
                              hb_dram.ap()[i:i + 2, :, :].transpose([1, 0, 2]),
                              ldh6[par], waits=[(tq, hb_use[par])])
                v_wd = em.dma("g", wb[par].rearrange("p (a m) -> p a m", a=4)[:, 0:2, :],
                              bdown.ap()[i * 128:(i + 2) * 128, ocq * 512:(ocq + 1) * 512]
                              .rearrange("(a p) m -> p a m", p=128),
                              wsem[par], waits=[(tq, wd_use[par])])
                em.wait("t", ldh6[par], v_hb)
                em.wait("t", wsem[par], v_wd)
                for ii in range(2):
                    for oc2 in range(4):
                        for hf in range(2):
                            gidx = oc2 * 2 + hf
                            em.op("t", lambda e, _d=pool[:, gidx * 512:(gidx + 1) * 512],
                                  _w=wb[par].rearrange("p (a m) -> p a m", a=4)[:, ii, oc2 * 128:(oc2 + 1) * 128],
                                  _h=hb2[par][:, ii * 1024 + hf * 512:ii * 1024 + (hf + 1) * 512],
                                  _s=(step == 0 and ii == 0):
                                  e.matmul(_d, _w, _h, start=_s, stop=False),
                                  inc=tq if (ii == 1 and oc2 == 3 and hf == 1) else None)
                hb_use[par] = tq.v
                wd_use[par] = tq.v
            em.wait("t", wsem[3], v_db)
            for oc2 in range(4):
                for hf in range(2):
                    gidx = oc2 * 2 + hf
                    for e_ in range(8):
                        em.op("t", lambda e, _d=pool[:, gidx * 512:(gidx + 1) * 512],
                              _w=dB6[0:64, e_ * 512 + oc2 * 128:e_ * 512 + (oc2 + 1) * 128],
                              _y=y_sb[:, e_ * 1024 + hf * 512:e_ * 1024 + (hf + 1) * 512],
                              _p=(e_ == 7):
                              e.matmul(_d, _w, _y, start=False, stop=_p),
                              inc=tq if e_ == 7 else None)
            db_use = tq.v
            mm_done = tq.v
            for oc2 in range(4):
                oc = ocq * 4 + oc2
                x3b = x3c[oc % 2]
                v_x3 = em.dma("y", x3b[:, :],
                              x3_dram.ap().rearrange("(kc p) t -> p kc t", p=128)[:, oc, :],
                              x3sem[oc % 2], waits=[(vv, x3use[oc % 2]), (spl, v_spl)])
                em.wait("v", x3sem[oc % 2], v_x3)
                for hf in range(2):
                    gidx = oc2 * 2 + hf
                    em.op("v", lambda e, _d=strv[:, oc, hf * 512:(hf + 1) * 512],
                          _p=pool[:, gidx * 512:(gidx + 1) * 512],
                          _x=x3b[:, hf * 512:(hf + 1) * 512]:
                          e.tensor_tensor(_d, _p, _x, ALU.add), waits=[(tq, mm_done)])
                x3use[oc % 2] = vv.v

        layernorm()
        fin = vv.v
        for kc in range(KC):
            em.op("y", lambda e, _d=out_d.ap().rearrange("(kc p) t -> p kc t", p=128)[:, kc, :],
                  _s=strv[:, kc, :]:
                  e.dma_start(out=_d, in_=_s), waits=[(vv, fin)], inc=st, inc_n=16)
        em.wait("y", st, st.v)
        em.run_block()
    return nc
from concourse.bass_utils import run_bass_kernel_spmd

_NCACHE = {}


def _get_nc(name, builder):
    if name not in _NCACHE:
        _NCACHE[name] = builder()
    return _NCACHE[name]


def _own_cols(c):
    a, b_ = stripes_of_core(c)
    return np.r_[a * 512:(a + 1) * 512, b_ * 512:(b_ + 1) * 512]


def kernel(**inputs):
    cores = list(range(8))

    nc1 = _get_nc("a1", build_a1)
    r1 = run_bass_kernel_spmd(nc1, a1_in_maps(inputs), core_ids=cores)
    att_full = np.stack([
        np.concatenate([r1.results[b * 4 + g]["att"].reshape(512, S)
                        for g in range(4)], axis=0)
        for b in range(B)])

    nc2 = _get_nc("a2", build_a2)
    r2 = run_bass_kernel_spmd(nc2, a2_in_maps(inputs, att_full), core_ids=cores)

    qf = np.zeros((B, D, S), BF)
    kf = np.zeros((B, D, S), BF)
    vt = np.zeros((B, S, D), BF)
    x2 = [None] * 8
    for c in range(8):
        b = c // 4
        cols = _own_cols(c)
        qf[b][:, cols] = r2.results[c]["q1_out"]
        kf[b][:, cols] = r2.results[c]["k1_out"]
        vt[b][cols, :] = r2.results[c]["v1_out"].T
        x2[c] = r2.results[c]["x2_out"]

    nc3 = _get_nc("b1", build_b1)
    tri = host_masks()[0][:, 0:128].copy()
    betasT = np.ascontiguousarray(np.asarray(inputs["L1_attn_betas"])[0, :, 0, :].T)
    maps3 = []
    for c in range(8):
        b, hq = c // 4, c % 4
        hsl = slice(hq * 512, (hq + 1) * 512)
        maps3.append({"qf": np.ascontiguousarray(qf[b][hsl, :]),
                      "kf": np.ascontiguousarray(kf[b][hsl, :]),
                      "vt": np.ascontiguousarray(vt[b][:, hsl]),
                      "betasT": np.ascontiguousarray(betasT[:, hq * 4:(hq + 1) * 4]),
                      "tri": tri, "ident": ident_np(),
                      "ones1b": np.ones((1, 128), BF)})
    r3 = run_bass_kernel_spmd(nc3, maps3, core_ids=cores)
    T = np.arange(S)
    seg_i, t_i = T // 512, T % 512
    a_i, b_i = t_i // 32, t_i % 32
    f = np.arange(D)
    j_i, v_i = f // 128, f % 128
    _rows = a_i[None, :] * 128 + v_i[:, None]
    _cols = seg_i[None, :] * 512 + b_i[None, :] * 16 + j_i[:, None]
    att1 = []
    for b in range(B):
        full = np.concatenate([r3.results[b * 4 + j]["att"] for j in range(4)], axis=0)
        att1.append(np.ascontiguousarray(full[_rows, _cols]))

    nc4 = _get_nc("b2", build_b2)
    common4 = {
        "wo": np.asarray(inputs["L1_attn_Wo"]),
        "conf_t": np.ascontiguousarray(np.asarray(inputs["L1_ffn_conf_W"]).T),
        "conf_b": np.asarray(inputs["L1_ffn_conf_b"]).reshape(1, 8),
        "wealth": np.asarray(inputs["L1_ffn_expert_wealth"]).reshape(1, 8),
        "bgate": np.asarray(inputs["L1_ffn_base_gate"]),
        "bup": np.asarray(inputs["L1_ffn_base_up"]),
        "bdown": np.asarray(inputs["L1_ffn_base_down"]),
        "gA": np.asarray(inputs["L1_ffn_gate_A"]), "gB": np.asarray(inputs["L1_ffn_gate_B"]),
        "uA": np.asarray(inputs["L1_ffn_up_A"]), "uB": np.asarray(inputs["L1_ffn_up_B"]),
        "dA": np.asarray(inputs["L1_ffn_down_A"]), "dB": np.asarray(inputs["L1_ffn_down_B"]),
        "ones1": np.ones((1, 128), BF), "ones128": np.ones((128, 1), BF),
        "sel": np.ascontiguousarray(np.eye(8, dtype=BF)[:, :, None] * np.ones((1, 1, 128), BF)),
        "ident32": np.eye(128, dtype=np.float32),
        "identb": np.eye(128, dtype=BF),
    }
    maps4 = []
    for c in range(8):
        b = c // 4
        cols = _own_cols(c)
        m = dict(common4)
        m["att_in"] = np.ascontiguousarray(att1[b][:, cols])
        m["x2_in"] = x2[c]
        maps4.append(m)
    r4 = run_bass_kernel_spmd(nc4, maps4, core_ids=cores)

    out = np.zeros((B, S, D), np.float32)
    for c in range(8):
        b = c // 4
        cols = _own_cols(c)
        out[b][cols, :] = r4.results[c]["out"].T
    return out


# version 09 (final)
# speedup vs baseline: 1.6413x; 1.2499x over previous
import math
import numpy as np
import ml_dtypes
from contextlib import ExitStack

import concourse.bass as bass
import concourse.mybir as mybir

F32 = mybir.dt.float32
BF16 = mybir.dt.bfloat16
AF = mybir.ActivationFunctionType
ALU = mybir.AluOpType
AX = mybir.AxisListType

BF = ml_dtypes.bfloat16

D = 2048
S = 4096
B = 2
SCALE0 = 1.0 / math.sqrt(128.0)
NEG = -3.0e38


class Sem:
    __slots__ = ("h", "v")

    def __init__(self, ctx, nc, name):
        self.h = ctx.enter_context(nc.semaphore(name))
        self.v = 0


class Em:

    ENGS = ("t", "v", "s", "g", "y")

    def __init__(self, nc, ctx):
        self.nc = nc
        self.ctx = ctx
        self.q = {k: [] for k in self.ENGS}
        self._sems = {}

    def sem(self, name) -> Sem:
        if name not in self._sems:
            self._sems[name] = Sem(self.ctx, self.nc, name)
        return self._sems[name]

    def set_autochain(self, eng, sem):
        if not hasattr(self, "_auto"):
            self._auto = {}
        self._auto[eng] = sem

    def op(self, eng, fn, waits=(), inc=None, inc_n=1):
        ac = getattr(self, "_auto", {}).get(eng)
        if ac is not None:
            waits = list(waits) + [(ac, ac.v)]
            if inc is None:
                inc = ac
        ret = None
        if inc is not None:
            inc.v += inc_n
            ret = inc.v
        wl = tuple((s.h, v) for (s, v) in waits if v > 0)

        def run(e, _fn=fn, _wl=wl, _inc=inc, _n=inc_n):
            for h, v in _wl:
                e.wait_ge(h, v)
            inst = _fn(e)
            if _inc is not None:
                inst.then_inc(_inc.h, _n)
        self.q[eng].append(run)
        return ret

    def wait(self, eng, sem, val):
        if val > 0:
            self.q[eng].append(lambda e, _h=sem.h, _v=val: e.wait_ge(_h, _v))

    def dma(self, eng, out_ap, in_ap, sem, waits=()):
        return self.op(eng, lambda e, _o=out_ap, _i=in_ap: e.dma_start(out=_o, in_=_i),
                       waits=waits, inc=sem, inc_n=16)

    def run_block(self):
        block = self.ctx.enter_context(self.nc.Block())

        def mk(lst):
            def body(eng):
                for fn in lst:
                    fn(eng)
            return body
        block.tensor(mk(self.q["t"]))
        block.vector(mk(self.q["v"]))
        block.scalar(mk(self.q["s"]))
        block.gpsimd(mk(self.q["g"]))
        block.sync(mk(self.q["y"]))


def sb(ctx, nc, name, shape, dt=BF16):
    return ctx.enter_context(nc.sbuf_tensor(name, shape, dt))


def ps(ctx, nc, name, shape, dt=F32):
    return ctx.enter_context(nc.psum_tensor(name, shape, dt))



def host_masks():
    qi = np.arange(128)[:, None]
    kj = np.arange(512)[None, :]
    out = np.zeros((4, 128, 512), np.float32)
    for u in range(4):
        out[u] = np.where(kj <= u * 128 + qi, 0.0, NEG)
    return np.ascontiguousarray(out)


def ident_np():
    return np.eye(128, dtype=BF)


def stripes_of_core(c):
    j = c % 4
    return (j, 7 - j)



def build_a1():
    nc = bass.Bass("TRN2", target_bir_lowering=False)
    xb = nc.dram_tensor("xb", [D, S], F32, kind="ExternalInput")
    wq_t = nc.dram_tensor("wq_t", [D, 512], F32, kind="ExternalInput")
    wk_t = nc.dram_tensor("wk_t", [D, 128], F32, kind="ExternalInput")
    wv_t = nc.dram_tensor("wv_t", [D, 128], F32, kind="ExternalInput")
    masks = nc.dram_tensor("masks", [4, 128, 512], F32, kind="ExternalInput")
    ident = nc.dram_tensor("ident", [128, 128], BF16, kind="ExternalInput")
    att = nc.dram_tensor("att", [4, 128, S], BF16, kind="ExternalOutput")

    KC = D // 128

    ctx = ExitStack()
    with ctx:
        em = Em(nc, ctx)
        wq = sb(ctx, nc, "wq", [128, KC * 512])
        wk = sb(ctx, nc, "wk", [128, KC * 128])
        wv = sb(ctx, nc, "wv", [128, KC * 128])
        xs = [sb(ctx, nc, f"xs{i}", [128, KC * 512]) for i in range(2)]
        qf = [sb(ctx, nc, f"qf{h}", [128, S]) for h in range(4)]
        kf = sb(ctx, nc, "kf", [128, S])
        vt = sb(ctx, nc, "vt", [128, 32 * 128])
        msk = sb(ctx, nc, "msk", [128, 4 * 512], F32)
        idn = sb(ctx, nc, "idn", [128, 128])
        p_sb = [sb(ctx, nc, f"p{h}", [128, 8 * 512]) for h in range(4)]
        pt_sb = sb(ctx, nc, "pt", [128, 512])
        att_sb = [sb(ctx, nc, f"atts{h}", [128, S]) for h in range(4)]
        stat = sb(ctx, nc, "stat", [128, 16], F32)
        sc_ps = ps(ctx, nc, "sc_ps", [128, 3072])
        pt_ps = ps(ctx, nc, "pt_ps", [128, 512], BF16)
        av_ps = ps(ctx, nc, "av_ps", [128, 512])

        ldw = em.sem("ldw")
        ldx = [em.sem("ldx0"), em.sem("ldx1")]
        st = em.sem("st")
        tp = em.sem("tp")
        ta = em.sem("ta")
        vv = em.sem("vv")
        ex = em.sem("ex")

        wqv = wq[:, :].rearrange("p (kc m) -> p kc m", kc=KC)
        wkv = wk[:, :].rearrange("p (kc m) -> p kc m", kc=KC)
        wvv = wv[:, :].rearrange("p (kc m) -> p kc m", kc=KC)

        em.dma("g", wqv, wq_t.ap().rearrange("(kc p) m -> p kc m", p=128), ldw)
        em.dma("g", wkv, wk_t.ap().rearrange("(kc p) m -> p kc m", p=128), ldw)
        em.dma("g", wvv, wv_t.ap().rearrange("(kc p) m -> p kc m", p=128), ldw)
        em.dma("g", msk[:, :].rearrange("p (u m) -> p u m", u=4),
               masks.ap().transpose([1, 0, 2]), ldw)
        v_allw = em.dma("g", idn[:, :], ident[:, :], ldw)

        xb_v = xb.ap().rearrange("(kc p) t -> p kc t", p=128)
        x_use = [0, 0]
        for tc in range(8):
            buf = tc % 2
            tsl = slice(tc * 512, (tc + 1) * 512)
            v_x = em.dma("g", xs[buf][:, :].rearrange("p (kc t) -> p kc t", kc=KC),
                         xb_v[:, :, tsl], ldx[buf],
                         waits=[(tp, x_use[buf])])
            xr = xs[buf][:, :].rearrange("p (kc t) -> p kc t", kc=KC)
            em.wait("t", vv, 9 * tc)
            em.wait("t", ldw, v_allw)
            em.wait("t", ldx[buf], v_x)
            for oc in range(5):
                dst = sc_ps[:, oc * 512:(oc + 1) * 512]
                for kc in range(KC):
                    w_ap = wqv[:, kc, oc * 128:(oc + 1) * 128] if oc < 4 else wkv[:, kc, :]
                    em.op("t", lambda e, _d=dst, _w=w_ap, _x=xr[:, kc, :],
                          _s=(kc == 0), _p=(kc == KC - 1):
                          e.matmul(_d, _w, _x, start=_s, stop=_p),
                          inc=tp if kc == KC - 1 else None)
            xr4 = xr.rearrange("p kc (a q) -> p kc a q", a=4)
            for stt in range(4):
                dst = av_ps[:, stt * 128:(stt + 1) * 128]
                for kc in range(KC):
                    em.op("t", lambda e, _d=dst, _x=xr4[:, kc, stt, :], _w=wvv[:, kc, :],
                          _s=(kc == 0), _p=(kc == KC - 1):
                          e.matmul(_d, _x, _w, start=_s, stop=_p),
                          inc=tp if kc == KC - 1 else None)
            x_use[buf] = tp.v
            base = 9 * tc
            for oc in range(4):
                em.op("v", lambda e, _d=qf[oc][:, tsl], _s=sc_ps[:, oc * 512:(oc + 1) * 512]:
                      e.tensor_copy(_d, _s), waits=[(tp, base + oc + 1)], inc=vv)
            em.op("v", lambda e, _d=kf[:, tsl], _s=sc_ps[:, 4 * 512:5 * 512]:
                  e.tensor_copy(_d, _s), waits=[(tp, base + 5)], inc=vv)
            for stt in range(4):
                dcol = (tc * 4 + stt) * 128
                em.op("v", lambda e, _d=vt[:, dcol:dcol + 128], _s=av_ps[:, stt * 128:(stt + 1) * 128]:
                      e.tensor_copy(_d, _s), waits=[(tp, base + 9)], inc=vv)
        proj_done = vv.v

        def stc(i):
            return stat[:, i:i + 1]

        em.wait("t", vv, proj_done)
        em.wait("v", ldw, v_allw)
        for t in range(32):
            s_stripe, u = t // 4, t % 4
            nkc = s_stripe + 1
            passes = [(0, min(nkc, 6))]
            if nkc > 6:
                passes.append((6, nkc))
            em.wait("s", ta, ta.v)
            em.wait("v", ta, ta.v)
            for h in range(4):
                q_ap = qf[h][:, t * 128:(t + 1) * 128]
                em.wait("v", ex, ex.v)
                for pi, (c0, c1) in enumerate(passes):
                    em.wait("t", ex, ex.v)
                    for c in range(c0, c1):
                        em.op("t", lambda e, _d=sc_ps[:, (c - c0) * 512:(c - c0 + 1) * 512],
                              _q=q_ap, _k=kf[:, c * 512:(c + 1) * 512]:
                              e.matmul(_d, _q, _k, start=True, stop=True),
                              inc=tp if c == c1 - 1 else None)
                    plen = (c1 - c0) * 512
                    wl = [(tp, tp.v)]
                    if pi == len(passes) - 1:
                        moff = (nkc - 1 - c0) * 512
                        em.op("v", lambda e, _d=sc_ps[:, moff:moff + 512], _m=msk[:, u * 512:(u + 1) * 512]:
                              e.tensor_tensor(_d, _d, _m, ALU.add), waits=wl, inc=vv)
                        wl = [(vv, vv.v)]
                    mcol = 0 if pi == 0 else 1
                    em.op("v", lambda e, _d=stc(mcol), _s=sc_ps[:, 0:plen]:
                          e.reduce_max(_d, _s, axis=AX.X), waits=wl, inc=vv)
                    em.wait("v", vv, vv.v)
                    if pi == 0:
                        em.op("v", lambda e: e.tensor_scalar_mul(stc(3), stc(0), -SCALE0), inc=vv)
                    else:
                        em.op("v", lambda e: e.tensor_tensor(stc(2), stc(0), stc(1), ALU.max), inc=vv)
                        em.wait("v", vv, vv.v)
                        em.op("v", lambda e: e.tensor_scalar_mul(stc(3), stc(2), -SCALE0), inc=vv)
                    vb = vv.v
                    scol = 4 if pi == 0 else 5
                    em.op("s", lambda e, _d=p_sb[h][:, c0 * 512:c0 * 512 + plen], _s=sc_ps[:, 0:plen],
                          _b=stc(3), _a=stc(scol):
                          e.activation(_d, _s, AF.Exp, bias=_b, scale=SCALE0, accum_out=_a),
                          waits=[(vv, vb)], inc=ex)
                if len(passes) == 1:
                    em.op("v", lambda e: e.reciprocal(stc(7), stc(4)), waits=[(ex, ex.v)], inc=vv)
                    em.wait("v", vv, vv.v)
                    tot = nkc * 512
                    em.op("v", lambda e, _p=p_sb[h][:, 0:tot]:
                          e.tensor_scalar_mul(_p, _p, stc(7)), inc=vv)
                else:
                    em.op("v", lambda e: e.tensor_tensor(stc(8), stc(0), stc(2), ALU.subtract),
                          waits=[(ex, ex.v)], inc=vv)
                    em.op("s", lambda e: e.activation(stc(9), stc(8), AF.Exp, bias=0.0, scale=SCALE0),
                          waits=[(vv, vv.v)], inc=ex)
                    em.op("v", lambda e: e.scalar_tensor_tensor(stc(6), stc(4), stc(9), stc(5),
                                                                ALU.mult, ALU.add),
                          waits=[(ex, ex.v)], inc=vv)
                    em.wait("v", vv, vv.v)
                    em.op("v", lambda e: e.reciprocal(stc(7), stc(6)), inc=vv)
                    em.wait("v", vv, vv.v)
                    em.op("v", lambda e: e.tensor_mul(stc(10), stc(9), stc(7)), inc=vv)
                    em.wait("v", vv, vv.v)
                    em.op("v", lambda e, _p=p_sb[h][:, 0:3072]:
                          e.tensor_scalar_mul(_p, _p, stc(10)), inc=vv)
                    tot2 = (nkc - 6) * 512
                    em.op("v", lambda e, _p=p_sb[h][:, 3072:3072 + tot2]:
                          e.tensor_scalar_mul(_p, _p, stc(7)), inc=vv)
            p_done = vv.v
            nkb = nkc * 4
            em.wait("t", vv, p_done)
            for kb in range(nkb):
                for h in range(4):
                    em.op("t", lambda e, _d=pt_ps[:, h * 128:(h + 1) * 128],
                          _s=p_sb[h][:, kb * 128:(kb + 1) * 128]:
                          e.transpose(_d, _s, idn[:, :]),
                          waits=[(vv, vv.v)] if h == 0 else (),
                          inc=ta if h == 3 else None)
                em.op("v", lambda e, _d=pt_sb[:, :], _s=pt_ps[:, :]:
                      e.tensor_copy(_d, _s), waits=[(ta, ta.v)], inc=vv)
                em.op("t", lambda e, _v=vt[:, kb * 128:(kb + 1) * 128], _p=pt_sb[:, :],
                      _s=(kb == 0), _e=(kb == nkb - 1):
                      e.matmul(av_ps[:, :], _v, _p, start=_s, stop=_e),
                      waits=[(vv, vv.v)], inc=ta if kb == nkb - 1 else None)
            for h in range(4):
                em.op("v", lambda e, _d=att_sb[h][:, t * 128:(t + 1) * 128],
                      _s=av_ps[:, h * 128:(h + 1) * 128]:
                      e.tensor_copy(_d, _s), waits=[(ta, ta.v)] if h == 0 else (), inc=vv)
        fin = vv.v
        for h in range(4):
            em.op("y", lambda e, _d=att[h, :, :], _s=att_sb[h][:, :]:
                  e.dma_start(out=_d, in_=_s), waits=[(vv, fin)], inc=st, inc_n=16)
        em.wait("y", st, st.v)
        em.run_block()
    return nc


def a1_in_maps(inputs):
    x = np.asarray(inputs["L0_attn_x"])
    wqkv = np.asarray(inputs["L0_attn_weight_qkv"])
    masks = host_masks()
    idn = ident_np()
    maps = []
    for c in range(8):
        b, g = c // 4, c % 4
        xb = np.ascontiguousarray(x[b].T)
        base = g * 768
        wq_t = np.ascontiguousarray(wqkv[base:base + 512].T)
        wk_t = np.ascontiguousarray(wqkv[base + 512:base + 640].T)
        wv_t = np.ascontiguousarray(wqkv[base + 640:base + 768].T)
        maps.append({"xb": xb, "wq_t": wq_t, "wk_t": wk_t, "wv_t": wv_t,
                     "masks": masks, "ident": idn})
    return maps



EPS = 1e-5


def build_a2():
    nc = bass.Bass("TRN2", target_bir_lowering=False)
    KC = D // 128
    T_ = 1024
    IE = 1408
    ICH = IE // 128
    ISH = 2816
    ISCH = ISH // 128

    att_in = nc.dram_tensor("att_in", [D, T_], BF16, kind="ExternalInput")
    x_res = nc.dram_tensor("x_res", [D, T_], F32, kind="ExternalInput")
    wproj_t = nc.dram_tensor("wproj_t", [D, D], F32, kind="ExternalInput")
    router_t = nc.dram_tensor("router_t", [D, 8], F32, kind="ExternalInput")
    gate_w = nc.dram_tensor("gate_w", [8, D, IE], F32, kind="ExternalInput")
    up_w = nc.dram_tensor("up_w", [8, D, IE], F32, kind="ExternalInput")
    down_w = nc.dram_tensor("down_w", [8, IE, D], F32, kind="ExternalInput")
    sg_w = nc.dram_tensor("sg_w", [D, ISH], F32, kind="ExternalInput")
    su_w = nc.dram_tensor("su_w", [D, ISH], F32, kind="ExternalInput")
    sd_w = nc.dram_tensor("sd_w", [ISH, D], F32, kind="ExternalInput")
    sg_b = nc.dram_tensor("sg_b", [ISH], F32, kind="ExternalInput")
    su_b = nc.dram_tensor("su_b", [ISH], F32, kind="ExternalInput")
    sd_b = nc.dram_tensor("sd_b", [D], F32, kind="ExternalInput")
    wq1 = nc.dram_tensor("wq1", [D, D], F32, kind="ExternalInput")
    wk1 = nc.dram_tensor("wk1", [D, D], F32, kind="ExternalInput")
    wv1 = nc.dram_tensor("wv1", [D, D], F32, kind="ExternalInput")
    ones1 = nc.dram_tensor("ones1", [1, 128], BF16, kind="ExternalInput")
    ones128 = nc.dram_tensor("ones128", [128, 1], BF16, kind="ExternalInput")
    sel = nc.dram_tensor("sel", [8, 8, 128], BF16, kind="ExternalInput")
    ident32 = nc.dram_tensor("ident32", [128, 128], F32, kind="ExternalInput")

    x2_out = nc.dram_tensor("x2_out", [D, T_], F32, kind="ExternalOutput")
    q1_out = nc.dram_tensor("q1_out", [D, T_], BF16, kind="ExternalOutput")
    k1_out = nc.dram_tensor("k1_out", [D, T_], BF16, kind="ExternalOutput")
    v1_out = nc.dram_tensor("v1_out", [D, T_], BF16, kind="ExternalOutput")

    ctx = ExitStack()
    with ctx:
        em = Em(nc, ctx)
        stream = sb(ctx, nc, "stream", [128, KC * T_], F32)
        xbf = sb(ctx, nc, "xbf", [128, KC * T_])
        att_sb = sb(ctx, nc, "att_sb", [128, 8 * T_])
        h_sb = sb(ctx, nc, "h_sb", [128, ICH * T_])
        cbc = sb(ctx, nc, "cbc", [128, T_])
        wb = [sb(ctx, nc, f"wb{i}", [128, 16 * 128]) for i in range(8)]
        s_ch = sb(ctx, nc, "s_ch", [128, T_])
        u_ch = sb(ctx, nc, "u_ch", [128, T_])
        sq_ch = [sb(ctx, nc, "sq0", [128, T_])] * 2
        sq2 = [sb(ctx, nc, "sq2_0", [128, T_])] * 2
        rwt = sb(ctx, nc, "rwt", [128, KC * 8])
        comb = sb(ctx, nc, "comb", [128, 8], F32)
        comb_fm = sb(ctx, nc, "comb_fm", [8, T_])
        rstat = sb(ctx, nc, "rstat", [128, 28], F32)
        lnrow = sb(ctx, nc, "lnrow", [1, 4 * T_], F32)
        lnrow_bf = sb(ctx, nc, "lnrow_bf", [1, 2 * T_])
        o1 = sb(ctx, nc, "o1", [1, 128])
        o128 = sb(ctx, nc, "o128", [128, 1])
        sel_sb = sb(ctx, nc, "sel_sb", [8, 8 * 128])
        id32 = sb(ctx, nc, "id32", [128, 128], F32)
        sgb = sb(ctx, nc, "sgb", [128, ISCH], F32)
        sub = sb(ctx, nc, "sub", [128, ISCH], F32)
        sdb = sb(ctx, nc, "sdb", [128, KC], F32)
        stg = [sb(ctx, nc, f"stg{i}", [128, T_]) for i in range(2)]
        pool = ps(ctx, nc, "pool", [128, 4096])
        A = pool[:, 0:1024]
        Bk = pool[:, 1024:2048]
        C = pool[:, 2048:3072]
        Dk = pool[:, 3072:4096]

        ldc = em.sem("ldc")
        ldg = em.sem("ldg")
        wsem = [em.sem(f"ws{i}") for i in range(8)]
        st = em.sem("st")
        tq = em.sem("tq")
        vv = em.sem("vv")
        ss = em.sem("ss")
        em.set_autochain("v", vv)
        em.set_autochain("s", ss)

        em.dma("y", stream[:, :].rearrange("p (kc t) -> p kc t", kc=KC),
               x_res.ap().rearrange("(kc p) t -> p kc t", p=128), ldc)
        for i in range(ISCH):
            em.dma("y", sgb[:, i:i + 1], sg_b.ap().rearrange("(i p) -> i p", p=128)[i:i + 1, :].transpose([1, 0]), ldc)
            em.dma("y", sub[:, i:i + 1], su_b.ap().rearrange("(i p) -> i p", p=128)[i:i + 1, :].transpose([1, 0]), ldc)
        for i in range(KC):
            em.dma("y", sdb[:, i:i + 1], sd_b.ap().rearrange("(i p) -> i p", p=128)[i:i + 1, :].transpose([1, 0]), ldc)
        em.dma("y", o1[:, :], ones1[:, :], ldc)
        em.dma("y", o128[:, :], ones128[:, :], ldc)
        em.dma("y", sel_sb[:, :].rearrange("p (e m) -> p e m", e=8),
               sel.ap().transpose([1, 0, 2]), ldc)
        v_ldc = em.dma("y", id32[:, :], ident32[:, :], ldc)
        v_rwt = em.dma("g", rwt[:, :].rearrange("p (kc m) -> p kc m", kc=KC),
                       router_t.ap().rearrange("(kc p) m -> p kc m", p=128), ldg)
        em.wait("t", ldc, v_ldc)
        em.wait("v", ldc, v_ldc)
        em.wait("s", ldc, v_ldc)

        wuse = [0] * 8

        def wload(i, dram_slice):
            kcs = dram_slice.shape[0] // 128
            return em.dma("g", wb[i][:, 0:kcs * 128].rearrange("p (kc m) -> p kc m", kc=kcs),
                          dram_slice.rearrange("(kc p) m -> p kc m", p=128),
                          wsem[i], waits=[(tq, wuse[i])])

        strv = stream[:, :].rearrange("p (kc t) -> p kc t", kc=KC)
        xbv = xbf[:, :].rearrange("p (kc t) -> p kc t", kc=KC)
        attv = att_sb[:, :].rearrange("p (kc t) -> p kc t", kc=8)

        def wbv(i, kcs=16):
            return wb[i][:, 0:kcs * 128].rearrange("p (kc m) -> p kc m", kc=kcs)

        lda = em.sem("lda")
        for kh in range(2):
            v_att = em.dma("g", attv[:, :, :],
                           att_in.ap().rearrange("(kc p) t -> p kc t", p=128)[:, kh * 8:(kh + 1) * 8, :],
                           lda, waits=[(vv, vv.v)])
            em.wait("t", lda, v_att)
            for oc in range(KC):
                wi = oc % 2
                v_w = wload(wi, wproj_t.ap()[kh * 1024:(kh + 1) * 1024, oc * 128:(oc + 1) * 128])
                dst = A if oc % 2 == 0 else Bk
                em.wait("t", wsem[wi], v_w)
                em.wait("t", vv, vv.v)
                for hf in range(2):
                    for kc in range(8):
                        em.op("t", lambda e, _d=dst[:, hf * 512:(hf + 1) * 512],
                              _w=wbv(wi, 8)[:, kc, :],
                              _x=attv[:, kc, hf * 512:(hf + 1) * 512],
                              _s=(kc == 0), _p=(kc == 7):
                              e.matmul(_d, _w, _x, start=_s, stop=_p),
                              inc=tq if (kc == 7 and hf == 1) else None)
                wuse[wi] = tq.v
                em.op("v", lambda e, _d=strv[:, oc, :], _s=dst[:, :]:
                      e.tensor_tensor(_d, _d, _s, ALU.add), waits=[(tq, tq.v)], inc=vv)

        def layernorm():
            em.wait("s", vv, vv.v)
            squse = [0, 0]
            for kc in range(KC):
                b2 = 0
                em.op("v", lambda e, _d=sq_ch[b2][:, :], _s=strv[:, kc, :]:
                      e.tensor_copy(_d, _s), waits=[(tq, squse[b2])], inc=vv)
                em.op("s", lambda e, _d=sq2[b2][:, :], _s=strv[:, kc, :]:
                      e.activation(_d, _s, AF.Square, bias=0.0, scale=1.0),
                      waits=[(tq, squse[b2])], inc=ss)
                for hf in range(2):
                    em.op("t", lambda e, _d=pool[0:1, 2048 + hf * 512:2048 + (hf + 1) * 512],
                          _o=o128[:, :], _x=sq_ch[b2][:, hf * 512:(hf + 1) * 512],
                          _s=(kc == 0), _p=(kc == KC - 1):
                          e.matmul(_d, _o, _x, start=_s, stop=_p),
                          waits=[(vv, vv.v)] if hf == 0 else (),
                          inc=tq if hf == 1 else None)
                for hf in range(2):
                    em.op("t", lambda e, _d=pool[0:1, 3072 + hf * 512:3072 + (hf + 1) * 512],
                          _o=o128[:, :], _x=sq2[b2][:, hf * 512:(hf + 1) * 512],
                          _s=(kc == 0), _p=(kc == KC - 1):
                          e.matmul(_d, _o, _x, start=_s, stop=_p),
                          waits=[(ss, ss.v)] if hf == 0 else (),
                          inc=tq if hf == 1 else None)
                squse[b2] = tq.v
            Ar, Br = lnrow[:, 0:T_], lnrow[:, T_:2 * T_]
            Cr, Dr = lnrow[:, 2 * T_:3 * T_], lnrow[:, 3 * T_:4 * T_]
            em.op("v", lambda e: e.tensor_copy(Ar, pool[0:1, 2048:3072]),
                  waits=[(tq, tq.v)], inc=vv)
            em.op("v", lambda e: e.tensor_copy(Br, pool[0:1, 3072:4096]), inc=vv)
            em.wait("v", vv, vv.v)
            em.op("v", lambda e: e.tensor_scalar_mul(Cr, Ar, 1.0 / D), inc=vv)
            em.wait("v", vv, vv.v)
            em.op("v", lambda e: e.tensor_scalar_mul(Ar, Br, 1.0 / D), inc=vv)
            em.wait("v", vv, vv.v)
            em.op("v", lambda e: e.tensor_mul(Br, Cr, Cr), inc=vv)
            em.wait("v", vv, vv.v)
            em.op("v", lambda e: e.tensor_sub(Ar, Ar, Br), inc=vv)
            em.wait("v", vv, vv.v)
            em.op("v", lambda e: e.tensor_scalar_add(Ar, Ar, EPS), inc=vv)
            em.op("s", lambda e: e.activation(Br, Ar, AF.Ln, bias=0.0, scale=1.0),
                  waits=[(vv, vv.v)], inc=ss)
            em.wait("s", ss, ss.v)
            em.op("s", lambda e: e.activation(Dr, Br, AF.Exp, bias=0.0, scale=-0.5),
                  inc=ss)
            em.op("v", lambda e: e.tensor_mul(Ar, Cr, Dr),
                  waits=[(ss, ss.v)], inc=vv)
            em.wait("v", vv, vv.v)
            em.op("v", lambda e: e.tensor_copy(lnrow_bf[:, 0:T_], Dr), inc=vv)
            em.op("v", lambda e: e.tensor_copy(lnrow_bf[:, T_:2 * T_], Ar), inc=vv)
            for hf in range(2):
                em.op("t", lambda e, _d=C[:, hf * 512:(hf + 1) * 512], _r=lnrow_bf[:, hf * 512:(hf + 1) * 512]:
                      e.matmul(_d, o1[:, :], _r, start=True, stop=True),
                      waits=[(vv, vv.v)] if hf == 0 else (), inc=tq if hf == 1 else None)
                em.op("t", lambda e, _d=Dk[:, hf * 512:(hf + 1) * 512], _r=lnrow_bf[:, T_ + hf * 512:T_ + (hf + 1) * 512]:
                      e.matmul(_d, o1[:, :], _r, start=True, stop=True),
                      inc=tq if hf == 1 else None)
            for kc in range(KC):
                em.op("v", lambda e, _x=strv[:, kc, :], _a=C[:, :]:
                      e.tensor_mul(_x, _x, _a), waits=[(tq, tq.v)] if kc == 0 else (), inc=vv)
                em.wait("v", vv, vv.v)
                em.op("v", lambda e, _x=strv[:, kc, :], _b=Dk[:, :]:
                      e.tensor_sub(_x, _x, _b), inc=vv)
                em.wait("v", vv, vv.v)
                em.op("v", lambda e, _d=xbv[:, kc, :], _s=strv[:, kc, :]:
                      e.tensor_copy(_d, _s), inc=vv)
            em.wait("t", vv, vv.v)

        layernorm()

        for tt in range(8):
            em.wait("t", ldg, v_rwt)
            for kc in range(KC):
                em.op("t", lambda e, _d=C[:, 0:8],
                      _x=xbv[:, kc, tt * 128:(tt + 1) * 128],
                      _w=rwt[:, :].rearrange("p (kc m) -> p kc m", kc=KC)[:, kc, :],
                      _s=(kc == 0), _p=(kc == KC - 1):
                      e.matmul(_d, _x, _w, start=_s, stop=_p),
                      inc=tq if kc == KC - 1 else None)
            em.op("v", lambda e: e.reduce_max(rstat[:, 0:1], C[:, 0:8], axis=AX.X),
                  waits=[(tq, tq.v)], inc=vv)
            em.wait("v", vv, vv.v)
            em.op("v", lambda e: e.tensor_scalar_mul(rstat[:, 1:2], rstat[:, 0:1], -1.0), inc=vv)
            em.op("s", lambda e: e.activation(comb[:, 0:8], C[:, 0:8], AF.Exp,
                                              bias=rstat[:, 1:2], scale=1.0),
                  waits=[(vv, vv.v), (tq, tq.v)], inc=ss)
            em.op("v", lambda e: e.reduce_max(rstat[:, 2:3], comb[:, 0:8], axis=AX.X),
                  waits=[(ss, ss.v)], inc=vv)
            em.wait("v", vv, vv.v)
            em.op("v", lambda e: e.tensor_single_scalar(rstat[:, 12:20], comb[:, 0:8], rstat[:, 2:3], ALU.is_equal), inc=vv)
            em.wait("v", vv, vv.v)
            em.op("v", lambda e: e.scalar_tensor_tensor(rstat[:, 20:28], rstat[:, 12:20], -1.0e30,
                                                        comb[:, 0:8], ALU.mult, ALU.add), inc=vv)
            em.wait("v", vv, vv.v)
            em.op("v", lambda e: e.reduce_max(rstat[:, 3:4], rstat[:, 20:28], axis=AX.X), inc=vv)
            em.wait("v", vv, vv.v)
            em.op("v", lambda e: e.tensor_single_scalar(rstat[:, 20:28], comb[:, 0:8], rstat[:, 3:4], ALU.is_equal), inc=vv)
            em.op("v", lambda e: e.tensor_add(rstat[:, 4:5], rstat[:, 2:3], rstat[:, 3:4]), inc=vv)
            em.wait("v", vv, vv.v)
            em.op("v", lambda e: e.reciprocal(rstat[:, 5:6], rstat[:, 4:5]), inc=vv)
            em.wait("v", vv, vv.v)
            em.op("v", lambda e: e.tensor_mul(rstat[:, 6:7], rstat[:, 2:3], rstat[:, 5:6]), inc=vv)
            em.op("v", lambda e: e.tensor_mul(rstat[:, 7:8], rstat[:, 3:4], rstat[:, 5:6]), inc=vv)
            em.wait("v", vv, vv.v)
            em.op("v", lambda e: e.tensor_single_scalar(rstat[:, 12:20], rstat[:, 12:20], rstat[:, 6:7], ALU.mult), inc=vv)
            em.wait("v", vv, vv.v)
            em.op("v", lambda e: e.scalar_tensor_tensor(comb[:, 0:8], rstat[:, 20:28], rstat[:, 7:8],
                                                        rstat[:, 12:20], ALU.mult, ALU.add), inc=vv)
            em.op("t", lambda e, _d=Dk[0:8, 0:128], _s=comb[:, 0:8]:
                  e.transpose(_d, _s, id32[:, :]),
                  waits=[(vv, vv.v)], inc=tq)
            em.op("v", lambda e, _d=comb_fm[0:8, tt * 128:(tt + 1) * 128], _s=Dk[0:8, 0:128]:
                  e.tensor_copy(_d, _s), waits=[(tq, tq.v)], inc=vv)
        comb_done = vv.v

        def gu_half(dst, wi, hf):
            for kc in range(KC):
                em.op("t", lambda e, _d=dst[:, hf * 512:(hf + 1) * 512],
                      _w=wbv(wi)[:, kc, :],
                      _x=xbv[:, kc, hf * 512:(hf + 1) * 512],
                      _s=(kc == 0), _p=(kc == KC - 1):
                      e.matmul(_d, _w, _x, start=_s, stop=_p),
                      inc=tq if kc == KC - 1 else None)

        for e_ in range(8):
            for hf in range(2):
                em.op("t", lambda e, _d=C[:, hf * 512:(hf + 1) * 512],
                      _s=sel_sb[:, :].rearrange("p (e m) -> p e m", e=8)[:, e_, :],
                      _r=comb_fm[0:8, hf * 512:(hf + 1) * 512]:
                      e.matmul(_d, _s, _r, start=True, stop=True),
                      waits=[(vv, comb_done if e_ == 0 else vv.v)] if hf == 0 else (),
                      inc=tq if hf == 1 else None)
            em.op("v", lambda e: e.tensor_copy(cbc[:, :], C[:, :]), waits=[(tq, tq.v)], inc=vv)
            cb_ready = vv.v
            for ii in range(ICH):
                wg_i, wu_i = 2 + ii % 2, 4 + ii % 2
                v_wg = wload(wg_i, gate_w.ap()[e_, :, ii * 128:(ii + 1) * 128])
                v_wu = wload(wu_i, up_w.ap()[e_, :, ii * 128:(ii + 1) * 128])
                em.wait("t", wsem[wg_i], v_wg)
                em.wait("t", wsem[wu_i], v_wu)
                em.wait("t", ss, ss.v)
                em.wait("t", vv, vv.v)
                gu_half(A, wg_i, 0); gu_half(A, wg_i, 1)
                wuse[wg_i] = tq.v
                g_done = tq.v
                gu_half(Bk, wu_i, 0); gu_half(Bk, wu_i, 1)
                wuse[wu_i] = tq.v
                em.op("s", lambda e: e.activation(s_ch[:, :], A[:, :], AF.Silu, bias=0.0, scale=1.0),
                      waits=[(tq, g_done), (vv, vv.v)], inc=ss)
                em.op("v", lambda e: e.tensor_tensor(u_ch[:, :], s_ch[:, :], Bk[:, :], ALU.mult),
                      waits=[(ss, ss.v), (tq, tq.v), (vv, cb_ready)], inc=vv)
                em.wait("v", vv, vv.v)
                em.op("v", lambda e, _d=h_sb[:, ii * T_:(ii + 1) * T_]:
                      e.tensor_tensor(_d, u_ch[:, :], cbc[:, :], ALU.mult), inc=vv)
            h_done = vv.v
            for oc in range(KC):
                wdi = 6 + (oc % 2)
                v_wd = wload(wdi, down_w.ap()[e_, :, oc * 128:(oc + 1) * 128])
                dst = C if oc % 2 == 0 else Dk
                em.wait("t", wsem[wdi], v_wd)
                em.wait("t", vv, h_done if oc == 0 else vv.v)
                for hf in range(2):
                    for ii in range(ICH):
                        em.op("t", lambda e, _d=dst[:, hf * 512:(hf + 1) * 512],
                              _w=wbv(wdi, ICH)[:, ii, :],
                              _h=h_sb[:, ii * T_ + hf * 512: ii * T_ + (hf + 1) * 512],
                              _s=(ii == 0), _p=(ii == ICH - 1):
                              e.matmul(_d, _w, _h, start=_s, stop=_p),
                              inc=tq if (ii == ICH - 1 and hf == 1) else None)
                wuse[wdi] = tq.v
                em.op("v", lambda e, _x=strv[:, oc, :], _s=dst[:, :]:
                      e.tensor_tensor(_x, _x, _s, ALU.add), waits=[(tq, tq.v)], inc=vv)

        for half in range(2):
            ib0 = half * 11
            for ii in range(11):
                ic = ib0 + ii
                wg_i, wu_i = 2 + ii % 2, 4 + ii % 2
                v_wg = wload(wg_i, sg_w.ap()[:, ic * 128:(ic + 1) * 128])
                v_wu = wload(wu_i, su_w.ap()[:, ic * 128:(ic + 1) * 128])
                em.wait("t", wsem[wg_i], v_wg)
                em.wait("t", wsem[wu_i], v_wu)
                em.wait("t", ss, ss.v)
                em.wait("t", vv, vv.v)
                gu_half(A, wg_i, 0); gu_half(A, wg_i, 1)
                wuse[wg_i] = tq.v
                g_done = tq.v
                gu_half(Bk, wu_i, 0); gu_half(Bk, wu_i, 1)
                wuse[wu_i] = tq.v
                em.op("s", lambda e, _b=sgb[:, ic:ic + 1]:
                      e.activation(s_ch[:, :], A[:, :], AF.Silu, bias=_b, scale=1.0),
                      waits=[(tq, g_done), (vv, vv.v)], inc=ss)
                em.op("v", lambda e, _b=sub[:, ic:ic + 1]:
                      e.tensor_scalar_add(u_ch[:, :], Bk[:, :], _b),
                      waits=[(tq, tq.v)], inc=vv)
                em.wait("v", vv, vv.v)
                em.op("v", lambda e, _d=h_sb[:, ii * T_:(ii + 1) * T_]:
                      e.tensor_tensor(_d, s_ch[:, :], u_ch[:, :], ALU.mult),
                      waits=[(ss, ss.v)], inc=vv)
            h_done = vv.v
            for oc in range(KC):
                wdi = 6 + (oc % 2)
                v_wd = wload(wdi, sd_w.ap()[ib0 * 128:(ib0 + 11) * 128, oc * 128:(oc + 1) * 128])
                dst = C if oc % 2 == 0 else Dk
                em.wait("t", wsem[wdi], v_wd)
                em.wait("t", vv, h_done if oc == 0 else vv.v)
                for hf in range(2):
                    for ii in range(11):
                        em.op("t", lambda e, _d=dst[:, hf * 512:(hf + 1) * 512],
                              _w=wbv(wdi, 11)[:, ii, :],
                              _h=h_sb[:, ii * T_ + hf * 512: ii * T_ + (hf + 1) * 512],
                              _s=(ii == 0), _p=(ii == 11 - 1):
                              e.matmul(_d, _w, _h, start=_s, stop=_p),
                              inc=tq if (ii == 11 - 1 and hf == 1) else None)
                wuse[wdi] = tq.v
                em.op("v", lambda e, _x=strv[:, oc, :], _s=dst[:, :]:
                      e.tensor_tensor(_x, _x, _s, ALU.add), waits=[(tq, tq.v)], inc=vv)
        for oc in range(KC):
            em.op("v", lambda e, _x=strv[:, oc, :], _b=sdb[:, oc:oc + 1]:
                  e.tensor_scalar_add(_x, _x, _b), inc=vv)
        em.wait("t", vv, vv.v)

        layernorm()
        x2_ready = vv.v
        for kc in range(KC):
            em.op("y", lambda e, _d=x2_out.ap().rearrange("(kc p) t -> p kc t", p=128)[:, kc, :],
                  _s=strv[:, kc, :]:
                  e.dma_start(out=_d, in_=_s), waits=[(vv, x2_ready)], inc=st, inc_n=16)

        stsem = [em.sem("stg0"), em.sem("stg1")]
        stguse = [0, 0]
        for wt, outt in ((wq1, q1_out), (wk1, k1_out), (wv1, v1_out)):
            for oc in range(KC):
                wi = oc % 2
                v_w = wload(wi, wt.ap()[:, oc * 128:(oc + 1) * 128])
                dst = A if oc % 2 == 0 else Bk
                sgi = oc % 2
                em.wait("t", wsem[wi], v_w)
                em.wait("t", vv, vv.v)
                for hf in range(2):
                    for kc in range(KC):
                        em.op("t", lambda e, _d=dst[:, hf * 512:(hf + 1) * 512],
                              _w=wbv(wi)[:, kc, :],
                              _x=xbv[:, kc, hf * 512:(hf + 1) * 512],
                              _s=(kc == 0), _p=(kc == KC - 1):
                              e.matmul(_d, _w, _x, start=_s, stop=_p),
                              inc=tq if (kc == KC - 1 and hf == 1) else None)
                wuse[wi] = tq.v
                em.op("v", lambda e, _d=stg[sgi][:, :], _s=dst[:, :]:
                      e.tensor_copy(_d, _s), waits=[(tq, tq.v), (stsem[sgi], stguse[sgi])], inc=vv)
                stguse[sgi] = em.dma("y", outt.ap().rearrange("(kc p) t -> p kc t", p=128)[:, oc, :],
                                     stg[sgi][:, :], stsem[sgi], waits=[(vv, vv.v)])
        em.wait("y", st, st.v)
        em.wait("y", stsem[0], stguse[0])
        em.wait("y", stsem[1], stguse[1])
        em.run_block()
    return nc


def a2_in_maps(inputs, att_full):
    x = np.asarray(inputs["L0_attn_x"])
    maps = []
    common = {
        "wproj_t": np.ascontiguousarray(np.asarray(inputs["L0_attn_weight_proj"]).T),
        "router_t": np.ascontiguousarray(np.asarray(inputs["L0_ffn_router_w"]).T),
        "gate_w": np.asarray(inputs["L0_ffn_gate_w"]),
        "up_w": np.asarray(inputs["L0_ffn_up_w"]),
        "down_w": np.asarray(inputs["L0_ffn_down_w"]),
        "sg_w": np.asarray(inputs["L0_ffn_s_gate_w"]),
        "su_w": np.asarray(inputs["L0_ffn_s_up_w"]),
        "sd_w": np.asarray(inputs["L0_ffn_s_down_w"]),
        "sg_b": np.asarray(inputs["L0_ffn_s_gate_b"]),
        "su_b": np.asarray(inputs["L0_ffn_s_up_b"]),
        "sd_b": np.asarray(inputs["L0_ffn_s_down_b"]),
        "wq1": np.asarray(inputs["L1_attn_Wq"]),
        "wk1": np.asarray(inputs["L1_attn_Wk"]),
        "wv1": np.asarray(inputs["L1_attn_Wv"]),
        "ones1": np.ones((1, 128), BF),
        "ones128": np.ones((128, 1), BF),
        "sel": np.eye(8, dtype=BF)[:, :, None] * np.ones((1, 1, 128), BF),
        "ident32": np.eye(128, dtype=np.float32),
    }
    for c in range(8):
        b = c // 4
        s0, s1 = stripes_of_core(c)
        cols = np.r_[s0 * 512:(s0 + 1) * 512, s1 * 512:(s1 + 1) * 512]
        m = dict(common)
        m["att_in"] = np.ascontiguousarray(att_full[b][:, cols])
        m["x_res"] = np.ascontiguousarray(x[b].T[:, cols].astype(np.float32))
        maps.append(m)
    return maps



def build_b1():
    nc = bass.Bass("TRN2", target_bir_lowering=False)
    H = 4
    HD = H * 128
    SEG = 512

    qf_d = nc.dram_tensor("qf", [HD, S], BF16, kind="ExternalInput")
    kf_d = nc.dram_tensor("kf", [HD, S], BF16, kind="ExternalInput")
    vt_d = nc.dram_tensor("vt", [S, HD], BF16, kind="ExternalInput")
    betasT = nc.dram_tensor("betasT", [128, H], F32, kind="ExternalInput")
    tri = nc.dram_tensor("tri", [128, 128], F32, kind="ExternalInput")
    ident = nc.dram_tensor("ident", [128, 128], BF16, kind="ExternalInput")
    ones1b = nc.dram_tensor("ones1b", [1, 128], BF16, kind="ExternalInput")
    att_d = nc.dram_tensor("att", [HD, S], BF16, kind="ExternalOutput")

    ctx = ExitStack()
    with ctx:
        em = Em(nc, ctx)
        qseg = sb(ctx, nc, "qseg", [128, H * 1024])
        kseg = sb(ctx, nc, "kseg", [128, H * 1024])
        vseg = sb(ctx, nc, "vseg", [128, 8 * HD])
        mem_aug = sb(ctx, nc, "mem_aug", [128, H * 129], F32)
        mem_bf = sb(ctx, nc, "mem_bf", [128, H * 129])
        sigk_b = [sb(ctx, nc, f"sigk{i}", [128, 512]) for i in range(2)]
        sigq_b = [sb(ctx, nc, f"sigq{i}", [128, 512]) for i in range(2)]
        sigkT = sb(ctx, nc, "sigkT", [128, 512])
        sck_b = [sb(ctx, nc, f"sck{i}", [128, 512], F32) for i in range(2)]
        sce_b = [sb(ctx, nc, f"sce{i}", [128, 512]) for i in range(2)]
        delt = sb(ctx, nc, "delt", [128, 512])
        p_sb = sb(ctx, nc, "p_sb", [128, 512])
        pt2 = sb(ctx, nc, "pt2", [128, 512])
        adot = sb(ctx, nc, "adot", [128, 512])
        astg = [sb(ctx, nc, f"astg{i}", [128, 512]) for i in range(2)]
        rowr = sb(ctx, nc, "rowr", [1, 1024], F32)
        rowb = sb(ctx, nc, "rowb", [1, 512])
        beta_sb = sb(ctx, nc, "beta_sb", [128, H], F32)
        omb_sb = sb(ctx, nc, "omb_sb", [128, H], F32)
        tri_sb = sb(ctx, nc, "tri_sb", [128, 128], F32)
        idn = sb(ctx, nc, "idn", [128, 128])
        o1 = sb(ctx, nc, "o1", [1, 128])
        stat_t = sb(ctx, nc, "stat_t", [128, 16], F32)
        pool = ps(ctx, nc, "pool", [128, 4096])
        sc_ps = pool[:, 0:512]
        ptp = pool[:, 512:1024]
        av_ps = pool[:, 1024:1152]
        md_ps = pool[:, 1152:1280]
        attm_ps = pool[:, 1536:2048]
        bc_ps = pool[:, 2048:2560]
        zq_ps = pool[0:1, 2560:3072]
        ds_ps = [pool[:, 3072:3201], pool[:, 3584:3713]]
        skT_ps = pool[:, 3328:3584]

        ldc = em.sem("ldc")
        lds = [em.sem("lds0"), em.sem("lds1")]
        st = em.sem("st")
        stsem = [em.sem("stg0"), em.sem("stg1")]
        tq = em.sem("tq")
        vv = em.sem("vv")
        ss = em.sem("ss")
        em.set_autochain("v", vv)
        em.set_autochain("s", ss)

        ptp_bf = ptp.tensor.bitcast(BF16)
        skT_bf = skT_ps.tensor.bitcast(BF16)

        em.dma("y", beta_sb[:, :], betasT[:, :], ldc)
        em.dma("y", tri_sb[:, :], tri[:, :], ldc)
        em.dma("y", idn[:, :], ident[:, :], ldc)
        v_ldc = em.dma("y", o1[:, :], ones1b[:, :], ldc)
        em.wait("t", ldc, v_ldc)
        em.wait("v", ldc, v_ldc)
        em.wait("s", ldc, v_ldc)
        em.op("s", lambda e: e.activation(beta_sb[:, :], beta_sb[:, :], AF.Sigmoid, bias=0.0, scale=1.0))
        em.op("v", lambda e: e.scalar_tensor_tensor(omb_sb[:, :], beta_sb[:, :], -1.0, beta_sb[:, :],
                                                    ALU.mult, ALU.bypass), waits=[(ss, ss.v)])
        em.op("v", lambda e: e.tensor_scalar_add(omb_sb[:, :], omb_sb[:, :], 1.0))
        em.op("v", lambda e: e.memset(mem_aug[:, :], 0.0))
        for h in range(H):
            em.op("v", lambda e, _c=mem_aug[:, h * 129 + 128:h * 129 + 129]:
                  e.memset(_c, 1.0 / 128.0))

        qsv = qseg[:, :].rearrange("p (h t) -> p h t", h=H)
        ksv = kseg[:, :].rearrange("p (h t) -> p h t", h=H)
        vsv = vseg[:, :].rearrange("p (s f) -> p s f", s=8)

        sig_use = [0, 0]

        def sigma(dst, src_ap, hp):
            em.op("v", lambda e, _d=sck_b[hp][:, :]: e.tensor_scalar_min(_d, src_ap, 0.0),
                  waits=[(tq, sig_use[hp]), (ss, ss.v)])
            em.op("s", lambda e, _d=sce_b[hp][:, :], _s=sck_b[hp][:, :]:
                  e.activation(_d, _s, AF.Exp, bias=0.0, scale=1.0),
                  waits=[(vv, vv.v)])
            em.op("v", lambda e, _e=sce_b[hp][:, :]: e.scalar_tensor_tensor(dst, src_ap, 0.0, _e,
                                                        ALU.max, ALU.add), waits=[(ss, ss.v)])

        seg_use = [0, 0]
        seg_use_v = [0, 0]
        stg_use = [0, 0]
        for s_ in range(8):
            par = s_ % 2
            tsl = slice(s_ * 512, (s_ + 1) * 512)
            v_q = em.dma("g", qsv[:, :, par * 512:(par + 1) * 512],
                         qf_d.ap().rearrange("(h p) t -> p h t", p=128)[:, :, tsl],
                         lds[par], waits=[(tq, seg_use[par]), (vv, seg_use_v[par])])
            v_k = em.dma("g", ksv[:, :, par * 512:(par + 1) * 512],
                         kf_d.ap().rearrange("(h p) t -> p h t", p=128)[:, :, tsl],
                         lds[par], waits=[])
            v_v = em.dma("g", vsv[:, par * 4:(par + 1) * 4, :],
                         vt_d.ap()[tsl, :].rearrange("(sub p) f -> p sub f", p=128),
                         lds[par], waits=[])
            em.wait("t", lds[par], v_v)
            em.wait("v", lds[par], v_v)
            for h in range(H):
                hp = h % 2
                sigk = sigk_b[hp]
                sigq = sigq_b[hp]
                kb0 = h * 1024 + par * 512
                kk = kseg[:, kb0:kb0 + 512]
                qq = qseg[:, kb0:kb0 + 512]
                mb = mem_bf[:, h * 129:(h + 1) * 129]
                ma = mem_aug[:, h * 129:(h + 1) * 129]
                em.op("v", lambda e, _d=mb, _s=ma: e.tensor_copy(_d, _s),
                      waits=[(tq, sig_use[hp])])
                sigma(sigk[:, :], kk, hp)
                sigma(sigq[:, :], qq, hp)
                sig_done = vv.v
                em.op("t", lambda e, _z=mem_bf[:, h * 129 + 128:h * 129 + 129], _q=sigq[:, :]:
                      e.matmul(zq_ps, _z, _q, start=True, stop=True),
                      waits=[(vv, sig_done)], inc=tq)
                em.op("v", lambda e: e.tensor_copy(rowr[:, 0:512], zq_ps), waits=[(tq, tq.v)])
                em.op("v", lambda e: e.reciprocal(rowr[:, 512:1024], rowr[:, 0:512]),
                      waits=[(vv, vv.v)])
                em.op("v", lambda e: e.tensor_copy(rowb[:, :], rowr[:, 512:1024]))
                em.op("t", lambda e: e.matmul(bc_ps, o1[:, :], rowb[:, :], start=True, stop=True),
                      waits=[(vv, vv.v)], inc=tq)
                em.op("v", lambda e, _q=sigq[:, :]: e.tensor_tensor(_q, _q, bc_ps, ALU.mult),
                      waits=[(tq, tq.v)])
                sq_done = vv.v
                em.op("t", lambda e, _m=mem_bf[:, h * 129:h * 129 + 128], _q=sigq[:, :]:
                      e.matmul(attm_ps, _m, _q, start=True, stop=True),
                      waits=[(vv, sq_done)], inc=tq)
                attm_done = tq.v
                for sub in range(4):
                    dsp = ds_ps[sub % 2]
                    em.op("t", lambda e, _l=sigk[:, sub * 128:(sub + 1) * 128], _m=mb, _d=dsp:
                          e.matmul(_d, _l, _m, start=True, stop=True),
                          waits=[(vv, vv.v)], inc=tq)
                    em.op("v", lambda e, _d=dsp, _o=stat_t[:, hp * 8 + 0:hp * 8 + 1]:
                          e.reciprocal(_o, _d[:, 128:129]),
                          waits=[(tq, tq.v)])
                    em.op("v", lambda e, _o=stat_t[:, hp * 8 + 1:hp * 8 + 2], _i=stat_t[:, hp * 8 + 0:hp * 8 + 1]:
                          e.tensor_scalar_mul(_o, _i, -1.0),
                          waits=[(vv, vv.v)])
                    em.op("v", lambda e, _d=delt[:, sub * 128:(sub + 1) * 128], _p=dsp,
                          _sc=stat_t[:, hp * 8 + 1:hp * 8 + 2],
                          _v=vseg[:, (par * 4 + sub) * HD + h * 128:(par * 4 + sub) * HD + (h + 1) * 128]:
                          e.scalar_tensor_tensor(_d, _p[:, 0:128], _sc, _v,
                                                 ALU.mult, ALU.add),
                          waits=[(vv, vv.v), (tq, tq.v)])
                delta_done = vv.v
                sgi = (s_ * H + h) % 2
                for sub in range(4):
                    nkb = sub + 1
                    em.op("t", lambda e, _q=qseg[:, kb0 + sub * 128:kb0 + (sub + 1) * 128],
                          _k=kseg[:, kb0:kb0 + nkb * 128], _d=pool[:, 0:nkb * 128]:
                          e.matmul(_d, _q, _k, start=True, stop=True),
                          waits=[(ss, ss.v), (vv, vv.v)], inc=tq)
                    em.op("v", lambda e, _d=pool[:, (nkb - 1) * 128:nkb * 128]:
                          e.tensor_tensor(_d, _d, tri_sb[:, :], ALU.add), waits=[(tq, tq.v)])
                    em.op("v", lambda e, _s=pool[:, 0:nkb * 128], _o=stat_t[:, hp * 8 + 2:hp * 8 + 3]:
                          e.reduce_max(_o, _s, axis=AX.X), waits=[(vv, vv.v)])
                    em.op("v", lambda e, _o=stat_t[:, hp * 8 + 3:hp * 8 + 4], _i=stat_t[:, hp * 8 + 2:hp * 8 + 3]:
                          e.tensor_scalar_mul(_o, _i, -SCALE0),
                          waits=[(vv, vv.v)])
                    em.op("s", lambda e, _d=p_sb[:, 0:nkb * 128], _s=pool[:, 0:nkb * 128],
                          _b=stat_t[:, hp * 8 + 3:hp * 8 + 4], _a=stat_t[:, hp * 8 + 4:hp * 8 + 5]:
                          e.activation(_d, _s, AF.Exp, bias=_b, scale=SCALE0,
                                       accum_out=_a),
                          waits=[(vv, vv.v), (tq, tq.v)])
                    em.op("v", lambda e, _o=stat_t[:, hp * 8 + 5:hp * 8 + 6], _i=stat_t[:, hp * 8 + 4:hp * 8 + 5]:
                          e.reciprocal(_o, _i),
                          waits=[(ss, ss.v)])
                    em.op("s", lambda e, _p=p_sb[:, 0:nkb * 128], _m=stat_t[:, hp * 8 + 5:hp * 8 + 6]:
                          e.mul(_p, _p, _m),
                          waits=[(vv, vv.v)])
                    pn_s = ss.v
                    pn = vv.v
                    for kb in range(nkb):
                        em.op("t", lambda e, _d=ptp_bf[:, 1024 + kb * 128:1024 + (kb + 1) * 128],
                              _s=p_sb[:, kb * 128:(kb + 1) * 128]:
                              e.transpose(_d, _s, idn[:, :]),
                              waits=[(ss, pn_s), (vv, pn)] if kb == 0 else (),
                              inc=tq if kb == nkb - 1 else None)
                    em.op("v", lambda e, _d=pt2[:, 0:nkb * 128], _s=ptp_bf[:, 1024:1024 + nkb * 128]:
                          e.tensor_copy(_d, _s), waits=[(tq, tq.v)])
                    ptd = vv.v
                    for kb in range(nkb):
                        em.op("t", lambda e, _v=vseg[:, (par * 4 + kb) * HD + h * 128:(par * 4 + kb) * HD + (h + 1) * 128],
                              _p=pt2[:, kb * 128:(kb + 1) * 128],
                              _s=(kb == 0), _e=(kb == nkb - 1):
                              e.matmul(av_ps, _v, _p, start=_s, stop=_e),
                              waits=[(vv, ptd)] if kb == 0 else (),
                              inc=tq if kb == nkb - 1 else None)
                    em.op("s", lambda e, _d=adot[:, sub * 128:(sub + 1) * 128],
                          _o=omb_sb[:, h:h + 1]:
                          e.mul(_d, av_ps, _o),
                          waits=[(tq, tq.v)])
                    em.op("v", lambda e, _d=astg[sgi][:, sub * 128:(sub + 1) * 128],
                          _a=pool[:, 1536 + sub * 128:1536 + (sub + 1) * 128],
                          _o=adot[:, sub * 128:(sub + 1) * 128],
                          _bt=beta_sb[:, h:h + 1]:
                          e.scalar_tensor_tensor(_d, _a, _bt, _o,
                                                 ALU.mult, ALU.add),
                          waits=[(tq, attm_done), (stsem[sgi], stg_use[sgi]), (ss, ss.v)])
                stg_use[sgi] = em.dma("y", att_d.ap()[h * 128:(h + 1) * 128, tsl],
                                      astg[sgi][:, :], stsem[sgi], waits=[(vv, vv.v)])
                for sub in range(4):
                    em.op("t", lambda e, _d=skT_bf[:, 6656 + sub * 128:6656 + (sub + 1) * 128],
                          _s=sigk[:, sub * 128:(sub + 1) * 128]:
                          e.transpose(_d, _s, idn[:, :]),
                          waits=[(vv, delta_done)] if sub == 0 else (),
                          inc=tq if sub == 3 else None)
                em.op("v", lambda e: e.tensor_copy(sigkT[:, :], skT_bf[:, 6656:6656 + 512]),
                      waits=[(tq, tq.v)])
                em.wait("t", vv, vv.v)
                for sub in range(4):
                    em.op("t", lambda e, _l=sigkT[:, sub * 128:(sub + 1) * 128],
                          _r=delt[:, sub * 128:(sub + 1) * 128],
                          _s=(sub == 0), _e=(sub == 3):
                          e.matmul(md_ps, _l, _r, start=_s, stop=_e),
                          inc=tq if sub == 3 else None)
                em.op("v", lambda e, _m=mem_aug[:, h * 129:h * 129 + 128]:
                      e.tensor_tensor(_m, _m, md_ps, ALU.add), waits=[(tq, tq.v)])
                em.op("v", lambda e, _o=stat_t[:, hp * 8 + 6:hp * 8 + 7], _k=sigk[:, :]:
                      e.reduce_sum(_o, _k, axis=AX.X))
                em.op("v", lambda e, _z=mem_aug[:, h * 129 + 128:h * 129 + 129],
                      _zs=stat_t[:, hp * 8 + 6:hp * 8 + 7]:
                      e.tensor_tensor(_z, _z, _zs, ALU.add), waits=[(vv, vv.v)])
                sig_use[hp] = tq.v
            seg_use[par] = tq.v
            seg_use_v[par] = vv.v
        em.wait("y", stsem[0], stg_use[0])
        em.wait("y", stsem[1], stg_use[1])
        em.run_block()
    return nc



def build_b2():
    nc = bass.Bass("TRN2", target_bir_lowering=False)
    KC = D // 128
    T_ = 1024
    I1 = 7168
    IC = I1 // 128
    R = 64
    SCL = 16.0 / 64.0

    att_in = nc.dram_tensor("att_in", [D, T_], BF16, kind="ExternalInput")
    x2_in = nc.dram_tensor("x2_in", [D, T_], F32, kind="ExternalInput")
    wo = nc.dram_tensor("wo", [D, D], F32, kind="ExternalInput")
    conf_t = nc.dram_tensor("conf_t", [D, 8], F32, kind="ExternalInput")
    conf_b = nc.dram_tensor("conf_b", [1, 8], F32, kind="ExternalInput")
    wealth = nc.dram_tensor("wealth", [1, 8], F32, kind="ExternalInput")
    bgate = nc.dram_tensor("bgate", [D, I1], F32, kind="ExternalInput")
    bup = nc.dram_tensor("bup", [D, I1], F32, kind="ExternalInput")
    bdown = nc.dram_tensor("bdown", [I1, D], F32, kind="ExternalInput")
    gA = nc.dram_tensor("gA", [8, D, R], F32, kind="ExternalInput")
    gB = nc.dram_tensor("gB", [8, R, I1], F32, kind="ExternalInput")
    uA = nc.dram_tensor("uA", [8, D, R], F32, kind="ExternalInput")
    uB = nc.dram_tensor("uB", [8, R, I1], F32, kind="ExternalInput")
    dA = nc.dram_tensor("dA", [8, I1, R], F32, kind="ExternalInput")
    dB = nc.dram_tensor("dB", [8, R, D], F32, kind="ExternalInput")
    ones1 = nc.dram_tensor("ones1", [1, 128], BF16, kind="ExternalInput")
    ones128 = nc.dram_tensor("ones128", [128, 1], BF16, kind="ExternalInput")
    sel = nc.dram_tensor("sel", [8, 8, 128], BF16, kind="ExternalInput")
    ident32 = nc.dram_tensor("ident32", [128, 128], F32, kind="ExternalInput")
    identb = nc.dram_tensor("identb", [128, 128], BF16, kind="ExternalInput")

    out_d = nc.dram_tensor("out", [D, T_], F32, kind="ExternalOutput")
    x3_dram = nc.dram_tensor("x3_sp", [D, T_], F32, kind="Internal")
    hb_dram = nc.dram_tensor("hb_sp", [IC, 128, T_], F32, kind="Internal")

    ctx = ExitStack()
    with ctx:
        em = Em(nc, ctx)
        AR = sb(ctx, nc, "AR", [128, 94 * 1024])
        ARf = AR.bitcast(F32)
        xbf = AR[:, 0:16 * 1024]
        stream = ARf[:, 8192:8192 + 16 * 1024]
        z_g = AR[0:64, 16384:16384 + 8192]
        z_u = AR[0:64, 24576:24576 + 8192]
        base_g = AR[:, 32768:32768 + 1024]
        base_u = AR[:, 33792:33792 + 1024]
        gb_b = [AR[0:64, 34816 + i * 1024:34816 + (i + 1) * 1024] for i in range(2)]
        ub_b = [AR[0:64, 36864 + i * 1024:36864 + (i + 1) * 1024] for i in range(2)]
        dA_b = [AR[:, 38912 + i * 512:38912 + (i + 1) * 512] for i in range(2)]
        suw_b = [AR[:, 39936:39936 + 1024], AR[:, 44032:44032 + 1024]]
        hbar = ARf[:, 20480:20480 + 1024]
        s_ch = AR[:, 43008:43008 + 1024]
        g_ch = AR[:, 46080:46080 + 1024]
        u_ch = AR[:, 45056:45056 + 1024]
        att_sb = AR[:, 49152:49152 + 8192]
        y_sb = AR[0:64, 49152:49152 + 8192]
        wd = AR[:, 57344:57344 + 14336]
        wb = [AR[:, 71680 + i * 2048:71680 + (i + 1) * 2048] for i in range(4)]
        stg = [AR[:, 79872 + i * 1024:79872 + (i + 1) * 1024] for i in range(2)]
        lnrow = ARf[:, 40960:40960 + 4096][0:1, :]
        lnrow_bf = AR[:, 90112:90112 + 2048][0:1, :]
        cbc8 = AR[:, 81920:81920 + 8192]
        comb_fm = AR[0:8, 92160:92160 + 1024]
        hb_st = [AR[:, 93184 + i * 1024:93184 + (i + 1) * 1024] for i in range(2)]
        rwt = sb(ctx, nc, "rwt", [128, KC * 8])
        comb = sb(ctx, nc, "comb", [128, 8], F32)
        rstat = sb(ctx, nc, "rstat", [128, 12], F32)
        rows8 = sb(ctx, nc, "rows8", [1, 16])
        rows8f = sb(ctx, nc, "rows8f", [1, 16], F32)
        o1 = sb(ctx, nc, "o1", [1, 128])
        o128 = sb(ctx, nc, "o128", [128, 1])
        sel_sb = sb(ctx, nc, "sel_sb", [8, 8 * 128])
        id32 = sb(ctx, nc, "id32", [128, 128], F32)
        idnb = sb(ctx, nc, "idnb", [128, 128])
        sq_ch = [sb(ctx, nc, f"sq{i}", [128, T_]) for i in range(2)]
        sq2 = [sb(ctx, nc, f"sq2_{i}", [128, T_]) for i in range(2)]
        x3c = [sb(ctx, nc, f"x3c{i}", [128, T_], F32) for i in range(2)]
        pool = ps(ctx, nc, "pool", [128, 4096])
        A = pool[:, 0:1024]
        Bk = pool[:, 1024:2048]
        C = pool[:, 2048:3072]
        Dk = pool[:, 3072:4096]

        ldc = em.sem("ldc")
        ldg = em.sem("ldg")
        lda = em.sem("lda")
        ldh = [em.sem("ldh0"), em.sem("ldh1")]
        wsem = [em.sem(f"ws{i}") for i in range(4)]
        gsem = [em.sem(f"gs{i}") for i in range(2)]
        usem = [em.sem(f"us{i}") for i in range(2)]
        asem = [em.sem(f"as{i}") for i in range(2)]
        dsem = em.sem("ds")
        x3sem = [em.sem(f"x3s{i}") for i in range(2)]
        spl = em.sem("spl")
        st = em.sem("st")
        stsem = [em.sem("stg0"), em.sem("stg1")]
        tq = em.sem("tq")
        vv = em.sem("vv")
        ss = em.sem("ss")
        em.set_autochain("v", vv)
        em.set_autochain("s", ss)

        em.dma("y", stream.rearrange("p (kc t) -> p kc t", kc=KC),
               x2_in.ap().rearrange("(kc p) t -> p kc t", p=128), ldc)
        em.dma("y", rows8f[:, 0:8], conf_b[:, :], ldc)
        em.dma("y", rows8f[:, 8:16], wealth[:, :], ldc)
        em.dma("y", o1[:, :], ones1[:, :], ldc)
        em.dma("y", o128[:, :], ones128[:, :], ldc)
        em.dma("y", sel_sb[:, :].rearrange("p (e m) -> p e m", e=8),
               sel.ap().transpose([1, 0, 2]), ldc)
        em.dma("y", id32[:, :], ident32[:, :], ldc)
        v_ldc = em.dma("y", idnb[:, :], identb[:, :], ldc)
        v_rwt = em.dma("g", rwt[:, :].rearrange("p (kc m) -> p kc m", kc=KC),
                       conf_t.ap().rearrange("(kc p) m -> p kc m", p=128), ldg)
        em.wait("t", ldc, v_ldc)
        em.wait("v", ldc, v_ldc)
        em.wait("s", ldc, v_ldc)

        wuse = [0] * 4

        def wload(i, dram_slice, sem=None):
            sem = wsem[i] if sem is None else sem
            kcs = dram_slice.shape[0] // 128
            m_ = dram_slice.shape[1]
            return em.dma("g", wb[i][:, 0:kcs * m_].rearrange("p (kc m) -> p kc m", kc=kcs),
                          dram_slice.rearrange("(kc p) m -> p kc m", p=128),
                          sem, waits=[(tq, wuse[i])])

        strv = stream.rearrange("p (kc t) -> p kc t", kc=KC)
        xbv = xbf.rearrange("p (kc t) -> p kc t", kc=KC)
        attv = att_sb.rearrange("p (kc t) -> p kc t", kc=8)

        def wbv(i, kcs=16):
            return wb[i].rearrange("p (kc m) -> p kc m", kc=16)[:, 0:kcs, :]

        for kh in range(2):
            v_att = em.dma("g", attv[:, :, :],
                           att_in.ap().rearrange("(kc p) t -> p kc t", p=128)[:, kh * 8:(kh + 1) * 8, :],
                           lda, waits=[(vv, vv.v)])
            em.wait("t", lda, v_att)
            for oc in range(KC):
                wi = oc % 2
                v_w = wload(wi, wo.ap()[kh * 1024:(kh + 1) * 1024, oc * 128:(oc + 1) * 128])
                dst = A if oc % 2 == 0 else Bk
                em.wait("t", wsem[wi], v_w)
                em.wait("t", vv, vv.v)
                for hf in range(2):
                    for kc in range(8):
                        em.op("t", lambda e, _d=pool[:, (oc % 2) * 1024 + hf * 512:(oc % 2) * 1024 + (hf + 1) * 512],
                              _w=wbv(wi, 8)[:, kc, :],
                              _x=attv[:, kc, hf * 512:(hf + 1) * 512],
                              _s=(kc == 0), _p=(kc == 7):
                              e.matmul(_d, _w, _x, start=_s, stop=_p),
                              inc=tq if (kc == 7 and hf == 1) else None)
                wuse[wi] = tq.v
                em.op("v", lambda e, _d=strv[:, oc, :], _s=dst:
                      e.tensor_tensor(_d, _d, _s, ALU.add), waits=[(tq, tq.v)], inc=vv)

        def layernorm():
            em.wait("s", vv, vv.v)
            squse = [0, 0]
            for kc in range(KC):
                b2_ = kc % 2
                em.op("v", lambda e, _d=sq_ch[b2_][:, :], _s=strv[:, kc, :]:
                      e.tensor_copy(_d, _s), waits=[(tq, squse[b2_])])
                em.op("s", lambda e, _d=sq2[b2_][:, :], _s=strv[:, kc, :]:
                      e.activation(_d, _s, AF.Square, bias=0.0, scale=1.0),
                      waits=[(tq, squse[b2_])])
                for hf in range(2):
                    em.op("t", lambda e, _d=pool[0:1, 2048 + hf * 512:2048 + (hf + 1) * 512],
                          _o=o128[:, :], _x=sq_ch[b2_][:, hf * 512:(hf + 1) * 512],
                          _s=(kc == 0), _p=(kc == KC - 1):
                          e.matmul(_d, _o, _x, start=_s, stop=_p),
                          waits=[(vv, vv.v)] if hf == 0 else (),
                          inc=tq if hf == 1 else None)
                for hf in range(2):
                    em.op("t", lambda e, _d=pool[0:1, 3072 + hf * 512:3072 + (hf + 1) * 512],
                          _o=o128[:, :], _x=sq2[b2_][:, hf * 512:(hf + 1) * 512],
                          _s=(kc == 0), _p=(kc == KC - 1):
                          e.matmul(_d, _o, _x, start=_s, stop=_p),
                          waits=[(ss, ss.v)] if hf == 0 else (),
                          inc=tq if hf == 1 else None)
                squse[b2_] = tq.v
            Ar, Br = lnrow[:, 0:T_], lnrow[:, T_:2 * T_]
            Cr, Dr = lnrow[:, 2 * T_:3 * T_], lnrow[:, 3 * T_:4 * T_]
            em.op("v", lambda e: e.tensor_copy(Ar, pool[0:1, 2048:3072]), waits=[(tq, tq.v)])
            em.op("v", lambda e: e.tensor_copy(Br, pool[0:1, 3072:4096]))
            em.op("v", lambda e: e.tensor_scalar_mul(Cr, Ar, 1.0 / D))
            em.op("v", lambda e: e.tensor_scalar_mul(Ar, Br, 1.0 / D))
            em.op("v", lambda e: e.tensor_mul(Br, Cr, Cr))
            em.op("v", lambda e: e.tensor_sub(Ar, Ar, Br))
            em.op("v", lambda e: e.tensor_scalar_add(Ar, Ar, EPS))
            em.op("s", lambda e: e.activation(Br, Ar, AF.Ln, bias=0.0, scale=1.0),
                  waits=[(vv, vv.v)])
            em.op("s", lambda e: e.activation(Dr, Br, AF.Exp, bias=0.0, scale=-0.5))
            em.op("v", lambda e: e.tensor_mul(Ar, Cr, Dr), waits=[(ss, ss.v)])
            em.op("v", lambda e: e.tensor_copy(lnrow_bf[:, 0:T_], Dr))
            em.op("v", lambda e: e.tensor_copy(lnrow_bf[:, T_:2 * T_], Ar))
            for hf in range(2):
                em.op("t", lambda e, _d=C[:, hf * 512:(hf + 1) * 512], _r=lnrow_bf[:, hf * 512:(hf + 1) * 512]:
                      e.matmul(_d, o1[:, :], _r, start=True, stop=True),
                      waits=[(vv, vv.v)] if hf == 0 else (), inc=tq if hf == 1 else None)
                em.op("t", lambda e, _d=Dk[:, hf * 512:(hf + 1) * 512], _r=lnrow_bf[:, T_ + hf * 512:T_ + (hf + 1) * 512]:
                      e.matmul(_d, o1[:, :], _r, start=True, stop=True),
                      inc=tq if hf == 1 else None)
            for kc in range(KC):
                em.op("v", lambda e, _x=strv[:, kc, :]:
                      e.tensor_mul(_x, _x, C), waits=[(tq, tq.v)] if kc == 0 else ())
                em.op("v", lambda e, _x=strv[:, kc, :]:
                      e.tensor_sub(_x, _x, Dk))
                em.op("v", lambda e, _d=xbv[:, kc, :], _s=strv[:, kc, :]:
                      e.tensor_copy(_d, _s))
            em.wait("t", vv, vv.v)

        layernorm()

        v_spl = 0
        for kc in range(KC):
            v_spl = em.dma("y", x3_dram.ap().rearrange("(kc p) t -> p kc t", p=128)[:, kc, :],
                           strv[:, kc, :], spl, waits=[(vv, vv.v)] if kc == 0 else ())

        em.op("v", lambda e: e.tensor_copy(rows8[:, 0:8], rows8f[:, 0:8]))
        em.op("v", lambda e: e.tensor_copy(rows8[:, 8:16], rows8f[:, 8:16]))
        for tt in range(8):
            em.wait("t", ldg, v_rwt)
            for kc in range(KC):
                em.op("t", lambda e, _d=C[:, 0:8],
                      _x=xbv[:, kc, tt * 128:(tt + 1) * 128],
                      _w=rwt[:, :].rearrange("p (kc m) -> p kc m", kc=KC)[:, kc, :],
                      _s=(kc == 0), _p=(kc == KC - 1):
                      e.matmul(_d, _x, _w, start=_s, stop=_p),
                      inc=tq if kc == KC - 1 else None)
            em.op("t", lambda e: e.matmul(Dk[:, 0:8], o1[:, :], rows8[:, 0:8], start=True, stop=True),
                  waits=[(vv, vv.v)], inc=tq)
            em.op("t", lambda e: e.matmul(Dk[:, 512:520], o1[:, :], rows8[:, 8:16], start=True, stop=True),
                  inc=tq)
            em.op("v", lambda e: e.tensor_copy(comb[:, 0:8], C[:, 0:8]), waits=[(tq, tq.v)])
            em.op("v", lambda e: e.tensor_tensor(comb[:, 0:8], comb[:, 0:8], Dk[:, 0:8], ALU.add))
            em.op("s", lambda e: e.activation(comb[:, 0:8], comb[:, 0:8], AF.Sigmoid, bias=0.0, scale=1.0),
                  waits=[(vv, vv.v)])
            em.op("v", lambda e: e.tensor_tensor(comb[:, 0:8], comb[:, 0:8], Dk[:, 512:520], ALU.mult),
                  waits=[(ss, ss.v)])
            em.op("v", lambda e: e.reduce_max(rstat[:, 2:3], comb[:, 0:8], axis=AX.X))
            eqs = x3c[0]
            em.op("v", lambda e: e.tensor_single_scalar(eqs[:, 0:8], comb[:, 0:8], rstat[:, 2:3], ALU.is_equal))
            em.op("v", lambda e: e.scalar_tensor_tensor(eqs[:, 8:16], eqs[:, 0:8], -1.0e30,
                                                        comb[:, 0:8], ALU.mult, ALU.add))
            em.op("v", lambda e: e.reduce_max(rstat[:, 3:4], eqs[:, 8:16], axis=AX.X))
            em.op("v", lambda e: e.tensor_single_scalar(eqs[:, 8:16], comb[:, 0:8], rstat[:, 3:4], ALU.is_equal))
            em.op("v", lambda e: e.tensor_sub(rstat[:, 4:5], rstat[:, 3:4], rstat[:, 2:3]))
            em.op("s", lambda e: e.activation(rstat[:, 5:6], rstat[:, 4:5], AF.Exp, bias=0.0, scale=1.0),
                  waits=[(vv, vv.v)])
            em.op("v", lambda e: e.tensor_scalar_add(rstat[:, 6:7], rstat[:, 5:6], 1.0),
                  waits=[(ss, ss.v)])
            em.op("v", lambda e: e.reciprocal(rstat[:, 7:8], rstat[:, 6:7]))
            em.op("v", lambda e: e.tensor_mul(rstat[:, 8:9], rstat[:, 5:6], rstat[:, 7:8]))
            em.op("v", lambda e: e.tensor_single_scalar(eqs[:, 0:8], eqs[:, 0:8], rstat[:, 7:8], ALU.mult))
            em.op("v", lambda e: e.scalar_tensor_tensor(comb[:, 0:8], eqs[:, 8:16], rstat[:, 8:9],
                                                        eqs[:, 0:8], ALU.mult, ALU.add))
            em.op("t", lambda e, _d=Dk[0:8, 0:128], _s=comb[:, 0:8]:
                  e.transpose(_d, _s, id32[:, :]), waits=[(vv, vv.v)], inc=tq)
            em.op("v", lambda e, _d=comb_fm[:, tt * 128:(tt + 1) * 128], _s=Dk[0:8, 0:128]:
                  e.tensor_copy(_d, _s), waits=[(tq, tq.v)])
        comb_done = vv.v
        for e_ in range(8):
            for hf in range(2):
                em.op("t", lambda e, _d=C[:, hf * 512:(hf + 1) * 512],
                      _s=sel_sb[:, :].rearrange("p (e m) -> p e m", e=8)[:, e_, :],
                      _r=comb_fm[:, hf * 512:(hf + 1) * 512]:
                      e.matmul(_d, _s, _r, start=True, stop=True),
                      waits=[(vv, vv.v)] if hf == 0 else (), inc=tq if hf == 1 else None)
            em.op("v", lambda e, _d=cbc8[:, e_ * 1024:(e_ + 1) * 1024]:
                  e.tensor_copy(_d, C), waits=[(tq, tq.v)])

        em.wait("t", spl, v_spl)
        em.wait("v", spl, v_spl)
        em.wait("g", spl, v_spl)
        for e_ in range(8):
            for which, zdst, wAd in (("g", z_g, gA), ("u", z_u, uA)):
                wi = e_ % 2
                v_w = wload(wi, wAd.ap()[e_, :, :])
                em.wait("t", wsem[wi], v_w)
                em.wait("t", vv, vv.v)
                for hf in range(2):
                    for kc in range(KC):
                        em.op("t", lambda e, _d=C[0:64, hf * 512:(hf + 1) * 512],
                              _w=wb[wi][:, 0:16 * 64].rearrange("p (kc m) -> p kc m", kc=16)[:, kc, :],
                              _x=xbv[:, kc, hf * 512:(hf + 1) * 512],
                              _s=(kc == 0), _p=(kc == KC - 1):
                              e.matmul(_d, _w, _x, start=_s, stop=_p),
                              inc=tq if (kc == KC - 1 and hf == 1) else None)
                wuse[wi] = tq.v
                em.op("v", lambda e, _d=zdst[:, e_ * 1024:(e_ + 1) * 1024], _s=C[0:64, :]:
                      e.tensor_copy(_d, _s), waits=[(tq, tq.v)])
        em.op("v", lambda e: e.tensor_scalar_mul(z_g[:, :], z_g[:, :], SCL))
        em.op("v", lambda e: e.tensor_scalar_mul(z_u[:, :], z_u[:, :], SCL))
        z_done = vv.v

        em.op("v", lambda e: e.memset(y_sb[:, :], 0.0), waits=[(lda, 32), (tq, tq.v)])
        hb_dma = None
        lb_use = [0, 0]
        for i in range(IC):
            par = i % 2
            v_bg = wload(par, bgate.ap()[:, i * 128:(i + 1) * 128])
            v_bu = wload(2 + par, bup.ap()[:, i * 128:(i + 1) * 128], sem=wsem[2 + par])
            v_gb = em.dma("g", gb_b[par].rearrange("q (e m) -> q e m", e=8),
                          gB.ap()[:, :, i * 128:(i + 1) * 128].transpose([1, 0, 2]),
                          gsem[par], waits=[(tq, lb_use[par])])
            v_ub = em.dma("g", ub_b[par].rearrange("q (e m) -> q e m", e=8),
                          uB.ap()[:, :, i * 128:(i + 1) * 128].transpose([1, 0, 2]),
                          usem[par], waits=[(tq, lb_use[par])])
            v_da = em.dma("g", dA_b[par].rearrange("p (e m) -> p e m", e=8),
                          dA.ap()[:, i * 128:(i + 1) * 128, :].transpose([1, 0, 2]),
                          asem[par], waits=[(tq, lb_use[par])])
            em.wait("t", wsem[par], v_bg)
            em.wait("t", wsem[2 + par], v_bu)
            em.wait("t", gsem[par], v_gb)
            em.wait("t", usem[par], v_ub)
            em.wait("t", asem[par], v_da)
            em.wait("t", vv, vv.v)
            for hf in range(2):
                for kc in range(KC):
                    em.op("t", lambda e, _d=A[:, hf * 512:(hf + 1) * 512],
                          _w=wbv(par)[:, kc, :], _x=xbv[:, kc, hf * 512:(hf + 1) * 512],
                          _s=(kc == 0), _p=(kc == KC - 1):
                          e.matmul(_d, _w, _x, start=_s, stop=_p),
                          inc=tq if kc == KC - 1 else None)
            wuse[par] = tq.v
            bg_done = tq.v
            for hf in range(2):
                for kc in range(KC):
                    em.op("t", lambda e, _d=Bk[:, hf * 512:(hf + 1) * 512],
                          _w=wbv(2 + par)[:, kc, :], _x=xbv[:, kc, hf * 512:(hf + 1) * 512],
                          _s=(kc == 0), _p=(kc == KC - 1):
                          e.matmul(_d, _w, _x, start=_s, stop=_p),
                          inc=tq if kc == KC - 1 else None)
            wuse[2 + par] = tq.v
            em.op("v", lambda e: e.tensor_copy(base_g, A), waits=[(tq, bg_done)])
            em.op("v", lambda e: e.tensor_copy(base_u, Bk), waits=[(tq, tq.v)])
            base_done = vv.v
            free_s = [0, base_done * 0]
            free_v = [base_done, base_done]
            s_free = 0
            pend_y = None
            for e_ in range(8):
                suw = suw_b[e_ % 2]
                pg = A if e_ % 2 == 0 else C
                pu = Bk if e_ % 2 == 0 else Dk
                em.wait("t", ss, free_s[e_ % 2])
                em.wait("t", vv, free_v[e_ % 2])
                for hf in range(2):
                    em.op("t", lambda e, _d=pg[:, hf * 512:(hf + 1) * 512],
                          _w=gb_b[par].rearrange("q (e m) -> q e m", e=8)[:, e_, :],
                          _z=z_g[:, e_ * 1024 + hf * 512:e_ * 1024 + (hf + 1) * 512]:
                          e.matmul(_d, _w, _z, start=True, stop=False))
                    em.op("t", lambda e, _d=pg[:, hf * 512:(hf + 1) * 512],
                          _r=base_g[:, hf * 512:(hf + 1) * 512]:
                          e.matmul(_d, idnb[:, :], _r, start=False, stop=True),
                          inc=tq if hf == 1 else None)
                g_done_ = tq.v
                for hf in range(2):
                    em.op("t", lambda e, _d=pu[:, hf * 512:(hf + 1) * 512],
                          _w=ub_b[par].rearrange("q (e m) -> q e m", e=8)[:, e_, :],
                          _z=z_u[:, e_ * 1024 + hf * 512:e_ * 1024 + (hf + 1) * 512]:
                          e.matmul(_d, _w, _z, start=True, stop=False))
                    em.op("t", lambda e, _d=pu[:, hf * 512:(hf + 1) * 512],
                          _r=base_u[:, hf * 512:(hf + 1) * 512]:
                          e.matmul(_d, idnb[:, :], _r, start=False, stop=True),
                          inc=tq if hf == 1 else None)
                u_done_ = tq.v
                if pend_y is not None:
                    pe, pdone, ppg = pend_y
                    for hf in range(2):
                        em.op("t", lambda e, _d=ppg[0:64, hf * 512:(hf + 1) * 512],
                              _w=dA_b[par].rearrange("p (e m) -> p e m", e=8)[:, pe, :],
                              _r=suw_b[pe % 2][:, hf * 512:(hf + 1) * 512]:
                              e.matmul(_d, _w, _r, start=True, stop=True),
                              waits=[(vv, pdone)] if hf == 0 else (),
                              inc=tq if hf == 1 else None)
                    em.op("v", lambda e, _d=y_sb[:, pe * 1024:(pe + 1) * 1024], _p=ppg[0:64, :]:
                          e.tensor_tensor(_d, _d, _p, ALU.add), waits=[(tq, tq.v)])
                    free_v[pe % 2] = vv.v
                em.op("s", lambda e, _p=pg: e.copy(g_ch[:, :], _p), waits=[(tq, g_done_)])
                free_s[e_ % 2] = ss.v
                em.op("s", lambda e: e.activation(s_ch[:, :], g_ch[:, :], AF.Silu, bias=0.0, scale=1.0),
                      waits=[(vv, s_free)])
                em.op("v", lambda e, _p=pu: e.tensor_copy(u_ch[:, :], _p), waits=[(tq, u_done_)])
                em.op("v", lambda e: e.tensor_tensor(u_ch[:, :], s_ch[:, :], u_ch[:, :], ALU.mult),
                      waits=[(ss, ss.v)])
                s_free = vv.v
                em.op("v", lambda e, _c=cbc8[:, e_ * 1024:(e_ + 1) * 1024], _sw=suw:
                      e.tensor_tensor(_sw[:, :], u_ch[:, :], _c, ALU.mult))
                suw_done = vv.v
                if e_ == 0:
                    em.op("v", lambda e, _sw=suw: e.tensor_copy(hbar, _sw[:, :]),
                          waits=[hb_dma] if hb_dma else ())
                else:
                    em.op("v", lambda e, _sw=suw: e.tensor_tensor(hbar, hbar, _sw[:, :], ALU.add))
                pend_y = (e_, suw_done, pg)
            pe, pdone, ppg = pend_y
            for hf in range(2):
                em.op("t", lambda e, _d=ppg[0:64, hf * 512:(hf + 1) * 512],
                      _w=dA_b[par].rearrange("p (e m) -> p e m", e=8)[:, pe, :],
                      _r=suw_b[pe % 2][:, hf * 512:(hf + 1) * 512]:
                      e.matmul(_d, _w, _r, start=True, stop=True),
                      waits=[(vv, pdone)] if hf == 0 else (),
                      inc=tq if hf == 1 else None)
            em.op("v", lambda e, _d=y_sb[:, pe * 1024:(pe + 1) * 1024], _p=ppg[0:64, :]:
                  e.tensor_tensor(_d, _d, _p, ALU.add), waits=[(tq, tq.v)])
            lb_use[par] = tq.v
            hb_dma = (ldh[par], em.dma("y", hb_dram.ap()[i, :, :], hbar, ldh[par],
                                       waits=[(vv, vv.v)]))
        em.op("v", lambda e: e.tensor_scalar_mul(y_sb[:, :], y_sb[:, :], SCL),
              waits=[(tq, tq.v)])
        y_scaled = vv.v
        em.wait("g", ldh[0], ldh[0].v)
        em.wait("g", ldh[1], ldh[1].v)

        hb2 = [AR[:, 81920 + i * 2048:81920 + (i + 1) * 2048] for i in range(2)]
        ldh6 = [em.sem("ldh6_0"), em.sem("ldh6_1")]
        dB6 = AR[0:64, 86016:86016 + 4096]
        x3use = [vv.v, vv.v]
        hb_use = [0, 0]
        wd_use = [0, 0]
        db_use = 0
        ldh_fin = [ldh[0].v, ldh[1].v]
        em.wait("v", ldh[0], ldh_fin[0])
        em.wait("v", ldh[1], ldh_fin[1])
        em.wait("g", ldh[0], ldh_fin[0])
        em.wait("g", ldh[1], ldh_fin[1])
        for ocq in range(4):
            v_db = em.dma("g", dB6.rearrange("q (e m) -> q e m", e=8),
                          dB.ap()[:, :, ocq * 512:(ocq + 1) * 512].transpose([1, 0, 2]),
                          wsem[3], waits=[(tq, db_use)])
            em.wait("t", vv, vv.v)
            for step in range(28):
                i = step * 2
                par = step % 2
                v_hb = em.dma("g", hb2[par].rearrange("p (a t) -> p a t", a=2),
                              hb_dram.ap()[i:i + 2, :, :].transpose([1, 0, 2]),
                              ldh6[par], waits=[(tq, hb_use[par])])
                v_wd = em.dma("g", wb[par].rearrange("p (a m) -> p a m", a=4)[:, 0:2, :],
                              bdown.ap()[i * 128:(i + 2) * 128, ocq * 512:(ocq + 1) * 512]
                              .rearrange("(a p) m -> p a m", p=128),
                              wsem[par], waits=[(tq, wd_use[par])])
                em.wait("t", ldh6[par], v_hb)
                em.wait("t", wsem[par], v_wd)
                for ii in range(2):
                    for oc2 in range(4):
                        for hf in range(2):
                            gidx = oc2 * 2 + hf
                            em.op("t", lambda e, _d=pool[:, gidx * 512:(gidx + 1) * 512],
                                  _w=wb[par].rearrange("p (a m) -> p a m", a=4)[:, ii, oc2 * 128:(oc2 + 1) * 128],
                                  _h=hb2[par][:, ii * 1024 + hf * 512:ii * 1024 + (hf + 1) * 512],
                                  _s=(step == 0 and ii == 0):
                                  e.matmul(_d, _w, _h, start=_s, stop=False),
                                  inc=tq if (ii == 1 and oc2 == 3 and hf == 1) else None)
                hb_use[par] = tq.v
                wd_use[par] = tq.v
            em.wait("t", wsem[3], v_db)
            for oc2 in range(4):
                for hf in range(2):
                    gidx = oc2 * 2 + hf
                    for e_ in range(8):
                        em.op("t", lambda e, _d=pool[:, gidx * 512:(gidx + 1) * 512],
                              _w=dB6[0:64, e_ * 512 + oc2 * 128:e_ * 512 + (oc2 + 1) * 128],
                              _y=y_sb[:, e_ * 1024 + hf * 512:e_ * 1024 + (hf + 1) * 512],
                              _p=(e_ == 7):
                              e.matmul(_d, _w, _y, start=False, stop=_p),
                              inc=tq if e_ == 7 else None)
            db_use = tq.v
            mm_done = tq.v
            for oc2 in range(4):
                oc = ocq * 4 + oc2
                x3b = x3c[oc % 2]
                v_x3 = em.dma("y", x3b[:, :],
                              x3_dram.ap().rearrange("(kc p) t -> p kc t", p=128)[:, oc, :],
                              x3sem[oc % 2], waits=[(vv, x3use[oc % 2]), (spl, v_spl)])
                em.wait("v", x3sem[oc % 2], v_x3)
                for hf in range(2):
                    gidx = oc2 * 2 + hf
                    em.op("v", lambda e, _d=strv[:, oc, hf * 512:(hf + 1) * 512],
                          _p=pool[:, gidx * 512:(gidx + 1) * 512],
                          _x=x3b[:, hf * 512:(hf + 1) * 512]:
                          e.tensor_tensor(_d, _p, _x, ALU.add), waits=[(tq, mm_done)])
                x3use[oc % 2] = vv.v

        layernorm()
        fin = vv.v
        for kc in range(KC):
            em.op("y", lambda e, _d=out_d.ap().rearrange("(kc p) t -> p kc t", p=128)[:, kc, :],
                  _s=strv[:, kc, :]:
                  e.dma_start(out=_d, in_=_s), waits=[(vv, fin)], inc=st, inc_n=16)
        em.wait("y", st, st.v)
        em.run_block()
    return nc
from concourse.bass_utils import run_bass_kernel_spmd

_NCACHE = {}


def _get_nc(name, builder):
    if name not in _NCACHE:
        _NCACHE[name] = builder()
    return _NCACHE[name]


def _own_cols(c):
    a, b_ = stripes_of_core(c)
    return np.r_[a * 512:(a + 1) * 512, b_ * 512:(b_ + 1) * 512]


def kernel(**inputs):
    cores = list(range(8))

    nc1 = _get_nc("a1", build_a1)
    r1 = run_bass_kernel_spmd(nc1, a1_in_maps(inputs), core_ids=cores)
    att_full = np.stack([
        np.concatenate([r1.results[b * 4 + g]["att"].reshape(512, S)
                        for g in range(4)], axis=0)
        for b in range(B)])

    nc2 = _get_nc("a2", build_a2)
    r2 = run_bass_kernel_spmd(nc2, a2_in_maps(inputs, att_full), core_ids=cores)

    qf = np.zeros((B, D, S), BF)
    kf = np.zeros((B, D, S), BF)
    vt = np.zeros((B, S, D), BF)
    x2 = [None] * 8
    for c in range(8):
        b = c // 4
        cols = _own_cols(c)
        qf[b][:, cols] = r2.results[c]["q1_out"]
        kf[b][:, cols] = r2.results[c]["k1_out"]
        vt[b][cols, :] = r2.results[c]["v1_out"].T
        x2[c] = r2.results[c]["x2_out"]

    nc3 = _get_nc("b1", build_b1)
    tri = host_masks()[0][:, 0:128].copy()
    betasT = np.ascontiguousarray(np.asarray(inputs["L1_attn_betas"])[0, :, 0, :].T)
    maps3 = []
    for c in range(8):
        b, hq = c // 4, c % 4
        hsl = slice(hq * 512, (hq + 1) * 512)
        maps3.append({"qf": np.ascontiguousarray(qf[b][hsl, :]),
                      "kf": np.ascontiguousarray(kf[b][hsl, :]),
                      "vt": np.ascontiguousarray(vt[b][:, hsl]),
                      "betasT": np.ascontiguousarray(betasT[:, hq * 4:(hq + 1) * 4]),
                      "tri": tri, "ident": ident_np(),
                      "ones1b": np.ones((1, 128), BF)})
    r3 = run_bass_kernel_spmd(nc3, maps3, core_ids=cores)
    T = np.arange(S)
    seg_i, t_i = T // 512, T % 512
    a_i, b_i = t_i // 32, t_i % 32
    f = np.arange(D)
    j_i, v_i = f // 128, f % 128
    _rows = a_i[None, :] * 128 + v_i[:, None]
    _cols = seg_i[None, :] * 512 + b_i[None, :] * 16 + j_i[:, None]
    att1 = []
    for b in range(B):
        full = np.concatenate([r3.results[b * 4 + j]["att"] for j in range(4)], axis=0)
        att1.append(np.ascontiguousarray(full[_rows, _cols]))

    nc4 = _get_nc("b2", build_b2)
    common4 = {
        "wo": np.asarray(inputs["L1_attn_Wo"]),
        "conf_t": np.ascontiguousarray(np.asarray(inputs["L1_ffn_conf_W"]).T),
        "conf_b": np.asarray(inputs["L1_ffn_conf_b"]).reshape(1, 8),
        "wealth": np.asarray(inputs["L1_ffn_expert_wealth"]).reshape(1, 8),
        "bgate": np.asarray(inputs["L1_ffn_base_gate"]),
        "bup": np.asarray(inputs["L1_ffn_base_up"]),
        "bdown": np.asarray(inputs["L1_ffn_base_down"]),
        "gA": np.asarray(inputs["L1_ffn_gate_A"]), "gB": np.asarray(inputs["L1_ffn_gate_B"]),
        "uA": np.asarray(inputs["L1_ffn_up_A"]), "uB": np.asarray(inputs["L1_ffn_up_B"]),
        "dA": np.asarray(inputs["L1_ffn_down_A"]), "dB": np.asarray(inputs["L1_ffn_down_B"]),
        "ones1": np.ones((1, 128), BF), "ones128": np.ones((128, 1), BF),
        "sel": np.ascontiguousarray(np.eye(8, dtype=BF)[:, :, None] * np.ones((1, 1, 128), BF)),
        "ident32": np.eye(128, dtype=np.float32),
        "identb": np.eye(128, dtype=BF),
    }
    maps4 = []
    for c in range(8):
        b = c // 4
        cols = _own_cols(c)
        m = dict(common4)
        m["att_in"] = np.ascontiguousarray(att1[b][:, cols])
        m["x2_in"] = x2[c]
        maps4.append(m)
    r4 = run_bass_kernel_spmd(nc4, maps4, core_ids=cores)

    out = np.zeros((B, S, D), np.float32)
    for c in range(8):
        b = c // 4
        cols = _own_cols(c)
        out[b][cols, :] = r4.results[c]["out"].T
    return out
